# version 01
# baseline (speedup 1.0000x reference)
import numpy as np
import concourse.bass as bass
import concourse.mybir as mybir
from concourse.bass_utils import run_bass_kernel_spmd
from contextlib import ExitStack

F32 = mybir.dt.float32
BF16 = mybir.dt.bfloat16
U8 = mybir.dt.uint8
AF = mybir.ActivationFunctionType
ALU = mybir.AluOpType
AX = mybir.AxisListType

T = 2048
D = 1024
NT = T // 128
ND = D // 128
NCORES = 8
EPS = 1e-5

ENGS = ("sync", "gpsimd", "vector", "scalar", "tensor")
_EIDX = {e: i for i, e in enumerate(ENGS)}


def _ap_box(ap):
    t = ap.tensor
    esz = np.dtype(mybir.dt.np(ap.dtype)).itemsize
    lo = 0
    hi = 0
    for step, num in ap.ap:
        d = step * (num - 1)
        if d >= 0:
            hi += d
        else:
            lo += d
    off = ap.offset
    return (t.name, (off + lo) * esz, (off + hi) * esz + esz)


class Prog:

    def __init__(self, nc, stack):
        self.nc = nc
        self.stack = stack
        self.items = []
        self.sems = []
        self.counts = []
        self.is_dma_sem = []
        self.dma_snap = []
        for e in ENGS:
            self.sems.append(stack.enter_context(nc.semaphore(f"sem_{e}")))
            self.counts.append(0)
            self.is_dma_sem.append(False)
            self.dma_snap.append(None)
        self.dma_sid = {}
        self.sem_pool = {"sw": [], "hw": []}
        self.reg = {}
        self.clocks = {e: {} for e in ENGS}
        self.snaps = {}
        self.fence_toks = {}
        self.n_waits = 0
        self.n_ops = 0

    def _dma_sid_for(self, name, kind):
        sid = self.dma_sid.get((name, kind))
        if sid is not None:
            return sid
        if self.sem_pool[kind]:
            sid = self.sem_pool[kind].pop()
        else:
            sid = len(self.sems)
            self.sems.append(self.stack.enter_context(
                self.nc.semaphore(f"dsem{sid}")))
            self.counts.append(0)
            self.is_dma_sem.append(True)
            self.dma_snap.append({})
        self.dma_sid[(name, kind)] = sid
        return sid

    def release_dma_sems(self, names):
        for n in names:
            for kind in ("sw", "hw"):
                sid = self.dma_sid.pop((n, kind), None)
                if sid is not None:
                    self.sem_pool[kind].append(sid)

    def fence(self):
        for sid in range(len(self.sems)):
            c = self.counts[sid]
            if c > 0:
                self.fence_toks[sid] = max(self.fence_toks.get(sid, 0), c)

    def barrier(self):
        toks = [(sid, c) for sid, c in enumerate(self.counts) if c > 0]
        sems = self.sems
        for e in ENGS:
            clk = self.clocks[e]
            mywaits = [(sid, c) for sid, c in toks if clk.get(sid, 0) < c]

            def g(engine, mywaits=mywaits):
                for sid, v in mywaits:
                    engine.wait_ge(sems[sid], v)
            self.items.append((e, g))
            for sid, c in mywaits:
                clk[sid] = c
        self.fence()

    def _collect_deps(self, eng, reads, writes):
        need = {}
        my = _EIDX[eng]

        def consider(sid, cnt):
            if sid == my == _EIDX["tensor"]:
                return
            if self.is_dma_sem[sid]:
                cnt = self.counts[sid]
            if cnt > need.get(sid, 0):
                need[sid] = cnt

        for sid, c in self.fence_toks.items():
            consider(sid, c)
        for ap in reads:
            name, lo, hi = _ap_box(ap)
            for ent in self.reg.get(name, ()):
                if ent[4] and ent[0] < hi and lo < ent[1]:
                    consider(ent[2], ent[3])
        for ap in writes:
            name, lo, hi = _ap_box(ap)
            for ent in self.reg.get(name, ()):
                if ent[0] < hi and lo < ent[1]:
                    consider(ent[2], ent[3])
        clk = self.clocks[eng]
        waits = []
        for sid, cnt in need.items():
            if cnt > clk.get(sid, 0):
                waits.append((sid, cnt))
        return waits

    def _register(self, sid, cnt, reads, writes):
        for ap in writes:
            name, lo, hi = _ap_box(ap)
            lst = self.reg.setdefault(name, [])
            lst[:] = [e for e in lst if not (lo <= e[0] and e[1] <= hi)]
            lst.append([lo, hi, sid, cnt, True])
            if len(lst) > 256:
                self._compact(name)
        for ap in reads:
            name, lo, hi = _ap_box(ap)
            lst = self.reg.setdefault(name, [])
            lst.append([lo, hi, sid, cnt, False])
            if len(lst) > 256:
                self._compact(name)

    def _compact(self, name):
        lst = self.reg[name]
        out = []
        for is_w in (True, False):
            sub = [e for e in lst if e[4] == is_w]
            if not sub:
                continue
            lo = min(e[0] for e in sub)
            hi = max(e[1] for e in sub)
            best = {}
            for e in sub:
                if e[3] > best.get(e[2], 0):
                    best[e[2]] = e[3]
            out.extend([lo, hi, sid, c, is_w] for sid, c in best.items())
        self.reg[name] = out

    def op(self, eng, fn, reads=(), writes=(), extra_wait=(), dma=False):
        reads = [r for r in reads if r is not None]
        writes = [w for w in writes if w is not None]
        waits = self._collect_deps(eng, reads, writes)
        my = _EIDX[eng]
        clk = self.clocks[eng]
        for en, cv in extra_wait or ():
            sid = _EIDX[en] if isinstance(en, str) else en
            if cv > clk.get(sid, 0) and not (sid == my == _EIDX["tensor"]):
                waits.append((sid, cv))
        wm = {}
        for sid, cv in waits:
            if cv > wm.get(sid, 0):
                wm[sid] = cv
        waits = sorted(wm.items())
        for sid, cv in waits:
            if cv > clk.get(sid, 0):
                clk[sid] = cv
            snap = self.dma_snap[sid] if self.is_dma_sem[sid] else \
                self.snaps.get((sid, cv))
            if snap is not None:
                for k, v in snap.items():
                    if v > clk.get(k, 0):
                        clk[k] = v
        if dma:
            dest = writes[0].tensor.name
            kind = "hw" if eng in ("sync", "scalar") else "sw"
            sid_inc = self._dma_sid_for(dest, kind)
            d = 16
        else:
            sid_inc = my
            d = 1
        self.counts[sid_inc] += d
        cnt = self.counts[sid_inc]
        snap_now = dict(clk)
        snap_now[sid_inc] = cnt
        if not dma:
            snap_now[my] = cnt
        if self.is_dma_sem[sid_inc]:
            acc = self.dma_snap[sid_inc]
            for k, v in snap_now.items():
                if v > acc.get(k, 0):
                    acc[k] = v
        else:
            self.snaps[(sid_inc, cnt)] = snap_now
        self._register(sid_inc, cnt, reads, writes)
        sems = self.sems
        self.n_waits += len(waits)
        self.n_ops += 1

        def g(engine, waits=waits, fn=fn, sid_inc=sid_inc, d=d):
            for sid, v in waits:
                engine.wait_ge(sems[sid], v)
            inst = fn(engine)
            inst.then_inc(sems[sid_inc], d)
        self.items.append((eng, g))
        return (sid_inc, cnt)

    def raw_wait(self, eng, tokens=None):
        sems = self.sems
        if tokens is None:
            toks = [(sid, c) for sid, c in enumerate(self.counts) if c > 0]
        else:
            toks = [t for t in tokens if t[1] > 0]

        def g(engine, toks=toks):
            for sid, v in toks:
                engine.wait_ge(sems[sid], v)
        self.items.append((eng, g))

    def barrier_token(self, eng):
        sid = _EIDX[eng]
        return (sid, self.counts[sid])

    def replay(self, block):
        items = self.items

        def runner(name):
            def run(e):
                for n, f in items:
                    if n == name:
                        f(e)
            return run
        block.sync(runner("sync"))
        block.gpsimd(runner("gpsimd"))
        block.vector(runner("vector"))
        block.scalar(runner("scalar"))
        block.tensor(runner("tensor"))


class Ctx:

    def __init__(self, nc, p, stack, aps):
        self.nc = nc
        self.p = p
        self.stack = stack
        self.aps = aps
        self.ps_f = []
        self.ps_b = []
        for i in range(6):
            self.ps_f.append(stack.enter_context(nc.psum_tensor(f"psf{i}", [128, 512], F32)))
        for i in range(2):
            self.ps_b.append(stack.enter_context(nc.psum_tensor(f"psb{i}", [128, 1024], BF16)))
        self._psf_i = 0
        self._psb_i = 0
        self.identb = stack.enter_context(nc.sbuf_tensor("c_identb", [128, 128], BF16))
        self.identf = stack.enter_context(nc.sbuf_tensor("c_identf", [128, 128], F32))
        self.ones_b = stack.enter_context(nc.sbuf_tensor("ones_b", [1, 128], BF16))
        self.neg_s = stack.enter_context(nc.sbuf_tensor("neg_s", [128, 1], F32))
        self.eps_s = stack.enter_context(nc.sbuf_tensor("eps_s", [128, 1], F32))
        p.op("sync", lambda e: e.dma_start(out=self.identb[:], in_=aps["identb"][:]),
             writes=[self.identb[:]], dma=True)
        p.op("sync", lambda e: e.dma_start(out=self.identf[:], in_=aps["identf"][:]),
             writes=[self.identf[:]], dma=True)
        p.op("vector", lambda e: e.memset(self.ones_b[:], 1.0), writes=[self.ones_b[:]])
        self.ones_f = stack.enter_context(nc.sbuf_tensor("ones_f", [1, 128], F32))
        p.op("vector", lambda e: e.memset(self.neg_s[:], -1e30), writes=[self.neg_s[:]])
        p.op("vector", lambda e: e.memset(self.eps_s[:], EPS), writes=[self.eps_s[:]])
        p.op("vector", lambda e: e.memset(self.ones_f[:], 1.0), writes=[self.ones_f[:]])

    def psf(self):
        t = self.ps_f[self._psf_i % len(self.ps_f)]
        self._psf_i += 1
        return t

    def psb(self):
        t = self.ps_b[self._psb_i % len(self.ps_b)]
        self._psb_i += 1
        return t

    def ps8(self, i):
        if i < 6:
            return self.ps_f[i][:]
        return self.ps_b[i - 6][:].bitcast(F32)

    def negbc(self, n):
        return self.neg_s[:].broadcast_to((128, n))


class Phase:
    def __init__(self, C, name):
        self.C = C
        self.name = name
        self.stack = ExitStack()
        self.names = []

    def sb(self, name, shape, dt):
        full = self.name + "_" + name
        self.names.append(full)
        return self.stack.enter_context(self.C.nc.sbuf_tensor(full, shape, dt))

    def close(self):
        self.stack.close()
        self.C.p.barrier()
        self.C.p.release_dma_sems(self.names)


def wload(C, dst_sb, dram_ap, rows, cols, row0=0, col0=0, kb=128):
    nr = rows // 128
    src = dram_ap[row0:row0 + rows, col0:col0 + cols].rearrange("(r p) c -> p r c", p=128)
    dst = dst_sb[:, 0:nr * cols].rearrange("p (r c) -> p r c", c=cols)
    return C.p.op("gpsimd", lambda e: e.dma_start(out=dst, in_=src),
                  reads=[src], writes=[dst_sb[:, 0:nr * cols]], dma=True)


def transpose128(C, ph, src_ap, dst_ap, dtype, w=128):
    pb = C.psb() if dtype == BF16 else C.psf()
    ident = C.identb if dtype == BF16 else C.identf
    pslice = pb[:, 0:128] if dtype == BF16 else pb[:, 0:128]
    C.p.op("tensor", lambda e: e.transpose(out=pb[0:w, 0:128], in_=src_ap, identity=ident[:]),
           reads=[src_ap, ident[:]], writes=[pb[0:w, 0:128]])
    C.p.op("vector", lambda e: e.tensor_copy(out=dst_ap, in_=pb[0:w, 0:128]),
           reads=[pb[0:w, 0:128]], writes=[dst_ap])


def softmax_rows(C, ph, sc_f32, width, p_bf16, stats, col):
    nmx = stats[:, 3 * col:3 * col + 1]
    ssum = stats[:, 3 * col + 1:3 * col + 2]
    rs = stats[:, 3 * col + 2:3 * col + 3]
    C.p.op("vector", lambda e: e.reduce_max(out=nmx, in_=sc_f32, axis=AX.X, negate=True),
           reads=[sc_f32], writes=[nmx])
    C.p.op("scalar", lambda e: e.activation(out=p_bf16, in_=sc_f32, func=AF.Exp,
                                            bias=nmx, scale=1.0, accum_out=ssum),
           reads=[sc_f32, nmx], writes=[p_bf16, ssum])
    C.p.op("vector", lambda e: e.reciprocal(out=rs, in_=ssum), reads=[ssum], writes=[rs])
    C.p.op("vector", lambda e: e.tensor_scalar_mul(out=p_bf16, in0=p_bf16, scalar1=rs),
           reads=[p_bf16, rs], writes=[p_bf16])


def ln_stats(C, ph, src_ap, sq_tmp, stats, col, nfeat):
    ssum = stats[:, 4 * col + 0:4 * col + 1]
    svar = stats[:, 4 * col + 1:4 * col + 2]
    mu = stats[:, 4 * col + 2:4 * col + 3]
    rstd = stats[:, 4 * col + 3:4 * col + 4]
    p = C.p
    p.op("scalar", lambda e: e.activation(out=sq_tmp, in_=src_ap, func=AF.Square,
                                          accum_out=svar),
         reads=[src_ap], writes=[sq_tmp, svar])
    p.op("vector", lambda e: e.reduce_sum(out=ssum, in_=src_ap, axis=AX.X),
         reads=[src_ap], writes=[ssum])
    p.op("vector", lambda e: e.tensor_scalar_mul(out=mu, in0=ssum, scalar1=1.0 / nfeat),
         reads=[ssum], writes=[mu])
    p.op("vector", lambda e: e.tensor_scalar_mul(out=svar, in0=svar, scalar1=1.0 / nfeat),
         reads=[svar], writes=[svar])
    p.op("vector", lambda e: e.scalar_tensor_tensor(out=svar, in0=mu, scalar=mu, in1=svar,
                                                    op0=ALU.mult, op1=ALU.subtract),
         reads=[mu, svar], writes=[svar])
    p.op("scalar", lambda e: e.activation(out=rstd, in_=svar, func=AF.Ln, bias=C.eps_s[:],
                                          scale=-1.0),
         reads=[svar, C.eps_s[:]], writes=[rstd])
    p.op("scalar", lambda e: e.activation(out=rstd, in_=rstd, func=AF.Exp, scale=-0.5),
         reads=[rstd], writes=[rstd])
    return mu, rstd


def mm_accum(C, psum_ap, lhsT_list, rhs_list, extra=None):
    n = len(lhsT_list)
    for i in range(n):
        lh, rh = lhsT_list[i], rhs_list[i]
        C.p.op("tensor",
               lambda e, lh=lh, rh=rh, s=(i == 0), t=(i == n - 1): e.matmul(
                   out=psum_ap, lhsT=lh, rhs=rh, start=s, stop=t),
               reads=[lh, rh], writes=[psum_ap])


def transpose_batch(C, srcs, dst_ap, dtype):
    n = len(srcs)
    if dtype == BF16:
        pbt = C.psb()
        ident = C.identb
        pb = pbt[:, 0:n * 128]
        slots = [pbt[:, j * 128:(j + 1) * 128] for j in range(n)]
    else:
        pbt = C.psf()
        ident = C.identf
        pb = pbt[:, 0:n * 128]
        slots = [pbt[:, j * 128:(j + 1) * 128] for j in range(n)]
    for j, s in enumerate(srcs):
        C.p.op("tensor", lambda e, s=s, o=slots[j]: e.transpose(out=o, in_=s,
                                                                identity=ident[:]),
               reads=[s, ident[:]], writes=[slots[j]])
    C.p.op("vector", lambda e: e.tensor_copy(out=dst_ap, in_=pb),
           reads=[pb], writes=[dst_ap])


def make_bcast(C, ph, name, row_dram_ap, w=1024):
    row = ph.sb(name + "_r", [1, w], F32)
    out = ph.sb(name + "_bc", [128, w], F32)
    C.p.op("sync", lambda e: e.dma_start(out=row[:], in_=row_dram_ap),
           reads=[row_dram_ap], writes=[row[:]], dma=True)
    for nj in range((w + 511) // 512):
        wv = min(512, w - nj * 512)
        pA = C.psf()
        C.p.op("tensor", lambda e, r=row[0:1, nj * 512:nj * 512 + wv], o=pA[:, 0:wv]: e.matmul(
            out=o, lhsT=C.ones_f[:], rhs=r, start=True, stop=True),
            reads=[row[:], C.ones_f[:]], writes=[pA[:, 0:wv]])
        C.p.op("vector", lambda e, o=out[:, nj * 512:nj * 512 + wv], s=pA[:, 0:wv]:
               e.tensor_copy(out=o, in_=s), reads=[pA[:, 0:wv]], writes=[out[:]])
    return out


def ln_tile(C, ph, src_ap, dst_ap, sq_tmp, stats, col, affine=None):
    mu, rstd = ln_stats(C, ph, src_ap, sq_tmp, stats, col, D)
    C.p.op("vector", lambda e: e.tensor_scalar(
        out=dst_ap, in0=src_ap, scalar1=mu, scalar2=rstd, op0=ALU.subtract, op1=ALU.mult),
        reads=[src_ap, mu, rstd], writes=[dst_ap])
    if affine is not None:
        gbc, bbc = affine
        C.p.op("vector", lambda e: e.tensor_tensor(out=dst_ap, in0=dst_ap, in1=gbc, op=ALU.mult),
               reads=[dst_ap, gbc], writes=[dst_ap])
        C.p.op("vector", lambda e: e.tensor_tensor(out=dst_ap, in0=dst_ap, in1=bbc, op=ALU.add),
               reads=[dst_ap, bbc], writes=[dst_ap])


def make_xn(C, x_dram, ph, xnT, xlnT=None, lnwb=None):
    p = C.p
    xst = [ph.sb(f"xst{i}", [128, D], F32) for i in range(2)]
    sqt = [ph.sb(f"sqt{i}", [128, D], F32) for i in range(2)]
    xnb = [ph.sb(f"xnb{i}", [128, D], BF16) for i in range(2)]
    stats = ph.sb("stats", [128, 4 * NT], F32)
    stats2 = ph.sb("stats2", [128, 4 * NT], F32)
    xlst = [ph.sb(f"xlst{i}", [128, D], BF16) for i in range(2)]
    for ti in range(NT):
        st = xst[ti % 2]
        sq = sqt[ti % 2]
        xb = xnb[ti % 2]
        src = x_dram[ti * 128:(ti + 1) * 128, :]
        p.op("sync", lambda e, s=st, r=src: e.dma_start(out=s[:], in_=r),
             reads=[src], writes=[st[:]], dma=True)
        mu, rstd = ln_stats(C, ph, st[:], sq[:], stats, ti, D)
        p.op("vector", lambda e, s=st, o=xb, m=mu, r=rstd: e.tensor_scalar(
            out=o[:], in0=s[:], scalar1=m, scalar2=r, op0=ALU.subtract, op1=ALU.mult),
            reads=[st[:], mu, rstd], writes=[xb[:]])
        for fb in range(ND):
            src_b = xb[:, fb * 128:(fb + 1) * 128]
            dst_b = xnT[:, fb * T + ti * 128: fb * T + (ti + 1) * 128]
            transpose128(C, ph, src_b, dst_b, BF16)
        if xlnT is not None:
            lnw, lnb = lnwb
            xl = xlst[ti % 2]
            mu2, rstd2 = ln_stats(C, ph, xb[:], sq[:], stats2, ti, D)
            p.op("vector", lambda e, o=xl, i=xb, m=mu2, r=rstd2: e.tensor_scalar(
                out=o[:], in0=i[:], scalar1=m, scalar2=r, op0=ALU.subtract, op1=ALU.mult),
                reads=[xb[:], mu2, rstd2], writes=[xl[:]])
            for fb in range(ND):
                pb = C.psb()
                src_b = xl[:, fb * 128:(fb + 1) * 128]
                dst_b = xlnT[:, fb * T + ti * 128: fb * T + (ti + 1) * 128]
                p.op("tensor", lambda e, s=src_b, pb=pb: e.transpose(
                    out=pb[:, 0:128], in_=s, identity=C.identb[:]),
                    reads=[src_b, C.identb[:]], writes=[pb[:, 0:128]])
                p.op("vector", lambda e, pb=pb, d=dst_b, f=fb: e.tensor_scalar(
                    out=d, in0=pb[:, 0:128], scalar1=lnw[:, f:f + 1], scalar2=lnb[:, f:f + 1],
                    op0=ALU.mult, op1=ALU.add),
                    reads=[pb[:, 0:128], lnw[:], lnb[:]], writes=[dst_b])


def build_layer0(C, x_src, x_dst):
    p = C.p
    aps = C.aps
    ph = Phase(C, "l0")
    xlnT = ph.sb("xlnT", [128, ND * T], BF16)
    g_tok = ph.sb("g_tok", [128, NT * D], BF16)
    lnw = ph.sb("lnw", [128, ND], F32)
    lnb = ph.sb("lnb", [128, ND], F32)
    p.op("sync", lambda e: e.dma_start(out=lnw[:], in_=aps["L0_lnw_col"][:]),
         writes=[lnw[:]], dma=True)
    p.op("sync", lambda e: e.dma_start(out=lnb[:], in_=aps["L0_lnb_col"][:]),
         writes=[lnb[:]], dma=True)
    phx = Phase(C, "l0x")
    xnT = phx.sb("xnT", [128, ND * T], BF16)
    phpre = Phase(C, "l0pre")
    make_xn(C, x_src, phpre, xnT, xlnT, (lnw, lnb))
    phpre.close()

    phf = Phase(C, "l0ffn")
    s1c = phf.sb("s1c", [128, 32], F32)
    s2c = phf.sb("s2c", [128, 32], F32)
    sh1c = phf.sb("sh1c", [128, 32], F32)
    sh2c = phf.sb("sh2c", [128, 32], F32)
    s3c = phf.sb("s3c", [128, 8], F32)
    for nm, t in [("L0_s1c", s1c), ("L0_s2c", s2c), ("L0_sh1c", sh1c),
                  ("L0_sh2c", sh2c), ("L0_s3c", s3c)]:
        p.op("sync", lambda e, t=t, nm=nm: e.dma_start(out=t[:], in_=aps[nm][:]),
             writes=[t[:]], dma=True)
    scr_a = aps["scr_a"]
    scr_b = aps["scr_b"]
    scrA = scr_a.rearrange("(r c) -> r c", c=T)
    scrB = scr_b.rearrange("(r c) -> r c", c=T)

    tA = [phf.sb(f"tA{i}", [128, 512], F32) for i in range(2)]
    tB = [phf.sb(f"tB{i}", [128, 512], F32) for i in range(2)]
    tC = [phf.sb(f"tC{i}", [128, 512], F32) for i in range(2)]
    ast = [phf.sb(f"ast{i}", [128, T], BF16) for i in range(2)]
    FH = 4096
    phsA = Phase(C, "l0sa")
    w1b = [phsA.sb(f"w1b{i}", [128, 8 * 512], BF16) for i in range(2)]
    w2b = [phsA.sb(f"w2b{i}", [128, 8 * 512], BF16) for i in range(2)]
    for mg in range(8):
        wl1 = wload(C, w1b[mg % 2], aps["L0_w1T"], D, 512, col0=mg * 512)
        wl2 = wload(C, w2b[mg % 2], aps["L0_w2T"], D, 512, col0=mg * 512)
        for mi in range(4):
            gm = mg * 4 + mi
            a_stage = ast[gm % 2]
            for nj in range(4):
                pA = C.psf()
                pB = C.psf()
                mm_accum(C, pA[:, 0:512],
                         [w1b[mg % 2][:, k * 512 + mi * 128: k * 512 + mi * 128 + 128]
                          for k in range(8)],
                         [xnT[:, k * T + nj * 512: k * T + (nj + 1) * 512] for k in range(8)])
                mm_accum(C, pB[:, 0:512],
                         [w2b[mg % 2][:, k * 512 + mi * 128: k * 512 + mi * 128 + 128]
                          for k in range(8)],
                         [xnT[:, k * T + nj * 512: k * T + (nj + 1) * 512] for k in range(8)])
                t1, t2 = tA[nj % 2], tB[nj % 2]
                scol = s1c[:, gm:gm + 1]
                p.op("vector", lambda e, pA=pA, t1=t1, sc=scol: e.tensor_scalar_mul(
                    out=t1[:], in0=pA[:, 0:512], scalar1=sc),
                    reads=[pA[:, 0:512], s1c[:]], writes=[t1[:]])
                p.op("scalar", lambda e, t1=t1: e.activation(out=t1[:], in_=t1[:], func=AF.Silu),
                     reads=[t1[:]], writes=[t1[:]])
                scol2 = s2c[:, gm:gm + 1]
                p.op("vector", lambda e, pB=pB, t2=t2, sc=scol2: e.tensor_scalar_mul(
                    out=t2[:], in0=pB[:, 0:512], scalar1=sc),
                    reads=[pB[:, 0:512], s2c[:]], writes=[t2[:]])
                dst = a_stage[:, nj * 512:(nj + 1) * 512]
                p.op("vector", lambda e, t1=t1, t2=t2, d=dst: e.tensor_tensor(
                    out=d, in0=t1[:], in1=t2[:], op=ALU.mult),
                    reads=[t1[:], t2[:]], writes=[dst])
            dstdram = scrA[gm * 128:(gm + 1) * 128, :]
            p.op("sync", lambda e, s=a_stage, d=dstdram: e.dma_start(out=d, in_=s[:]),
                 reads=[a_stage[:]], writes=[dstdram], dma=True)
    phsA.close()

    for stage, (wname, scale_c, src_scr, dst_scr) in enumerate(
            [("L0_h1T", sh1c, scrA, scrB), ("L0_h2T", sh2c, scrB, scrA)]):
        phs = Phase(C, f"l0st{stage}")
        whb = phs.sb("whb", [128, 32 * 512], BF16)
        abuf = phs.sb("abuf", [128, 32 * 512], BF16)
        for mg in range(8):
            wl = wload(C, whb, aps[wname], FH, 512, col0=mg * 512)
            for nj in range(4):
                asrc = src_scr[:, nj * 512:(nj + 1) * 512].rearrange("(r p) c -> p r c", p=128)
                abv = abuf[:, 0:32 * 512].rearrange("p (r c) -> p r c", c=512)
                p.op("gpsimd", lambda e, d=abv, s=asrc: e.dma_start(out=d, in_=s),
                     reads=[asrc], writes=[abuf[:]], dma=True)
                for mi in range(4):
                    gm = mg * 4 + mi
                    pA = C.psf()
                    mm_accum(C, pA[:, 0:512],
                             [whb[:, k * 512 + mi * 128: k * 512 + mi * 128 + 128]
                              for k in range(32)],
                             [abuf[:, k * 512:(k + 1) * 512] for k in range(32)])
                    t1 = tA[mi % 2]
                    scol = scale_c[:, gm:gm + 1]
                    p.op("vector", lambda e, pA=pA, t1=t1, sc=scol: e.tensor_scalar_mul(
                        out=t1[:], in0=pA[:, 0:512], scalar1=sc),
                        reads=[pA[:, 0:512], scale_c[:]], writes=[t1[:]])
                    dstd = dst_scr[gm * 128:(gm + 1) * 128, nj * 512:(nj + 1) * 512]
                    gst = ast[mi % 2]
                    gsl = gst[:, nj * 512:(nj + 1) * 512]
                    p.op("scalar", lambda e, t1=t1, d=gsl: e.activation(
                        out=d, in_=t1[:], func=AF.Gelu),
                        reads=[t1[:]], writes=[gsl])
                    p.op("sync", lambda e, s=gsl, d=dstd: e.dma_start(out=d, in_=s),
                         reads=[gsl], writes=[dstd], dma=True)
        phs.close()

    phsD = Phase(C, "l0sd")
    w3b = [phsD.sb(f"w3b{i}", [128, 32 * 128], BF16) for i in range(2)]
    cbuf = phsD.sb("cbuf", [128, 32 * 512], BF16)
    oTst = [phsD.sb(f"oT{i}", [128, 512], F32) for i in range(2)]
    for nj in range(4):
        csrc = scrA[:, nj * 512:(nj + 1) * 512].rearrange("(r p) c -> p r c", p=128)
        cb = cbuf
        cbv = cb[:, 0:32 * 512].rearrange("p (r c) -> p r c", c=512)
        p.op("gpsimd", lambda e, d=cbv, s=csrc: e.dma_start(out=d, in_=s),
             reads=[csrc], writes=[cb[:]], dma=True)
        for mi in range(8):
            w3m = w3b[mi % 2]
            wload(C, w3m, aps["L0_w3T"], FH, 128, col0=mi * 128)
            pA = C.psf()
            mm_accum(C, pA[:, 0:512],
                     [w3m[:, k * 128:(k + 1) * 128] for k in range(32)],
                     [cb[:, k * 512:(k + 1) * 512] for k in range(32)])
            ot = oTst[mi % 2]
            scol = s3c[:, mi:mi + 1]
            p.op("vector", lambda e, pA=pA, ot=ot, sc=scol: e.tensor_scalar_mul(
                out=ot[:], in0=pA[:, 0:512], scalar1=sc),
                reads=[pA[:, 0:512], s3c[:]], writes=[ot[:]])
            for tt in range(4):
                pb = C.psf()
                src_b = ot[:, tt * 128:(tt + 1) * 128]
                p.op("tensor", lambda e, s=src_b, pb=pb: e.transpose(
                    out=pb[:, 0:128], in_=s, identity=C.identf[:]),
                    reads=[src_b, C.identf[:]], writes=[pb[:, 0:128]])
                ti = nj * 4 + tt
                dst = g_tok[:, ti * D + mi * 128: ti * D + (mi + 1) * 128]
                p.op("scalar", lambda e, pb=pb, d=dst: e.activation(
                    out=d, in_=pb[:, 0:128], func=AF.Silu),
                    reads=[pb[:, 0:128]], writes=[dst])
    phsD.close()
    phf.close()
    phx.close()

    pha = Phase(C, "l0att")
    cos_sb = pha.sb("cos", [128, NT * 64], F32)
    sin_sb = pha.sb("sin", [128, NT * 64], F32)
    p.op("sync", lambda e: e.dma_start(
        out=cos_sb[:].rearrange("p (m j) -> p m j", j=64),
        in_=aps["L0_cos64"].rearrange("(m p) j -> p m j", p=128)),
        reads=[aps["L0_cos64"][:]], writes=[cos_sb[:]], dma=True)
    p.op("sync", lambda e: e.dma_start(
        out=sin_sb[:].rearrange("p (m j) -> p m j", j=64),
        in_=aps["L0_sinm64"].rearrange("(m p) j -> p m j", p=128)),
        reads=[aps["L0_sinm64"][:]], writes=[sin_sb[:]], dma=True)
    mk_g = pha.sb("mask_g", [128, 160], F32)
    mk_0 = pha.sb("mask_0", [128, 160], F32)
    p.op("sync", lambda e: e.dma_start(out=mk_g[:], in_=aps["L0_maskg"][:]),
         writes=[mk_g[:]], dma=True)
    p.op("sync", lambda e: e.dma_start(out=mk_0[:], in_=aps["L0_mask0"][:]),
         writes=[mk_0[:]], dma=True)

    ctxT = pha.sb("ctxT", [128, ND * T], BF16)
    stats3 = pha.sb("sfx", [128, 3 * 4], F32)
    for hg in range(4):
        phg = Phase(C, f"l0hg{hg}")
        wq = phg.sb("wq", [128, 8 * 256], BF16)
        wk = phg.sb("wk", [128, 8 * 256], BF16)
        wv = phg.sb("wv", [128, 8 * 256], BF16)
        wload(C, wq, aps["L0_wqkvT"], D, 256, col0=hg * 256)
        wload(C, wk, aps["L0_wqkvT"], D, 256, col0=1024 + hg * 256)
        wload(C, wv, aps["L0_wqkvT"], D, 256, col0=2048 + hg * 256)
        qg = phg.sb("qg", [128, NT * 256], BF16)
        kg = phg.sb("kg", [128, NT * 256], BF16)
        vg = phg.sb("vg", [128, NT * 256], BF16)
        vsh = phg.sb("vsh", [128, NT * 256], BF16)
        qst = [phg.sb(f"qst{i}", [128, 256], F32) for i in range(2)]
        sh_t = [phg.sb(f"sht{i}", [128, 256], F32) for i in range(2)]
        t1_t = [phg.sb(f"t1t{i}", [128, 256], F32) for i in range(2)]

        def rope_evac(which, psum_ap, mi, out_tensor):
            st = qst[mi % 2]
            sh = sh_t[mi % 2]
            t1 = t1_t[mi % 2]
            p.op("vector", lambda e, ps=psum_ap, st=st: e.tensor_copy(out=st[:], in_=ps),
                 reads=[psum_ap], writes=[st[:]])
            sv = st[:].rearrange("p (h t j) -> p h t j", h=4, t=2)
            shv = sh[:].rearrange("p (h t j) -> p h t j", h=4, t=2)
            p.op("vector", lambda e, sv=sv, shv=shv: e.tensor_copy(
                out=shv[:, :, 0:1, :], in_=sv[:, :, 1:2, :]),
                reads=[st[:]], writes=[sh[:]])
            p.op("vector", lambda e, sv=sv, shv=shv: e.tensor_copy(
                out=shv[:, :, 1:2, :], in_=sv[:, :, 0:1, :]),
                reads=[st[:]], writes=[sh[:]])
            cosb = cos_sb[:, mi * 64:(mi + 1) * 64].unsqueeze(1).broadcast_to((128, 4, 64))
            sinb = sin_sb[:, mi * 64:(mi + 1) * 64].unsqueeze(1).broadcast_to((128, 4, 64))
            stv = st[:].rearrange("p (h j) -> p h j", h=4)
            shv2 = sh[:].rearrange("p (h j) -> p h j", h=4)
            t1v = t1[:].rearrange("p (h j) -> p h j", h=4)
            p.op("vector", lambda e: e.tensor_tensor(out=t1v, in0=stv, in1=cosb, op=ALU.mult),
                 reads=[st[:], cos_sb[:]], writes=[t1[:]])
            p.op("vector", lambda e: e.tensor_tensor(out=shv2, in0=shv2, in1=sinb, op=ALU.mult),
                 reads=[sh[:], sin_sb[:]], writes=[sh[:]])
            dst = out_tensor[:, mi * 256:(mi + 1) * 256]
            p.op("vector", lambda e, d=dst: e.tensor_tensor(out=d, in0=t1[:], in1=sh[:],
                                                            op=ALU.add),
                 reads=[t1[:], sh[:]], writes=[dst])

        for mi in range(NT):
            for which, wsb, out_t in [("q", wq, qg), ("k", wk, kg), ("v", wv, vg)]:
                pA = C.psf()
                mm_accum(C, pA[:, 0:256],
                         [xlnT[:, k * T + mi * 128: k * T + mi * 128 + 128] for k in range(8)],
                         [wsb[:, k * 256:(k + 1) * 256] for k in range(8)])
                if which == "v":
                    dst = vg[:, mi * 256:(mi + 1) * 256]
                    p.op("vector", lambda e, ps=pA, d=dst: e.tensor_copy(out=d, in_=ps[:, 0:256]),
                         reads=[pA[:, 0:256]], writes=[dst])
                else:
                    rope_evac(which, pA[:, 0:256], mi, out_t)
        p.op("gpsimd", lambda e, d=vsh[32:128, 0:NT * 256], s=vg[0:96, 0:NT * 256]:
             e.dma_start(out=d, in_=s),
             reads=[vg[:]], writes=[vsh[:]], dma=True)
        p.op("gpsimd", lambda e, d=vsh[0:32, 256:NT * 256], s=vg[96:128, 0:(NT - 1) * 256]:
             e.dma_start(out=d, in_=s),
             reads=[vg[:]], writes=[vsh[:]], dma=True)
        qT = phg.sb("qT", [128, 2 * T], BF16)
        kT = phg.sb("kT", [128, 2 * T], BF16)
        for mi in range(NT):
            for b2 in range(2):
                src_q = qg[:, mi * 256 + b2 * 128: mi * 256 + (b2 + 1) * 128]
                transpose128(C, pha, src_q, qT[:, b2 * T + mi * 128:b2 * T + (mi + 1) * 128], BF16)
                src_k = kg[:, mi * 256 + b2 * 128: mi * 256 + (b2 + 1) * 128]
                transpose128(C, pha, src_k, kT[:, b2 * T + mi * 128:b2 * T + (mi + 1) * 128], BF16)
        scf = [phg.sb(f"scf{i}", [128, 160], F32) for i in range(2)]
        pbf = [phg.sb(f"pbf{i}", [128, 160], BF16) for i in range(2)]
        ptp = [phg.sb(f"ptp{i}", [128, 128], BF16) for i in range(2)]
        ptc = [phg.sb(f"ptc{i}", [128, 128], BF16) for i in range(2)]
        for h4 in range(4):
            h = hg * 4 + h4
            rb = (h4 % 2) * 64
            blk = h4 // 2
            for mi in range(NT):
                ps = C.psf()
                lh = qT[rb:rb + 64, blk * T + mi * 128: blk * T + (mi + 1) * 128]
                if mi == 0:
                    rh = kT[rb:rb + 64, blk * T + 0: blk * T + 128]
                    C.p.op("tensor", lambda e, o=ps[:, 32:160], lh=lh, rh=rh: e.matmul(
                        out=o, lhsT=lh, rhs=rh, start=True, stop=True),
                        reads=[lh, rh], writes=[ps[:, 0:160]])
                else:
                    rh = kT[rb:rb + 64, blk * T + mi * 128 - 32: blk * T + (mi + 1) * 128]
                    C.p.op("tensor", lambda e, o=ps[:, 0:160], lh=lh, rh=rh: e.matmul(
                        out=o, lhsT=lh, rhs=rh, start=True, stop=True),
                        reads=[lh, rh], writes=[ps[:, 0:160]])
                sc = scf[mi % 2]
                pb16 = pbf[mi % 2]
                p.op("scalar", lambda e, ps=ps, sc=sc: e.activation(
                    out=sc[:], in_=ps[:, 0:160], func=AF.Identity, scale=0.125),
                    reads=[ps[:, 0:160]], writes=[sc[:]])
                if mi == 0:
                    p.op("vector", lambda e, sc=sc: e.memset(sc[:, 0:32], -1e30),
                         reads=[], writes=[sc[:, 0:32]])
                mk = mk_0 if mi == 0 else mk_g
                p.op("vector", lambda e, sc=sc, mk=mk: e.tensor_tensor(
                    out=sc[:], in0=sc[:], in1=mk[:], op=ALU.add),
                    reads=[sc[:], mk[:]], writes=[sc[:]])
                softmax_rows(C, phg, sc[:], 160, pb16[:], stats3, mi % 4)
                ppv = ptp[mi % 2]
                pcv = ptc[mi % 2]
                pbb = C.psb()
                p.op("tensor", lambda e, s=pb16, pb=pbb: e.transpose(
                    out=pb[0:32, 0:128], in_=s[:, 0:32], identity=C.identb[:]),
                    reads=[pb16[:], C.identb[:]], writes=[pbb[:, 0:128]])
                p.op("vector", lambda e, pb=pbb, d=ppv: e.tensor_copy(
                    out=d[0:32, :], in_=pb[0:32, 0:128]),
                    reads=[pbb[:, 0:128]], writes=[ppv[:]])
                pbb2 = C.psb()
                p.op("tensor", lambda e, s=pb16, pb=pbb2: e.transpose(
                    out=pb[:, 0:128], in_=s[:, 32:160], identity=C.identb[:]),
                    reads=[pb16[:], C.identb[:]], writes=[pbb2[:, 0:128]])
                p.op("vector", lambda e, pb=pbb2, d=pcv: e.tensor_copy(
                    out=d[:], in_=pb[:, 0:128]),
                    reads=[pbb2[:, 0:128]], writes=[pcv[:]])
                pc = C.psf()
                if mi > 0:
                    lh1 = vsh[0:32, mi * 256 + h4 * 64: mi * 256 + (h4 + 1) * 64]
                    C.p.op("tensor", lambda e, o=pc[0:64, 0:128], lh=lh1, rh=ppv: e.matmul(
                        out=o, lhsT=lh, rhs=rh[0:32, :], start=True, stop=False),
                        reads=[lh1, ppv[:]], writes=[pc[:, 0:128]])
                lh2 = vg[:, mi * 256 + h4 * 64: mi * 256 + (h4 + 1) * 64]
                C.p.op("tensor", lambda e, o=pc[0:64, 0:128], lh=lh2, rh=pcv,
                       s=(mi == 0): e.matmul(out=o, lhsT=lh, rhs=rh[:], start=s, stop=True),
                       reads=[lh2, pcv[:]], writes=[pc[:, 0:128]])
                rbase = (h % 2) * 64
                dst = ctxT[rbase:rbase + 64, (h // 2) * T + mi * 128:(h // 2) * T + (mi + 1) * 128]
                p.op("vector", lambda e, ps=pc, d=dst: e.tensor_copy(out=d, in_=ps[0:64, 0:128]),
                     reads=[pc[:, 0:128]], writes=[dst])
        phg.close()

    wob = pha.sb("wob", [128, 8 * 1024], BF16)
    wload(C, wob, aps["L0_woutT"], D, 1024)
    sst = [pha.sb(f"sst{i}", [128, D], F32) for i in range(2)]
    xst2 = [pha.sb(f"xst2{i}", [128, D], F32) for i in range(2)]
    xnt2 = [pha.sb(f"xnt2{i}", [128, D], F32) for i in range(2)]
    sqt2 = [pha.sb(f"sqt2{i}", [128, D], F32) for i in range(2)]
    stats4 = pha.sb("stats4", [128, 4 * NT], F32)
    for mi in range(NT):
        xs = xst2[mi % 2]
        src = x_src[mi * 128:(mi + 1) * 128, :]
        p.op("sync", lambda e, s=xs, r=src: e.dma_start(out=s[:], in_=r),
             reads=[src], writes=[xs[:]], dma=True)
        xn = xnt2[mi % 2]
        ln_tile(C, pha, xs[:], xn[:], sqt2[mi % 2][:], stats4, mi)
        ss = sst[mi % 2]
        for nj in range(2):
            pA = C.psf()
            mm_accum(C, pA[:, 0:512],
                     [ctxT[:, k * T + mi * 128: k * T + mi * 128 + 128] for k in range(8)],
                     [wob[:, k * 1024 + nj * 512: k * 1024 + (nj + 1) * 512] for k in range(8)])
            dsl = ss[:, nj * 512:(nj + 1) * 512]
            xnsl = xn[:, nj * 512:(nj + 1) * 512]
            p.op("vector", lambda e, ps=pA, d=dsl, x2=xnsl: e.tensor_tensor(
                out=d, in0=ps[:, 0:512], in1=x2, op=ALU.add),
                reads=[pA[:, 0:512], xnsl], writes=[dsl])
        gsl = g_tok[:, mi * D:(mi + 1) * D]
        p.op("vector", lambda e, s=ss, g=gsl: e.tensor_tensor(out=s[:], in0=s[:], in1=g,
                                                              op=ALU.mult),
             reads=[ss[:], gsl], writes=[ss[:]])
        p.op("vector", lambda e, s=ss, x=xs: e.tensor_tensor(out=x[:], in0=x[:], in1=s[:],
                                                             op=ALU.add),
             reads=[ss[:], xs[:]], writes=[xs[:]])
        dst = x_dst[mi * 128:(mi + 1) * 128, :]
        p.op("sync", lambda e, s=xs, d=dst: e.dma_start(out=d, in_=s[:]),
             reads=[xs[:]], writes=[dst], dma=True)
    pha.close()
    ph.close()


L1_DBG = None


def _dump_flat(C, src_ap, x_dst, nelem):
    dst = x_dst.rearrange("a b -> (a b)")[0:nelem]
    C.p.op("gpsimd", lambda e: e.dma_start(out=dst, in_=src_ap),
           reads=[src_ap], writes=[dst], dma=True)


def build_layer1(C, x_src, x_dst):
    p = C.p
    aps = C.aps
    NH, DH, FF, NE = 16, 64, 4096, 8
    s_scr = aps["s_scr"].rearrange("(r c) -> r c", c=D)
    ctx2d = aps["ctx_scr"].rearrange("(r c) -> r c", c=ND * T)

    ph = Phase(C, "l1")
    phH = Phase(C, "l1H")
    h_tok = phH.sb("h_tok", [128, NT * D], BF16)
    phB = Phase(C, "l1B")
    vbuf = phB.sb("vbuf", [128, NT * D], BF16)
    arenaA = phB.sb("arenaA", [128, ND * T], BF16)
    arenaB = phB.sb("arenaB", [128, ND * T], BF16)
    phx = Phase(C, "l1x")
    xnT = phx.sb("xnT", [128, ND * T], BF16)
    phpre = Phase(C, "l1pre")
    make_xn(C, x_src, phpre, xnT)
    phpre.close()

    pha = Phase(C, "l1att")
    qT = arenaA
    kT = arenaB
    v_tok = vbuf
    wbuf = pha.sb("wbuf", [128, 8 * 1024], BF16)
    bcol = pha.sb("bcol", [128, 2 * ND], F32)
    p.op("sync", lambda e: e.dma_start(out=bcol[:, 0:ND], in_=aps["L1_bqc"][:]),
         writes=[bcol[:, 0:ND]], dma=True)
    p.op("sync", lambda e: e.dma_start(out=bcol[:, ND:2 * ND], in_=aps["L1_bkc"][:]),
         writes=[bcol[:, ND:2 * ND]], dma=True)
    bvrow = pha.sb("bvrow", [1, D], BF16)
    p.op("gpsimd", lambda e: e.dma_start(out=bvrow[:], in_=aps["L1_bv_row"][:]),
         reads=[aps["L1_bv_row"][:]], writes=[bvrow[:]], dma=True)
    for wi, (wname, dstT, bc0) in enumerate([("L1_wqT", qT, 0), ("L1_wkT", kT, ND)]):
        wload(C, wbuf, aps[wname], D, 1024)
        for m in range(ND):
            for nj in range(4):
                pA = C.psf()
                mm_accum(C, pA[:, 0:512],
                         [wbuf[:, k * 1024 + m * 128: k * 1024 + m * 128 + 128]
                          for k in range(8)],
                         [xnT[:, k * T + nj * 512: k * T + (nj + 1) * 512] for k in range(8)])
                dst = dstT[:, m * T + nj * 512: m * T + (nj + 1) * 512]
                bc = bcol[:, bc0 + m: bc0 + m + 1]
                p.op("vector", lambda e, ps=pA, d=dst, b=bc: e.tensor_scalar(
                    out=d, in0=ps[:, 0:512], scalar1=b, scalar2=None, op0=ALU.add),
                    reads=[pA[:, 0:512], bcol[:]], writes=[dst])
    wload(C, wbuf, aps["L1_wvT"], D, 1024)
    for mi in range(NT):
        for nj in range(2):
            pA = C.psf()
            lhs = [xnT[:, k * T + mi * 128: k * T + mi * 128 + 128] for k in range(8)]
            rhs = [wbuf[:, k * 1024 + nj * 512: k * 1024 + (nj + 1) * 512] for k in range(8)]
            lhs.append(C.ones_b[0:1, 0:128])
            rhs.append(bvrow[0:1, nj * 512:(nj + 1) * 512])
            mm_accum(C, pA[:, 0:512], lhs, rhs)
            dst = v_tok[:, mi * D + nj * 512: mi * D + (nj + 1) * 512]
            p.op("vector", lambda e, ps=pA, d=dst: e.tensor_copy(out=d, in_=ps[:, 0:512]),
                 reads=[pA[:, 0:512]], writes=[dst])
    sc = pha.sb("sc", [128, T], F32)
    pbf = pha.sb("pbf", [128, T], BF16)
    sfx = pha.sb("sfx", [128, 6], F32)
    sst = [pha.sb(f"sst{i}", [128, D], BF16) for i in range(2)]
    for mi in range(NT):
        for nj in range(4):
            pA = C.psf()
            mm_accum(C, pA[:, 0:512],
                     [qT[:, k * T + mi * 128: k * T + mi * 128 + 128] for k in range(8)],
                     [kT[:, k * T + nj * 512: k * T + (nj + 1) * 512] for k in range(8)])
            dst = sc[:, nj * 512:(nj + 1) * 512]
            p.op("scalar", lambda e, ps=pA, d=dst: e.copy(out=d, in_=ps[:, 0:512]),
                 reads=[pA[:, 0:512]], writes=[dst])
        softmax_rows(C, pha, sc[:], T, pbf[:], sfx, 0)
        for g4 in range(4):
            transpose_batch(C, [pbf[:, (g4 * 4 + j) * 128:(g4 * 4 + j + 1) * 128]
                                for j in range(4)],
                            pbf[:, g4 * 512:(g4 + 1) * 512], BF16)
        ssl = sst[mi % 2]
        for nj in range(2):
            pA = C.psf()
            mm_accum(C, pA[:, 0:512],
                     [pbf[:, kb * 128:(kb + 1) * 128] for kb in range(16)],
                     [v_tok[:, kb * D + nj * 512: kb * D + (nj + 1) * 512] for kb in range(16)])
            dst = ssl[:, nj * 512:(nj + 1) * 512]
            p.op("vector", lambda e, ps=pA, d=dst: e.tensor_copy(out=d, in_=ps[:, 0:512]),
                 reads=[pA[:, 0:512]], writes=[dst])
        ddst = s_scr[mi * 128:(mi + 1) * 128, :]
        p.op("sync", lambda e, s=ssl, d=ddst: e.dma_start(out=d, in_=s[:]),
             reads=[ssl[:]], writes=[ddst], dma=True)
    pha.close()
    if L1_DBG == "s":
        _dump_flat(C, aps["s_scr"][0:T * D], x_dst, T * D)
        phx.close(); phB.close(); phH.close(); ph.close()
        return

    phq = Phase(C, "l1qkv")
    fv = vbuf
    wqkvb = [phq.sb("wqkvb0", [128, 8 * 512], BF16)]
    for cg in range(2):
        wb2 = wqkvb[0]
        wload(C, wb2, aps["L1_wqkv"], D, 512, col0=2048 + cg * 512)
        for mi in range(NT):
            pA = C.psf()
            mm_accum(C, pA[:, 0:512],
                     [xnT[:, k * T + mi * 128: k * T + mi * 128 + 128] for k in range(8)],
                     [wb2[:, k * 512:(k + 1) * 512] for k in range(8)])
            dst = fv[:, mi * D + cg * 512: mi * D + (cg + 1) * 512]
            p.op("vector", lambda e, ps=pA, d=dst: e.tensor_copy(out=d, in_=ps[:, 0:512]),
                 reads=[pA[:, 0:512]], writes=[dst])
    scrbuf = phq.sb("scrbuf", [128, 8 * 512], BF16)
    peb = scrbuf
    ucol = phq.sb("ucol", [128, ND], F32)
    vcol = phq.sb("vcol", [128, ND], F32)
    p.op("sync", lambda e: e.dma_start(out=ucol[:], in_=aps["L1_uc"][:]),
         writes=[ucol[:]], dma=True)
    p.op("sync", lambda e: e.dma_start(out=vcol[:], in_=aps["L1_vc"][:]),
         writes=[vcol[:]], dma=True)
    tri = phq.sb("tri", [128, 128], F32)
    p.op("sync", lambda e: e.dma_start(out=tri[:], in_=aps["diagbias"][:]),
         writes=[tri[:]], dma=True)
    zz = phq.sb("zz", [128, 2065], BF16)
    p.op("vector", lambda e: e.memset(zz[:], 0.0), writes=[zz[:]])
    for shd0 in (aps["scr_sh0"], aps["scr_sh1"]):
        dstz = shd0[0:128 * 2065].rearrange("(r c) -> r c", c=2065)
        p.op("gpsimd", lambda e, d=dstz: e.dma_start(out=d, in_=zz[:]),
             reads=[zz[:]], writes=[dstz], dma=True)
    qu = phq.sb("qu", [128, T], BF16)
    qv = phq.sb("qv", [128, T], BF16)
    rawst = scrbuf[:, 0:T]
    shf = scrbuf[:, T:2 * T]
    scc = phq.sb("scc", [128, T], F32)
    pbfc = phq.sb("pbfc", [128, T], BF16)
    sfx2 = phq.sb("sfx2", [128, 6], F32)
    ctxh = phq.sb("ctxh", [64, T], BF16)
    for t_ in (qu, qv, scc, pbfc, ctxh):
        p.op("vector", lambda e, t_=t_: e.memset(t_[:], 0.0), writes=[t_[:]])
    shsc = [aps["scr_sh0"], aps["scr_sh1"]]
    NHG = 4
    for g2 in range(2):
        qTg = arenaA[:, 0:4 * T]
        kTg = arenaA[:, 4 * T:8 * T]
        rTg = arenaB[:, 0:4 * T]
        for (dstT, dram, c0) in [(qTg, "L1_wqkv", g2 * 512),
                                 (kTg, "L1_wqkv", 1024 + g2 * 512),
                                 (rTg, "L1_wr", g2 * 512)]:
            wb2 = wqkvb[0]
            wload(C, wb2, aps[dram], D, 512, col0=c0)
            if dram == "L1_wr":
                for nj in range(4):
                    pe_sl = aps["L1_peT"][:, nj * 512:(nj + 1) * 512].rearrange(
                        "(r p) c -> p r c", p=128)
                    pev = peb[:].rearrange("p (r c) -> p r c", c=512)
                    p.op("gpsimd", lambda e, d=pev, s=pe_sl: e.dma_start(out=d, in_=s),
                         reads=[aps["L1_peT"][:, nj * 512:(nj + 1) * 512]],
                         writes=[peb[:]], dma=True)
                    for m4 in range(4):
                        pA = C.psf()
                        mm_accum(C, pA[:, 0:512],
                                 [wb2[:, k * 512 + m4 * 128: k * 512 + m4 * 128 + 128]
                                  for k in range(8)],
                                 [peb[:, k * 512:(k + 1) * 512] for k in range(8)])
                        dst = dstT[:, m4 * T + nj * 512: m4 * T + (nj + 1) * 512]
                        p.op("vector", lambda e, ps=pA, d=dst: e.tensor_copy(
                            out=d, in_=ps[:, 0:512]),
                            reads=[pA[:, 0:512]], writes=[dst])
            else:
                for m4 in range(4):
                    for nj in range(4):
                        pA = C.psf()
                        mm_accum(C, pA[:, 0:512],
                                 [wb2[:, k * 512 + m4 * 128: k * 512 + m4 * 128 + 128]
                                  for k in range(8)],
                                 [xnT[:, k * T + nj * 512: k * T + (nj + 1) * 512]
                                  for k in range(8)])
                        dst = dstT[:, m4 * T + nj * 512: m4 * T + (nj + 1) * 512]
                        p.op("vector", lambda e, ps=pA, d=dst: e.tensor_copy(
                            out=d, in_=ps[:, 0:512]),
                            reads=[pA[:, 0:512]], writes=[dst])
        if L1_DBG in ("qg", "kg", "rg") and g2 == 0:
            srcT = {"qg": qTg, "kg": kTg, "rg": rTg}[L1_DBG]
            _dump_flat(C, srcT, x_dst, 128 * 4 * T)
            phq.close(); phx.close(); phB.close(); phH.close(); ph.close()
            return
        for j in range(8):
            h = g2 * 8 + j
            rb = (j % 2) * 64
            blk = j // 2
            usl = ucol[(h % 2) * 64:(h % 2) * 64 + 64, (h // 2):(h // 2) + 1]
            vsl = vcol[(h % 2) * 64:(h % 2) * 64 + 64, (h // 2):(h // 2) + 1]
            qsl = qTg[rb:rb + 64, blk * T:(blk + 1) * T]
            p.op("vector", lambda e, d=qu[rb:rb + 64, :], s=qsl, b=usl: e.tensor_scalar(
                out=d, in0=s, scalar1=b, scalar2=None, op0=ALU.add),
                reads=[qsl, ucol[:]], writes=[qu[:]])
            p.op("vector", lambda e, d=qv[rb:rb + 64, :], s=qsl, b=vsl: e.tensor_scalar(
                out=d, in0=s, scalar1=b, scalar2=None, op0=ALU.add),
                reads=[qsl, vcol[:]], writes=[qv[:]])
            for mi in range(NT):
                W = (mi + 1) * 128
                c0 = (T - W) // 512
                ncj = (mi + 4) // 4
                shd = shsc[mi % 2]
                for cj in range(c0, 4):
                    pA = C.psf()
                    lh = qv[rb:rb + 64, mi * 128:(mi + 1) * 128]
                    rh = rTg[rb:rb + 64, blk * T + cj * 512: blk * T + (cj + 1) * 512]
                    C.p.op("tensor", lambda e, o=pA[:, 0:512], lh=lh, rh=rh: e.matmul(
                        out=o, lhsT=lh, rhs=rh, start=True, stop=True),
                        reads=[lh, rh], writes=[pA[:, 0:512]])
                    dst = rawst[:, cj * 512:(cj + 1) * 512]
                    p.op("vector", lambda e, ps=pA, d=dst: e.tensor_copy(out=d,
                                                                         in_=ps[:, 0:512]),
                         reads=[pA[:, 0:512]], writes=[dst])
                wr = aps_pitch_write_view(shd, c0)
                p.op("gpsimd", lambda e, d=wr, s=rawst[:, c0 * 512:T]: e.dma_start(
                    out=d, in_=s),
                    reads=[rawst], writes=[wr], dma=True)
                rd = shd[T - mi * 128: T - mi * 128 + 128 * T].rearrange(
                    "(r c) -> r c", c=T)[:, 0:W]
                p.op("gpsimd", lambda e, d=shf[:, 0:W], s=rd: e.dma_start(out=d, in_=s),
                     reads=[rd], writes=[shf], dma=True)
                for cj in range(ncj):
                    wv = min(512, W - cj * 512)
                    pA = C.psf()
                    lh = qu[rb:rb + 64, mi * 128:(mi + 1) * 128]
                    rh = kTg[rb:rb + 64, blk * T + cj * 512: blk * T + cj * 512 + wv]
                    C.p.op("tensor", lambda e, o=pA[:, 0:wv], lh=lh, rh=rh: e.matmul(
                        out=o, lhsT=lh, rhs=rh, start=True, stop=True),
                        reads=[lh, rh], writes=[pA[:, 0:wv]])
                    dsc = scc[:, cj * 512:cj * 512 + wv]
                    p.op("vector", lambda e, ps=pA[:, 0:wv], d=dsc,
                         b=shf[:, cj * 512:cj * 512 + wv]: e.tensor_tensor(
                             out=d, in0=ps, in1=b, op=ALU.add),
                         reads=[pA[:, 0:wv], shf], writes=[dsc])
                if L1_DBG == "sh15" and h == 0 and mi == 15:
                    _dump_flat(C, shf[:, 0:T], x_dst, 128 * T)
                    phq.close(); phx.close(); phB.close(); phH.close(); ph.close()
                    return
                dg = scc[:, mi * 128:W]
                p.op("vector", lambda e, d=dg: e.tensor_tensor(out=d, in0=d, in1=tri[:],
                                                               op=ALU.add),
                     reads=[dg, tri[:]], writes=[dg])
                if L1_DBG == "sc15" and h == 0 and mi == 15:
                    _dump_flat(C, scc[:, 0:T], x_dst, 128 * T)
                    phq.close(); phx.close(); phB.close(); phH.close(); ph.close()
                    return
                nmx = sfx2[:, 0:1]
                ssum = sfx2[:, 1:2]
                rsx = sfx2[:, 2:3]
                scw = scc[:, 0:W]
                pbw = pbfc[:, 0:W]
                p.op("vector", lambda e, i_=scw, o=nmx: e.reduce_max(out=o, in_=i_, axis=AX.X,
                                                                     negate=True),
                     reads=[scw], writes=[nmx])
                p.op("vector", lambda e, o=nmx: e.tensor_scalar_mul(out=o, in0=o, scalar1=0.125),
                     reads=[nmx], writes=[nmx])
                p.op("scalar", lambda e, o=pbw, i_=scw, b=nmx, a=ssum: e.activation(
                    out=o, in_=i_, func=AF.Exp, bias=b, scale=0.125, accum_out=a),
                     reads=[scw, nmx], writes=[pbw, ssum])
                p.op("vector", lambda e, o=rsx, i_=ssum: e.reciprocal(out=o, in_=i_),
                     reads=[ssum], writes=[rsx])
                p.op("vector", lambda e, o=pbw, s=rsx: e.tensor_scalar_mul(out=o, in0=o,
                                                                           scalar1=s),
                     reads=[pbw, rsx], writes=[pbw])
                if L1_DBG == "p15" and h == 0 and mi == 15:
                    _dump_flat(C, pbfc[:, 0:T], x_dst, 128 * T)
                    phq.close(); phx.close(); phB.close(); phH.close(); ph.close()
                    return
                for g4 in range((mi + 4) // 4):
                    nb = min(4, mi + 1 - g4 * 4)
                    transpose_batch(C, [pbfc[:, (g4 * 4 + i2) * 128:(g4 * 4 + i2 + 1) * 128]
                                        for i2 in range(nb)],
                                    pbfc[:, g4 * 512:g4 * 512 + nb * 128], BF16)
                if L1_DBG == "pt15" and h == 0 and mi == 15:
                    _dump_flat(C, pbfc[:, 0:T], x_dst, 128 * T)
                    phq.close(); phx.close(); phB.close(); phH.close(); ph.close()
                    return
                if L1_DBG == "fv0" and h == 0 and mi == 0:
                    _dump_flat(C, fv[:, 0:NT * D], x_dst, T * D)
                    phq.close(); phx.close(); phB.close(); phH.close(); ph.close()
                    return
                pc = C.psf()
                for kb in range(mi + 1):
                    lh = fv[:, kb * D + h * 64: kb * D + (h + 1) * 64]
                    rh = pbfc[:, kb * 128:(kb + 1) * 128]
                    C.p.op("tensor", lambda e, o=pc[0:64, 0:128], lh=lh, rh=rh,
                           s=(kb == 0), t=(kb == mi): e.matmul(out=o, lhsT=lh, rhs=rh,
                                                               start=s, stop=t),
                           reads=[lh, rh], writes=[pc[:, 0:128]])
                dst = ctxh[0:64, mi * 128:(mi + 1) * 128]
                p.op("vector", lambda e, ps=pc, d=dst: e.tensor_copy(out=d,
                                                                     in_=ps[0:64, 0:128]),
                     reads=[pc[:, 0:128]], writes=[dst])
                if L1_DBG is not None and L1_DBG.startswith("cc") and h == 0 \
                        and mi == int(L1_DBG[2:]):
                    _dump_flat(C, ctxh[0:64, :], x_dst, 64 * T)
                    phq.close(); phx.close(); phB.close(); phH.close(); ph.close()
                    return
                if L1_DBG == "pt0" and h == 0 and mi == 0:
                    _dump_flat(C, pbfc[:, 0:128], x_dst, 128 * 128)
                    phq.close(); phx.close(); phB.close(); phH.close(); ph.close()
                    return
            if L1_DBG == "ctxh0" and h == 0:
                _dump_flat(C, ctxh[0:64, :], x_dst, 64 * T)
                phq.close(); phx.close(); phB.close(); phH.close(); ph.close()
                return
            grb = (h % 2) * 64
            ddst = ctx2d[grb:grb + 64, (h // 2) * T:(h // 2 + 1) * T]
            p.op("sync", lambda e, s=ctxh[0:64, :], d=ddst: e.dma_start(out=d, in_=s),
                 reads=[ctxh[:]], writes=[ddst], dma=True)
    phq.close()
    phx.close()
    phB.close()
    p.fence()
    if L1_DBG == "ctx":
        _dump_flat(C, aps["ctx_scr"][0:128 * ND * T], x_dst, 128 * ND * T)
        phH.close(); ph.close()
        return

    phh = Phase(C, "l1h")
    g1bc = make_bcast(C, phh, "g1", aps["L1_ln1g_row"][:])
    b1bc = make_bcast(C, phh, "b1", aps["L1_ln1b_row"][:])
    wob = phh.sb("wob", [128, 8 * 1024], BF16)
    wload(C, wob, aps["L1_wo"], D, 1024)
    ctt = [phh.sb(f"ctt{i}", [128, ND * 128], BF16) for i in range(2)]
    xst = [phh.sb(f"xst{i}", [128, D], F32) for i in range(2)]
    sqt = [phh.sb(f"sqt{i}", [128, D], F32) for i in range(2)]
    xnt = [phh.sb(f"xnt{i}", [128, D], F32) for i in range(2)]
    stats5 = phh.sb("stats5", [128, 4 * NT], F32)
    stats6 = phh.sb("stats6", [128, 4 * NT], F32)
    for mi in range(NT):
        ct = ctt[mi % 2]
        csrc = ctx2d[:, :].rearrange("p (b t) -> p b t", t=T)[:, :, mi * 128:(mi + 1) * 128]
        cdv = ct[:].rearrange("p (b t) -> p b t", t=128)
        p.op("sync", lambda e, d=cdv, s=csrc: e.dma_start(out=d, in_=s),
             reads=[ctx2d[:, :]], writes=[ct[:]], dma=True)
        xs = xst[mi % 2]
        src = x_src[mi * 128:(mi + 1) * 128, :]
        p.op("sync", lambda e, s=xs, r=src: e.dma_start(out=s[:], in_=r),
             reads=[src], writes=[xs[:]], dma=True)
        xn = xnt[mi % 2]
        ln_tile(C, phh, xs[:], xn[:], sqt[mi % 2][:], stats5, mi)
        for nj in range(2):
            pA = C.psf()
            mm_accum(C, pA[:, 0:512],
                     [ct[:, k * 128:(k + 1) * 128] for k in range(8)],
                     [wob[:, k * 1024 + nj * 512: k * 1024 + (nj + 1) * 512]
                      for k in range(8)])
            dsl = xn[:, nj * 512:(nj + 1) * 512]
            p.op("vector", lambda e, ps=pA, d=dsl: e.tensor_tensor(
                out=d, in0=ps[:, 0:512], in1=d, op=ALU.add),
                reads=[pA[:, 0:512], dsl], writes=[dsl])
        hsl = h_tok[:, mi * D:(mi + 1) * D]
        ln_tile(C, phh, xn[:], hsl, sqt[mi % 2][:], stats6, mi, affine=(g1bc[:], b1bc[:]))
    phh.close()
    p.fence()
    if L1_DBG == "h":
        _dump_flat(C, h_tok[:], x_dst, T * D)
        phH.close(); ph.close()
        return

    phe = Phase(C, "l1ff")
    hT = phe.sb("hT", [128, ND * T], BF16)
    for mi in range(NT):
        for fb in range(ND):
            src_b = h_tok[:, mi * D + fb * 128: mi * D + (fb + 1) * 128]
            pbt = C.psb()
            pb = pbt[:, 0:128]
            C.p.op("tensor", lambda e, s=src_b, o=pb: e.transpose(
                out=o, in_=s, identity=C.identb[:]),
                reads=[src_b, C.identb[:]], writes=[pb])
            dst = hT[:, fb * T + mi * 128: fb * T + (mi + 1) * 128]
            p.op("vector", lambda e, s=pb, d=dst: e.tensor_copy(out=d, in_=s),
                 reads=[pb], writes=[dst])
    g2bc = make_bcast(C, phe, "g2", aps["L1_ln2g_row"][:])
    b2bc = make_bcast(C, phe, "b2", aps["L1_ln2b_row"][:])
    bff1c = phe.sb("bff1c", [128, 32], F32)
    p.op("sync", lambda e: e.dma_start(out=bff1c[:], in_=aps["L1_bff1c"][:]),
         writes=[bff1c[:]], dma=True)
    bff2row = phe.sb("bff2row", [1, D], BF16)
    p.op("gpsimd", lambda e: e.dma_start(out=bff2row[:], in_=aps["L1_bff2_row"][:]),
         reads=[aps["L1_bff2_row"][:]], writes=[bff2row[:]], dma=True)
    f1c = phe.sb("f1c", [128, 32 * 512], BF16)
    wf1b = [phe.sb(f"wf1b{i}", [128, 8 * 512], BF16) for i in range(2)]
    wf2b = [phe.sb(f"wf2b{i}", [128, 1024], BF16) for i in range(2)]
    stats7 = phe.sb("stats7", [128, 4 * NT], F32)
    sq2 = [phe.sb(f"sq2{i}", [128, D], F32) for i in range(2)]
    tmpf = [phe.sb(f"tmpf{i}", [128, D], F32) for i in range(2)]
    for tc in range(4):
        t0 = tc * 512
        for mg in range(8):
            wb3 = wf1b[mg % 2]
            wload(C, wb3, aps["L1_wff1"], D, 512, col0=mg * 512)
            for m4 in range(4):
                fch = mg * 4 + m4
                pA = C.psf()
                mm_accum(C, pA[:, 0:512],
                         [wb3[:, k * 512 + m4 * 128: k * 512 + m4 * 128 + 128]
                          for k in range(8)],
                         [hT[:, k * T + t0: k * T + t0 + 512] for k in range(8)])
                dst = f1c[:, fch * 512:(fch + 1) * 512]
                bsl = bff1c[:, fch:fch + 1]
                p.op("scalar", lambda e, ps=pA, d=dst, b=bsl: e.activation(
                    out=d, in_=ps[:, 0:512], func=AF.Relu, bias=b),
                    reads=[pA[:, 0:512], bff1c[:]], writes=[dst])
        for k in range(32):
            wb4 = wf2b[k % 2]
            src = aps["L1_wff2"][k * 128:(k + 1) * 128, :]
            p.op("gpsimd", lambda e, d=wb4[:], s=src: e.dma_start(out=d, in_=s),
                 reads=[src], writes=[wb4[:]], dma=True)
            for m4 in range(4):
                for oc in range(2):
                    bank = C.ps8(m4 * 2 + oc)
                    lh = f1c[:, k * 512 + m4 * 128: k * 512 + m4 * 128 + 128]
                    rh = wb4[:, oc * 512:(oc + 1) * 512]
                    C.p.op("tensor", lambda e, o=bank, lh=lh, rh=rh, s=(k == 0):
                           e.matmul(out=o, lhsT=lh, rhs=rh, start=s, stop=False),
                           reads=[lh, rh], writes=[bank])
        for m4 in range(4):
            for oc in range(2):
                bank = C.ps8(m4 * 2 + oc)
                C.p.op("tensor", lambda e, o=bank, rh=bff2row[0:1, oc * 512:(oc + 1) * 512]:
                       e.matmul(out=o, lhsT=C.ones_b[0:1, 0:128], rhs=rh,
                                start=False, stop=True),
                       reads=[C.ones_b[:], bff2row[:]], writes=[bank])
        for m4 in range(4):
            mi = tc * 4 + m4
            tmp = tmpf[m4 % 2]
            for oc in range(2):
                bank = C.ps8(m4 * 2 + oc)
                dsl = tmp[:, oc * 512:(oc + 1) * 512]
                hsl = h_tok[:, mi * D + oc * 512: mi * D + oc * 512 + 512]
                p.op("vector", lambda e, ps=bank, d=dsl, h=hsl: e.tensor_tensor(
                    out=d, in0=ps, in1=h, op=ALU.add),
                    reads=[bank, hsl], writes=[dsl])
            hsl2 = h_tok[:, mi * D:(mi + 1) * D]
            ln_tile(C, phe, tmp[:], hsl2, sq2[m4 % 2][:], stats7, mi,
                    affine=(g2bc[:], b2bc[:]))
    phe.close()
    p.fence()
    if L1_DBG == "h2":
        _dump_flat(C, h_tok[:], x_dst, T * D)
        phH.close(); ph.close()
        return

    phg = Phase(C, "l1g")
    moe = phg.sb("moe", [128, NT * D], BF16)
    p.op("vector", lambda e: e.memset(moe[:], 0.0), writes=[moe[:]])
    disp = phg.sb("disp", [128, NT * NE], F32)
    gcol = phg.sb("gcol", [128, NT], F32)
    phm = Phase(C, "l1m")
    h2T = phm.sb("h2T", [128, ND * T], BF16)
    for mi in range(NT):
        for fb in range(ND):
            src_b = h_tok[:, mi * D + fb * 128: mi * D + (fb + 1) * 128]
            pbt = C.psb()
            pb = pbt[:, 0:128]
            C.p.op("tensor", lambda e, s=src_b, o=pb: e.transpose(
                out=o, in_=s, identity=C.identb[:]),
                reads=[src_b, C.identb[:]], writes=[pb])
            dst = h2T[:, fb * T + mi * 128: fb * T + (mi + 1) * 128]
            p.op("vector", lambda e, s=pb, d=dst: e.tensor_copy(out=d, in_=s),
                 reads=[pb], writes=[dst])
    wgb = phm.sb("wgb", [128, 8 * 8], BF16)
    wload(C, wgb, aps["L1_wg"], D, 8)
    prb = [phm.sb(f"prb{i}", [128, NE], F32) for i in range(2)]
    sfx3 = phm.sb("sfx3", [128, 6], F32)
    for mi in range(NT):
        pA = C.psf()
        mm_accum(C, pA[:, 0:NE],
                 [h2T[:, k * T + mi * 128: k * T + mi * 128 + 128] for k in range(8)],
                 [wgb[:, k * 8:(k + 1) * 8] for k in range(8)])
        pr = prb[mi % 2]
        nmx = sfx3[:, 0:1]
        ssum = sfx3[:, 1:2]
        rsx = sfx3[:, 2:3]
        p.op("vector", lambda e, ps=pA, o=nmx: e.reduce_max(out=o, in_=ps[:, 0:NE],
                                                            axis=AX.X, negate=True),
             reads=[pA[:, 0:NE]], writes=[nmx])
        p.op("scalar", lambda e, ps=pA, d=pr, b=nmx, a=ssum: e.activation(
            out=d[:], in_=ps[:, 0:NE], func=AF.Exp, bias=b, scale=1.0, accum_out=a),
             reads=[pA[:, 0:NE], nmx], writes=[pr[:], ssum])
        p.op("vector", lambda e, o=rsx, i_=ssum: e.reciprocal(out=o, in_=i_),
             reads=[ssum], writes=[rsx])
        p.op("vector", lambda e, d=pr, s=rsx: e.tensor_scalar_mul(out=d[:], in0=d[:],
                                                                  scalar1=s),
             reads=[pr[:], rsx], writes=[pr[:]])
        gsl = gcol[:, mi:mi + 1]
        p.op("vector", lambda e, s=pr, d=gsl: e.reduce_max(out=d, in_=s[:], axis=AX.X),
             reads=[pr[:]], writes=[gsl])
        dsl = disp[:, mi * NE:(mi + 1) * NE]
        p.op("vector", lambda e, s=pr, d=dsl, g=gsl: e.tensor_scalar(
            out=d, in0=s[:], scalar1=g, scalar2=None, op0=ALU.is_ge),
            reads=[pr[:], gsl], writes=[dsl])
    hid = phm.sb("hid", [128, 32 * 1024], BF16)
    we1b = [phm.sb(f"we1b{i}", [128, 8 * 512], BF16) for i in range(2)]
    we2b = [phm.sb(f"we2b{i}", [128, 512], BF16) for i in range(2)]
    be1c = phm.sb("be1c", [128, NE * 32], F32)
    p.op("sync", lambda e: e.dma_start(out=be1c[:], in_=aps["L1_be1c"][:]),
         writes=[be1c[:]], dma=True)
    be2r = phm.sb("be2r", [1, NE * D], BF16)
    p.op("gpsimd", lambda e: e.dma_start(out=be2r[:], in_=aps["L1_be2_row"][:]),
         reads=[aps["L1_be2_row"][:]], writes=[be2r[:]], dma=True)
    eoutf = [phm.sb(f"eoutf{i}", [128, 512], F32) for i in range(2)]
    for e_i in range(NE):
        for tc in range(2):
            t0 = tc * 1024
            for mg in range(8):
                wb5 = we1b[mg % 2]
                wload(C, wb5, aps["L1_we1"], D, 512, row0=e_i * D, col0=mg * 512)
                for m4 in range(4):
                    fch = mg * 4 + m4
                    for nj in range(2):
                        pA = C.psf()
                        mm_accum(C, pA[:, 0:512],
                                 [wb5[:, k * 512 + m4 * 128: k * 512 + m4 * 128 + 128]
                                  for k in range(8)],
                                 [h2T[:, k * T + t0 + nj * 512: k * T + t0 + (nj + 1) * 512]
                                  for k in range(8)])
                        dst = hid[:, fch * 1024 + nj * 512: fch * 1024 + (nj + 1) * 512]
                        bsl = be1c[:, e_i * 32 + fch: e_i * 32 + fch + 1]
                        p.op("scalar", lambda e, ps=pA, d=dst, b=bsl: e.activation(
                            out=d, in_=ps[:, 0:512], func=AF.Relu, bias=b),
                            reads=[pA[:, 0:512], be1c[:]], writes=[dst])
            for oh in range(2):
                for k in range(32):
                    wb6 = we2b[k % 2]
                    src = aps["L1_we2"][e_i * FF + k * 128: e_i * FF + (k + 1) * 128,
                                        oh * 512:(oh + 1) * 512]
                    p.op("gpsimd", lambda e, d=wb6[:], s=src: e.dma_start(out=d, in_=s),
                         reads=[src], writes=[wb6[:]], dma=True)
                    for m8 in range(8):
                        bank = C.ps8(m8)
                        lh = hid[:, k * 1024 + m8 * 128: k * 1024 + m8 * 128 + 128]
                        C.p.op("tensor", lambda e, o=bank, lh=lh, rh=wb6[:],
                               s=(k == 0): e.matmul(out=o, lhsT=lh, rhs=rh,
                                                    start=s, stop=False),
                               reads=[lh, wb6[:]], writes=[bank])
                for m8 in range(8):
                    bank = C.ps8(m8)
                    rh = be2r[0:1, e_i * D + oh * 512: e_i * D + (oh + 1) * 512]
                    C.p.op("tensor", lambda e, o=bank, rh=rh: e.matmul(
                        out=o, lhsT=C.ones_b[0:1, 0:128], rhs=rh, start=False, stop=True),
                        reads=[C.ones_b[:], be2r[:]], writes=[bank])
                for m8 in range(8):
                    mi = tc * 8 + m8
                    bank = C.ps8(m8)
                    ef = eoutf[m8 % 2]
                    p.op("vector", lambda e, ps=bank, d=ef: e.tensor_copy(out=d[:], in_=ps),
                         reads=[bank], writes=[ef[:]])
                    msl = moe[:, mi * D + oh * 512: mi * D + (oh + 1) * 512]
                    dcol = disp[:, mi * NE + e_i: mi * NE + e_i + 1]
                    p.op("vector", lambda e, s=ef, d=msl, dc=dcol: e.scalar_tensor_tensor(
                        out=d, in0=s[:], scalar=dc, in1=d, op0=ALU.mult, op1=ALU.add),
                        reads=[ef[:], disp[:], msl], writes=[msl])
    phm.close()
    p.fence()
    if L1_DBG == "moe":
        _dump_flat(C, moe[:], x_dst, T * D)
        phg.close(); phH.close(); ph.close()
        return

    phz = Phase(C, "l1z")
    g3bc = make_bcast(C, phz, "g3", aps["L1_ln3g_row"][:])
    b3bc = make_bcast(C, phz, "b3", aps["L1_ln3b_row"][:])
    woutb = phz.sb("woutb", [128, 8 * 1024], BF16)
    wload(C, woutb, aps["L1_wout"], D, 1024)
    h3t = [phz.sb(f"h3t{i}", [128, ND * 128], BF16) for i in range(2)]
    tmp3 = [phz.sb(f"tmp3{i}", [128, D], F32) for i in range(2)]
    h3f = [phz.sb(f"h3f{i}", [128, D], F32) for i in range(2)]
    sq3 = [phz.sb(f"sq3{i}", [128, D], F32) for i in range(2)]
    xupd = [phz.sb(f"xupd{i}", [128, D], F32) for i in range(2)]
    sld = [phz.sb(f"sld{i}", [128, D], BF16) for i in range(2)]
    stats8 = phz.sb("stats8", [128, 4 * NT], F32)
    for mi in range(NT):
        tmp = tmp3[mi % 2]
        msl = moe[:, mi * D:(mi + 1) * D]
        h2sl = h_tok[:, mi * D:(mi + 1) * D]
        gsl = gcol[:, mi:mi + 1]
        p.op("vector", lambda e, m=msl, h=h2sl, g=gsl, d=tmp: e.scalar_tensor_tensor(
            out=d[:], in0=m, scalar=g, in1=h, op0=ALU.mult, op1=ALU.add),
            reads=[msl, gcol[:], h2sl], writes=[tmp[:]])
        h3 = h3f[mi % 2]
        ln_tile(C, phz, tmp[:], h3[:], sq3[mi % 2][:], stats8, mi, affine=(g3bc[:], b3bc[:]))
        h3T = h3t[mi % 2]
        for g4 in range(2):
            transpose_batch(C, [h3[:, (g4 * 4 + i2) * 128:(g4 * 4 + i2 + 1) * 128]
                                for i2 in range(4)],
                            h3T[:, g4 * 512:(g4 + 1) * 512], F32)
        xs = xupd[mi % 2]
        src = x_src[mi * 128:(mi + 1) * 128, :]
        p.op("sync", lambda e, s=xs, r=src: e.dma_start(out=s[:], in_=r),
             reads=[src], writes=[xs[:]], dma=True)
        sl = sld[mi % 2]
        ssrc = s_scr[mi * 128:(mi + 1) * 128, :]
        p.op("sync", lambda e, s=sl, r=ssrc: e.dma_start(out=s[:], in_=r),
             reads=[ssrc], writes=[sl[:]], dma=True)
        for nj in range(2):
            pA = C.psf()
            mm_accum(C, pA[:, 0:512],
                     [h3T[:, k * 128:(k + 1) * 128] for k in range(8)],
                     [woutb[:, k * 1024 + nj * 512: k * 1024 + (nj + 1) * 512]
                      for k in range(8)])
            gsl2 = tmp[:, nj * 512:(nj + 1) * 512]
            p.op("scalar", lambda e, ps=pA, d=gsl2: e.activation(
                out=d, in_=ps[:, 0:512], func=AF.Silu),
                reads=[pA[:, 0:512]], writes=[gsl2])
        if L1_DBG == "gout":
            gdst = ctx2d[0:128, mi * D:(mi + 1) * D]
            p.op("gpsimd", lambda e, s=tmp, d=gdst: e.dma_start(out=d, in_=s[:]),
                 reads=[tmp[:]], writes=[gdst], dma=True)
        if L1_DBG == "sin":
            gdst = ctx2d[0:128, mi * D:(mi + 1) * D]
            p.op("gpsimd", lambda e, s=sl, d=gdst: e.dma_start(out=d, in_=s[:]),
                 reads=[sl[:]], writes=[gdst], dma=True)
        p.op("vector", lambda e, s=sl, g=tmp: e.tensor_tensor(out=g[:], in0=g[:], in1=s[:],
                                                              op=ALU.mult),
             reads=[sl[:], tmp[:]], writes=[tmp[:]])
        p.op("vector", lambda e, g=tmp, x=xs: e.tensor_tensor(out=x[:], in0=x[:], in1=g[:],
                                                              op=ALU.add),
             reads=[tmp[:], xs[:]], writes=[xs[:]])
        dst = x_dst[mi * 128:(mi + 1) * 128, :]
        p.op("sync", lambda e, s=xs, d=dst: e.dma_start(out=d, in_=s[:]),
             reads=[xs[:]], writes=[dst], dma=True)
    if L1_DBG in ("gout", "sin"):
        _dump_flat(C, aps["ctx_scr"][0:T * D], x_dst, T * D)
    phz.close()
    phg.close()
    phH.close()
    ph.close()


def aps_pitch_write_view(shd, c0):
    return shd[0:128 * (T + 1)].rearrange("(r c) -> r c", c=T + 1)[:, 1 + c0 * 512: 1 + T]


def quant_tile(C, src_f32, dst_bf16, W, tq):
    p = C.p
    nb = W // 16
    a = tq["a"][:, 0:W]
    q = tq["q"][:, 0:W]
    sg = tq["sg"][:, 0:W]
    bm = tq["bm"][:, 0:nb]
    iv = tq["iv"][:, 0:nb]
    p.op("scalar", lambda e, s=src_f32, o=a: e.activation(out=o, in_=s, func=AF.Abs),
         reads=[src_f32], writes=[a])
    cur = a
    width = W
    for step in range(4):
        half = width // 2
        nxt = tq["b1"][:, 0:half]
        cv = cur.rearrange("p (n two) -> p n two", two=2)
        p.op("vector", lambda e, i0=cv[:, :, 0:1].squeeze(), i1=cv[:, :, 1:2].squeeze(),
             o=nxt: e.tensor_tensor(out=o, in0=i0, in1=i1, op=ALU.max),
            reads=[tq["a"][:, 0:W], tq["b1"][:, 0:W // 2]], writes=[nxt])
        cur = nxt
        width = half
    p.op("vector", lambda e, i_=cur, o=bm: e.tensor_scalar(
        out=o, in0=i_, scalar1=1.0 / 6.0, scalar2=1e-12, op0=ALU.mult, op1=ALU.max),
        reads=[cur], writes=[bm])
    p.op("vector", lambda e, o=iv, i_=bm: e.reciprocal(out=o, in_=i_),
         reads=[bm], writes=[iv])
    ivb = iv.unsqueeze(2).broadcast_to((128, nb, 16))
    av = a.rearrange("p (n k) -> p n k", k=16)
    p.op("vector", lambda e, o=av, i0=av, i1=ivb: e.tensor_tensor(out=o, in0=i0, in1=i1,
                                                                  op=ALU.mult),
         reads=[a, iv], writes=[a])
    p.op("vector", lambda e, i_=a, o=q: e.tensor_scalar(
        out=o, in0=i_, scalar1=0.25, scalar2=0.5, op0=ALU.is_gt, op1=ALU.mult),
        reads=[a], writes=[q])
    for thr, inc in ((0.75, 0.5), (1.25, 0.5), (1.75, 0.5), (2.5, 1.0), (3.5, 1.0),
                     (5.0, 2.0)):
        p.op("vector", lambda e, i_=a, o=sg, t_=thr, ic=inc: e.tensor_scalar(
            out=o, in0=i_, scalar1=t_, scalar2=ic, op0=ALU.is_gt, op1=ALU.mult),
            reads=[a], writes=[sg])
        p.op("vector", lambda e, o=q, i1=sg: e.tensor_tensor(out=o, in0=o, in1=i1,
                                                             op=ALU.add),
             reads=[q, sg], writes=[q])
    p.op("vector", lambda e, s=src_f32, o=sg: e.tensor_scalar(
        out=o, in0=s, scalar1=0.0, scalar2=2.0, op0=ALU.is_gt, op1=ALU.mult),
        reads=[src_f32], writes=[sg])
    p.op("vector", lambda e, o=sg: e.tensor_scalar(
        out=o, in0=o, scalar1=-1.0, scalar2=None, op0=ALU.add),
        reads=[sg], writes=[sg])
    p.op("vector", lambda e, o=q, i1=sg: e.tensor_tensor(out=o, in0=o, in1=i1, op=ALU.mult),
         reads=[q, sg], writes=[q])
    qv = q.rearrange("p (n k) -> p n k", k=16)
    bmb = bm.unsqueeze(2).broadcast_to((128, nb, 16))
    dv = dst_bf16.rearrange("p (n k) -> p n k", k=16)
    p.op("vector", lambda e, o=dv, i0=qv, i1=bmb: e.tensor_tensor(out=o, in0=i0, in1=i1,
                                                                  op=ALU.mult),
         reads=[q, bm], writes=[dst_bf16])


def quant_wide(C, src_ap, dst_ap, W, tq):
    for c0 in range(0, W, 512):
        w = min(512, W - c0)
        quant_tile(C, src_ap[:, c0:c0 + w], dst_ap[:, c0:c0 + w], w, tq)


def build_layer2(C, x_src, x_dst):
    p = C.p
    aps = C.aps
    II = 2816
    ph = Phase(C, "l2")
    s_tok = ph.sb("s_tok", [128, NT * D], BF16)
    phx = Phase(C, "l2x")
    xnT = phx.sb("xnT", [128, ND * T], BF16)
    phpre = Phase(C, "l2pre")
    make_xn(C, x_src, phpre, xnT)
    phpre.close()

    pha = Phase(C, "l2a")
    qT = pha.sb("qT", [128, ND * T], BF16)
    k_tok = pha.sb("k_tok", [128, NT * D], BF16)
    v_tok = pha.sb("v_tok", [128, NT * D], BF16)
    wbuf = pha.sb("wbuf", [128, 8 * 512], BF16)
    bqc = pha.sb("bqc", [128, ND], F32)
    p.op("sync", lambda e: e.dma_start(out=bqc[:], in_=aps["L2_bqc"][:]),
         writes=[bqc[:]], dma=True)
    brow = pha.sb("brow", [1, 3 * D], BF16)
    for i, nm in enumerate(["L2_bk_row", "L2_bv_row", "L2_bo_row"]):
        p.op("gpsimd", lambda e, i=i, nm=nm: e.dma_start(
            out=brow[0:1, i * D:(i + 1) * D], in_=aps[nm][:]),
            reads=[aps[nm][:]], writes=[brow[0:1, i * D:(i + 1) * D]], dma=True)
    dec_c = pha.sb("dec_c", [128, NT], F32)
    eps_c = pha.sb("eps_c", [128, NT], F32)
    p.op("sync", lambda e: e.dma_start(out=dec_c[:], in_=aps["L2_dec"][:]),
         writes=[dec_c[:]], dma=True)
    p.op("sync", lambda e: e.dma_start(out=eps_c[:], in_=aps["L2_epsdec"][:]),
         writes=[eps_c[:]], dma=True)
    onec = pha.sb("onec", [128, 1], BF16)
    p.op("vector", lambda e: e.memset(onec[:], 1.0), writes=[onec[:]])
    for half in range(2):
        wload(C, wbuf, aps["L2_wqT"], D, 512, col0=half * 512)
        for m4 in range(4):
            m = half * 4 + m4
            for nj in range(4):
                pA = C.psf()
                mm_accum(C, pA[:, 0:512],
                         [wbuf[:, k * 512 + m4 * 128: k * 512 + m4 * 128 + 128]
                          for k in range(8)],
                         [xnT[:, k * T + nj * 512: k * T + (nj + 1) * 512]
                          for k in range(8)])
                dst = qT[:, m * T + nj * 512: m * T + (nj + 1) * 512]
                p.op("scalar", lambda e, ps=pA, d=dst, b=bqc[:, m:m + 1]: e.activation(
                    out=d, in_=ps[:, 0:512], func=AF.Relu, bias=b),
                    reads=[pA[:, 0:512], bqc[:]], writes=[dst])
    for wi, (wname, b0, isk) in enumerate([("L2_wkT", 0, True), ("L2_wvT", D, False)]):
        for nj in range(2):
            wload(C, wbuf, aps[wname], D, 512, col0=nj * 512)
            for mi in range(NT):
                pA = C.psf()
                lhs = [xnT[:, k * T + mi * 128: k * T + mi * 128 + 128] for k in range(8)]
                rhs = [wbuf[:, k * 512:(k + 1) * 512] for k in range(8)]
                lhs.append(C.ones_b[0:1, 0:128])
                rhs.append(brow[0:1, b0 + nj * 512: b0 + (nj + 1) * 512])
                mm_accum(C, pA[:, 0:512], lhs, rhs)
                dstt = (k_tok if isk else v_tok)[:, mi * D + nj * 512:
                                                 mi * D + (nj + 1) * 512]
                if isk:
                    p.op("vector", lambda e, ps=pA, d=dstt, dc=dec_c[:, mi:mi + 1]:
                         e.tensor_scalar(out=d, in0=ps[:, 0:512], scalar1=0.0, scalar2=dc,
                                         op0=ALU.max, op1=ALU.mult),
                         reads=[pA[:, 0:512], dec_c[:]], writes=[dstt])
                else:
                    p.op("vector", lambda e, ps=pA, d=dstt: e.tensor_copy(
                        out=d, in_=ps[:, 0:512]),
                        reads=[pA[:, 0:512]], writes=[dstt])
    bdr = pha.sb("bdr", [128, 8 * 130], BF16)
    dnt = pha.sb("dnt", [128, 4], F32)
    p.op("vector", lambda e: e.memset(bdr[:], 0.0), writes=[bdr[:]])
    for hg in range(8):
        pk = C.psf()
        for kb in range(NT):
            lh = k_tok[:, kb * D + hg * 128: kb * D + (hg + 1) * 128]
            rh = v_tok[:, kb * D + hg * 128: kb * D + (hg + 1) * 128]
            C.p.op("tensor", lambda e, o=pk[:, 0:128], lh=lh, rh=rh, s=(kb == 0),
                   t=(kb == NT - 1): e.matmul(out=o, lhsT=lh, rhs=rh, start=s, stop=t),
                   reads=[lh, rh], writes=[pk[:, 0:128]])
            C.p.op("tensor", lambda e, o=pk[:, 128:129], lh=lh, s=(kb == 0),
                   t=(kb == NT - 1): e.matmul(out=o, lhsT=lh, rhs=onec[:],
                                              start=s, stop=t),
                   reads=[lh, onec[:]], writes=[pk[:, 128:129]])
        for j2 in range(2):
            src1 = pk[j2 * 64:(j2 + 1) * 64, j2 * 64:(j2 + 1) * 64]
            dst1 = bdr[j2 * 64:(j2 + 1) * 64, hg * 130 + j2 * 65:hg * 130 + j2 * 65 + 64]
            p.op("vector", lambda e, s=src1, d=dst1: e.tensor_copy(out=d, in_=s),
                 reads=[pk[:, 0:129]], writes=[dst1])
            src2 = pk[j2 * 64:(j2 + 1) * 64, 128:129]
            dst2 = bdr[j2 * 64:(j2 + 1) * 64,
                       hg * 130 + j2 * 65 + 64:hg * 130 + j2 * 65 + 65]
            p.op("vector", lambda e, s=src2, d=dst2: e.tensor_copy(out=d, in_=s),
                 reads=[pk[:, 0:129]], writes=[dst2])
    ao_tok = k_tok
    for hg in range(8):
        for mi in range(NT):
            po = C.psf()
            lh = qT[:, hg * T + mi * 128: hg * T + (mi + 1) * 128]
            C.p.op("tensor", lambda e, o=po[:, 0:130], lh=lh,
                   rh=bdr[:, hg * 130:(hg + 1) * 130]: e.matmul(
                out=o, lhsT=lh, rhs=rh, start=True, stop=True),
                reads=[lh, bdr[:]], writes=[po[:, 0:130]])
            for j2 in range(2):
                h = hg * 2 + j2
                dn = dnt[:, j2:j2 + 1]
                p.op("vector", lambda e, ps=po, o=dn, ec=eps_c[:, mi:mi + 1], j2=j2:
                     e.tensor_tensor(out=o, in0=ps[:, j2 * 65 + 64:j2 * 65 + 65],
                                     in1=ec, op=ALU.add),
                     reads=[po[:, 0:130], eps_c[:]], writes=[dn])
                dn2 = dnt[:, 2 + j2:3 + j2]
                p.op("vector", lambda e, o=dn2, i_=dn: e.reciprocal(out=o, in_=i_),
                     reads=[dn], writes=[dn2])
                dst = ao_tok[:, mi * D + h * 64: mi * D + (h + 1) * 64]
                p.op("vector", lambda e, ps=po, d=dst, r=dn2, j2=j2: e.tensor_scalar(
                    out=d, in0=ps[:, j2 * 65:j2 * 65 + 64], scalar1=r, scalar2=None,
                    op0=ALU.mult),
                    reads=[po[:, 0:130], dn2], writes=[dst])
    aoT = qT
    for mi in range(NT):
        for fb in range(ND):
            src_b = ao_tok[:, mi * D + fb * 128: mi * D + (fb + 1) * 128]
            dst_b = aoT[:, fb * T + mi * 128: fb * T + (mi + 1) * 128]
            transpose128(C, pha, src_b, dst_b, BF16)
    for nj in range(2):
        wload(C, wbuf, aps["L2_woT"], D, 512, col0=nj * 512)
        for mi in range(NT):
            pA = C.psf()
            lhs = [aoT[:, k * T + mi * 128: k * T + mi * 128 + 128] for k in range(8)]
            rhs = [wbuf[:, k * 512:(k + 1) * 512] for k in range(8)]
            lhs.append(C.ones_b[0:1, 0:128])
            rhs.append(brow[0:1, 2 * D + nj * 512: 2 * D + (nj + 1) * 512])
            mm_accum(C, pA[:, 0:512], lhs, rhs)
            dst = s_tok[:, mi * D + nj * 512: mi * D + (nj + 1) * 512]
            p.op("vector", lambda e, ps=pA, d=dst: e.tensor_copy(out=d, in_=ps[:, 0:512]),
                 reads=[pA[:, 0:512]], writes=[dst])
    pha.close()
    phx.close()
    g_tok = ph.sb("g_tok", [128, NT * D], BF16)

    wg2d = aps["l2wg_q"].rearrange("(r c) -> r c", c=II)
    wu2d = aps["l2wu_q"].rearrange("(r c) -> r c", c=II)
    wd2d = aps["l2wd_q"].rearrange("(r c) -> r c", c=1024)
    phw = Phase(C, "l2w")
    tq = {"a": phw.sb("qa", [128, 512], F32), "b1": phw.sb("qb1", [128, 256], F32),
          "q": phw.sb("qq", [128, 512], F32), "sg": phw.sb("qsg", [128, 512], F32),
          "bm": phw.sb("qbm", [128, 32], F32),
          "iv": phw.sb("qiv", [128, 32], F32)}
    phw1 = Phase(C, "l2w1")
    wst = [phw1.sb(f"wst{i}", [128, II], F32) for i in range(2)]
    wqb = [phw1.sb(f"wqb{i}", [128, II], BF16) for i in range(2)]
    wtt = [phw1.sb(f"wtt{i}", [128, 22 * 128], BF16) for i in range(2)]
    for wi, (wname, scr2d, rows, cols) in enumerate(
            [("L2_wgate", wg2d, II, 1024), ("L2_wup", wu2d, II, 1024),
             ("L2_wdown", wd2d, 1024, II)]):
        nrc = rows // 128
        nfb = cols // 128
        for rg in range(nrc):
            st = wst[rg % 2]
            qb = wqb[rg % 2]
            tt = wtt[rg % 2]
            src = aps[wname][rg * 128:(rg + 1) * 128, :]
            p.op("sync", lambda e, d=st[:, 0:cols], s=src: e.dma_start(out=d, in_=s),
                 reads=[src], writes=[st[:, 0:cols]], dma=True)
            quant_wide(C, st[:, 0:cols], qb[:, 0:cols], cols, tq)
            for fb in range(nfb):
                pbt = C.psb()
                pb = pbt[:, 0:128]
                C.p.op("tensor", lambda e, s=qb[:, fb * 128:(fb + 1) * 128], o=pb:
                       e.transpose(out=o, in_=s, identity=C.identb[:]),
                       reads=[qb[:, 0:cols], C.identb[:]], writes=[pb])
                p.op("vector", lambda e, s=pb, d=tt[:, fb * 128:(fb + 1) * 128]:
                     e.tensor_copy(out=d, in_=s),
                     reads=[pb], writes=[tt[:, fb * 128:(fb + 1) * 128]])
            dstv = scr2d[0:cols, rg * 128:(rg + 1) * 128].rearrange(
                "(fb p) j -> p fb j", p=128)
            ttv = tt[:, 0:nfb * 128].rearrange("p (fb j) -> p fb j", j=128)
            p.op("sync", lambda e, d=dstv, s=ttv: e.dma_start(out=d, in_=s),
                 reads=[tt[:, 0:nfb * 128]], writes=[dstv], dma=True)
    phw1.close()
    xq_tok = phw.sb("xq_tok", [128, NT * D], BF16)
    xqT = phw.sb("xqT", [128, ND * T], BF16)
    phxq = Phase(C, "l2xq")
    xst = [phxq.sb(f"xst{i}", [128, D], F32) for i in range(2)]
    sqt = [phxq.sb(f"sqt{i}", [128, D], F32) for i in range(2)]
    xnt = [phxq.sb(f"xnt{i}", [128, D], F32) for i in range(2)]
    stats9 = phxq.sb("stats9", [128, 4 * NT], F32)
    for mi in range(NT):
        xs = xst[mi % 2]
        src = x_src[mi * 128:(mi + 1) * 128, :]
        p.op("sync", lambda e, s=xs, r=src: e.dma_start(out=s[:], in_=r),
             reads=[src], writes=[xs[:]], dma=True)
        xn = xnt[mi % 2]
        ln_tile(C, phxq, xs[:], xn[:], sqt[mi % 2][:], stats9, mi)
        dstq = xq_tok[:, mi * D:(mi + 1) * D]
        quant_wide(C, xn[:], dstq, D, tq)
        for fb in range(ND):
            src_b = xq_tok[:, mi * D + fb * 128: mi * D + (fb + 1) * 128]
            dst_b = xqT[:, fb * T + mi * 128: fb * T + (mi + 1) * 128]
            transpose128(C, phxq, src_b, dst_b, BF16)
    phxq.close()
    hch = phw.sb("hch", [128, 4 * II], BF16)
    wsb = [phw.sb(f"wsb{i}", [128, 8 * 512], BF16) for i in range(2)]
    wdb = [phw.sb(f"wdb{i}", [128, 1024], BF16) for i in range(2)]
    t1b = [phw.sb(f"t1b{i}", [128, 512], F32) for i in range(2)]
    t2b = [phw.sb(f"t2b{i}", [128, 512], F32) for i in range(2)]
    hqt = phw.sb("hqt", [128, 4 * 128], BF16)
    NOG = [(0, 512), (512, 1024), (1024, 1536), (1536, 2048), (2048, 2560), (2560, 2816)]
    for tc in range(4):
        t0 = tc * 512
        for (o0, o1) in NOG:
            ow = o1 - o0
            for gi, scr in enumerate((wg2d, wu2d)):
                wsl = wsb[gi]
                srcw = scr[:, o0:o1].rearrange("(r p) c -> p r c", p=128)
                wv = wsl[:, 0:8 * ow].rearrange("p (r c) -> p r c", c=ow)
                p.op("gpsimd", lambda e, d=wv, s=srcw: e.dma_start(out=d, in_=s),
                     reads=[scr[:, o0:o1]], writes=[wsl[:, 0:8 * ow]], dma=True)
            for m4 in range(4):
                pG = C.psf()
                pU = C.psf()
                for gi, psx in ((0, pG), (1, pU)):
                    mm_accum(C, psx[:, 0:ow],
                             [xqT[:, k * T + t0 + m4 * 128: k * T + t0 + m4 * 128 + 128]
                              for k in range(8)],
                             [wsb[gi][:, k * ow:(k + 1) * ow] for k in range(8)])
                t1 = t1b[m4 % 2]
                t2 = t2b[m4 % 2]
                p.op("scalar", lambda e, ps=pG, o=t1[:, 0:ow], ow=ow: e.activation(
                    out=o, in_=ps[:, 0:ow], func=AF.Silu),
                    reads=[pG[:, 0:512]], writes=[t1[:, 0:512]])
                p.op("vector", lambda e, ps=pU, o=t2[:, 0:ow], ow=ow: e.tensor_copy(
                    out=o, in_=ps[:, 0:ow]),
                    reads=[pU[:, 0:512]], writes=[t2[:, 0:512]])
                dsth = hch[:, m4 * II + o0: m4 * II + o1]
                p.op("vector", lambda e, d=dsth, i0=t1[:, 0:ow], i1=t2[:, 0:ow]:
                     e.tensor_tensor(out=d, in0=i0, in1=i1, op=ALU.mult),
                     reads=[t1[:, 0:512], t2[:, 0:512]], writes=[dsth])
        for m4 in range(4):
            hsl = hch[:, m4 * II:(m4 + 1) * II]
            quant_wide(C, hsl, hsl, II, tq)
        nkc = II // 128
        for kc in range(nkc):
            wdl = wdb[kc % 2]
            srcw = wd2d[kc * 128:(kc + 1) * 128, :]
            p.op("gpsimd", lambda e, d=wdl[:], s=srcw: e.dma_start(out=d, in_=s),
                 reads=[srcw], writes=[wdl[:]], dma=True)
            for m4 in range(4):
                src_b = hch[:, m4 * II + kc * 128: m4 * II + (kc + 1) * 128]
                dst_b = hqt[:, m4 * 128:(m4 + 1) * 128]
                transpose128(C, phw, src_b, dst_b, BF16)
            for m4 in range(4):
                lh = hqt[:, m4 * 128:(m4 + 1) * 128]
                for oc in range(2):
                    bank = C.ps8(m4 * 2 + oc)
                    rh = wdl[:, oc * 512:(oc + 1) * 512]
                    C.p.op("tensor", lambda e, o=bank, lh=lh, rh=rh, s=(kc == 0),
                           t=(kc == nkc - 1): e.matmul(out=o, lhsT=lh, rhs=rh,
                                                       start=s, stop=t),
                           reads=[lh, rh], writes=[bank])
        for m4 in range(4):
            mi = tc * 4 + m4
            for oc in range(2):
                bank = C.ps8(m4 * 2 + oc)
                dst = g_tok[:, mi * D + oc * 512: mi * D + (oc + 1) * 512]
                p.op("scalar", lambda e, ps=bank, d=dst: e.activation(
                    out=d, in_=ps, func=AF.Silu),
                    reads=[bank], writes=[dst])
    phw.close()

    phu = Phase(C, "l2u")
    xst2 = [phu.sb(f"xu{i}", [128, D], F32) for i in range(2)]
    tmps = [phu.sb(f"tm{i}", [128, D], F32) for i in range(2)]
    for mi in range(NT):
        xs = xst2[mi % 2]
        tmp = tmps[mi % 2]
        src = x_src[mi * 128:(mi + 1) * 128, :]
        p.op("sync", lambda e, s=xs, r=src: e.dma_start(out=s[:], in_=r),
             reads=[src], writes=[xs[:]], dma=True)
        ssl = s_tok[:, mi * D:(mi + 1) * D]
        gsl = g_tok[:, mi * D:(mi + 1) * D]
        p.op("vector", lambda e, d=tmp, i0=ssl, i1=gsl: e.tensor_tensor(
            out=d[:], in0=i0, in1=i1, op=ALU.mult),
            reads=[ssl, gsl], writes=[tmp[:]])
        p.op("vector", lambda e, x=xs, t_=tmp: e.tensor_tensor(
            out=x[:], in0=x[:], in1=t_[:], op=ALU.add),
            reads=[tmp[:], xs[:]], writes=[xs[:]])
        dst = x_dst[mi * 128:(mi + 1) * 128, :]
        p.op("sync", lambda e, s=xs, d=dst: e.dma_start(out=d, in_=s[:]),
             reads=[xs[:]], writes=[dst], dma=True)
    phu.close()
    ph.close()


def build_layer3(C, x_src, x_dst):
    p = C.p
    aps = C.aps
    NHA, DA = 32, 32
    NEx, DH = 16, 64
    ph = Phase(C, "l3")
    s_tok = ph.sb("s_tok", [128, NT * D], BF16)
    phx = Phase(C, "l3x")
    xnT = phx.sb("xnT", [128, ND * T], BF16)
    phpre = Phase(C, "l3pre")
    make_xn(C, x_src, phpre, xnT)
    phpre.close()

    pha = Phase(C, "l3a")
    qT = pha.sb("qT", [128, ND * T], BF16)
    k_tok = pha.sb("k_tok", [128, NT * D], BF16)
    v_tok = pha.sb("v_tok", [128, NT * D], BF16)
    wbuf = pha.sb("wbuf", [128, 8 * 512], BF16)
    onec = pha.sb("onec", [128, 1], BF16)
    p.op("vector", lambda e: e.memset(onec[:], 1.0), writes=[onec[:]])
    pbrow = pha.sb("pbrow", [1, D], BF16)
    p.op("gpsimd", lambda e: e.dma_start(out=pbrow[:], in_=aps["L3_pb_row"][:]),
         reads=[aps["L3_pb_row"][:]], writes=[pbrow[:]], dma=True)
    for half in range(2):
        wload(C, wbuf, aps["L3_qkvT"], D, 512, col0=half * 512)
        for m4 in range(4):
            m = half * 4 + m4
            for nj in range(4):
                pA = C.psf()
                mm_accum(C, pA[:, 0:512],
                         [wbuf[:, k * 512 + m4 * 128: k * 512 + m4 * 128 + 128]
                          for k in range(8)],
                         [xnT[:, k * T + nj * 512: k * T + (nj + 1) * 512]
                          for k in range(8)])
                dst = qT[:, m * T + nj * 512: m * T + (nj + 1) * 512]
                p.op("scalar", lambda e, ps=pA, d=dst: e.activation(
                    out=d, in_=ps[:, 0:512], func=AF.Relu),
                    reads=[pA[:, 0:512]], writes=[dst])
    for wi, (c0, dstt) in enumerate([(1024, k_tok), (2048, v_tok)]):
        for nj in range(2):
            wload(C, wbuf, aps["L3_qkvT"], D, 512, col0=c0 + nj * 512)
            for mi in range(NT):
                pA = C.psf()
                mm_accum(C, pA[:, 0:512],
                         [xnT[:, k * T + mi * 128: k * T + mi * 128 + 128]
                          for k in range(8)],
                         [wbuf[:, k * 512:(k + 1) * 512] for k in range(8)])
                dst = dstt[:, mi * D + nj * 512: mi * D + (nj + 1) * 512]
                p.op("scalar", lambda e, ps=pA, d=dst: e.activation(
                    out=d, in_=ps[:, 0:512], func=AF.Relu),
                    reads=[pA[:, 0:512]], writes=[dst])
    bdr = pha.sb("bdr", [128, 8 * 132], BF16)
    dnt = pha.sb("dnt", [128, 4], F32)
    p.op("vector", lambda e: e.memset(bdr[:], 0.0), writes=[bdr[:]])
    for hg in range(8):
        pk = C.psf()
        for kb in range(NT):
            lh = k_tok[:, kb * D + hg * 128: kb * D + (hg + 1) * 128]
            rh = v_tok[:, kb * D + hg * 128: kb * D + (hg + 1) * 128]
            C.p.op("tensor", lambda e, o=pk[:, 0:128], lh=lh, rh=rh, s=(kb == 0),
                   t=(kb == NT - 1): e.matmul(out=o, lhsT=lh, rhs=rh, start=s, stop=t),
                   reads=[lh, rh], writes=[pk[:, 0:128]])
            C.p.op("tensor", lambda e, o=pk[:, 128:129], lh=lh, s=(kb == 0),
                   t=(kb == NT - 1): e.matmul(out=o, lhsT=lh, rhs=onec[:],
                                              start=s, stop=t),
                   reads=[lh, onec[:]], writes=[pk[:, 128:129]])
        for j4 in range(4):
            src1 = pk[j4 * 32:(j4 + 1) * 32, j4 * 32:(j4 + 1) * 32]
            dst1 = bdr[j4 * 32:(j4 + 1) * 32, hg * 132 + j4 * 33:hg * 132 + j4 * 33 + 32]
            p.op("vector", lambda e, s=src1, d=dst1: e.tensor_copy(out=d, in_=s),
                 reads=[pk[:, 0:129]], writes=[dst1])
            src2 = pk[j4 * 32:(j4 + 1) * 32, 128:129]
            dst2 = bdr[j4 * 32:(j4 + 1) * 32,
                       hg * 132 + j4 * 33 + 32:hg * 132 + j4 * 33 + 33]
            p.op("vector", lambda e, s=src2, d=dst2: e.tensor_copy(out=d, in_=s),
                 reads=[pk[:, 0:129]], writes=[dst2])
    ao_tok = k_tok
    for hg in range(8):
        for mi in range(NT):
            po = C.psf()
            lh = qT[:, hg * T + mi * 128: hg * T + (mi + 1) * 128]
            C.p.op("tensor", lambda e, o=po[:, 0:132], lh=lh,
                   rh=bdr[:, hg * 132:(hg + 1) * 132]: e.matmul(
                out=o, lhsT=lh, rhs=rh, start=True, stop=True),
                reads=[lh, bdr[:]], writes=[po[:, 0:132]])
            for j4 in range(4):
                h = hg * 4 + j4
                dn = dnt[:, 0:1]
                p.op("vector", lambda e, ps=po, o=dn, j4=j4: e.tensor_scalar(
                    out=o, in0=ps[:, j4 * 33 + 32:j4 * 33 + 33], scalar1=1e-15,
                    scalar2=None, op0=ALU.add),
                    reads=[po[:, 0:132]], writes=[dn])
                dn2 = dnt[:, 1:2]
                p.op("vector", lambda e, o=dn2, i_=dn: e.reciprocal(out=o, in_=i_),
                     reads=[dn], writes=[dn2])
                dst = ao_tok[:, mi * D + h * 32: mi * D + (h + 1) * 32]
                p.op("vector", lambda e, ps=po, d=dst, r=dn2, j4=j4: e.tensor_scalar(
                    out=d, in0=ps[:, j4 * 33:j4 * 33 + 32], scalar1=r, scalar2=None,
                    op0=ALU.mult),
                    reads=[po[:, 0:132], dn2], writes=[dst])
    aoT = qT
    for mi in range(NT):
        for fb in range(ND):
            src_b = ao_tok[:, mi * D + fb * 128: mi * D + (fb + 1) * 128]
            dst_b = aoT[:, fb * T + mi * 128: fb * T + (mi + 1) * 128]
            transpose128(C, pha, src_b, dst_b, BF16)
    for nj in range(2):
        wload(C, wbuf, aps["L3_projT"], D, 512, col0=nj * 512)
        for mi in range(NT):
            pA = C.psf()
            lhs = [aoT[:, k * T + mi * 128: k * T + mi * 128 + 128] for k in range(8)]
            rhs = [wbuf[:, k * 512:(k + 1) * 512] for k in range(8)]
            lhs.append(C.ones_b[0:1, 0:128])
            rhs.append(pbrow[0:1, nj * 512:(nj + 1) * 512])
            mm_accum(C, pA[:, 0:512], lhs, rhs)
            dst = s_tok[:, mi * D + nj * 512: mi * D + (nj + 1) * 512]
            p.op("vector", lambda e, ps=pA, d=dst: e.tensor_copy(out=d, in_=ps[:, 0:512]),
                 reads=[pA[:, 0:512]], writes=[dst])
    pha.close()

    phg = Phase(C, "l3g")
    rw = phg.sb("rw", [128, NT * NEx], F32)
    simb = phg.sb("simb", [128, 8 * 16], BF16)
    wload(C, simb, aps["L3_sim"], D, 16)
    simsq = phg.sb("simsq", [128, 8 * 16], F32)
    p.op("scalar", lambda e: e.activation(out=simsq[:], in_=simb[:], func=AF.Square),
         reads=[simb[:]], writes=[simsq[:]])
    onef = phg.sb("onef", [128, 1], F32)
    p.op("vector", lambda e: e.memset(onef[:], 1.0), writes=[onef[:]])
    pc = C.psf()
    for kb in range(8):
        C.p.op("tensor", lambda e, o=pc[0:1, 0:16], rh=simsq[:, kb * 16:(kb + 1) * 16],
               s=(kb == 0), t=(kb == 7): e.matmul(out=o, lhsT=onef[:], rhs=rh,
                                                  start=s, stop=t),
               reads=[onef[:], simsq[:]], writes=[pc[:, 0:16]])
    csum = phg.sb("csum", [1, 16], F32)
    p.op("vector", lambda e: e.tensor_copy(out=csum[:], in_=pc[0:1, 0:16]),
         reads=[pc[:, 0:16]], writes=[csum[:]])
    p.op("vector", lambda e: e.tensor_scalar(out=csum[:], in0=csum[:], scalar1=1e-24,
                                             scalar2=None, op0=ALU.max),
         reads=[csum[:]], writes=[csum[:]])
    p.op("scalar", lambda e: e.activation(out=csum[:], in_=csum[:], func=AF.Ln),
         reads=[csum[:]], writes=[csum[:]])
    p.op("scalar", lambda e: e.activation(out=csum[:], in_=csum[:], func=AF.Exp,
                                          scale=-0.5),
         reads=[csum[:]], writes=[csum[:]])
    grow = phg.sb("grow", [1, 16], F32)
    p.op("sync", lambda e: e.dma_start(out=grow[:], in_=aps["L3_gates_row"][:]),
         writes=[grow[:]], dma=True)
    p.op("scalar", lambda e: e.activation(out=grow[:], in_=grow[:], func=AF.Sigmoid),
         reads=[grow[:]], writes=[grow[:]])
    icn_bc = phg.sb("icn_bc", [128, 16], F32)
    sg_bc = phg.sb("sg_bc", [128, 16], F32)
    for rowap, dstbc in ((csum, icn_bc), (grow, sg_bc)):
        pB = C.psf()
        C.p.op("tensor", lambda e, o=pB[:, 0:16], rh=rowap[0:1, 0:16]: e.matmul(
            out=o, lhsT=C.ones_f[:], rhs=rh, start=True, stop=True),
            reads=[C.ones_f[:], rowap[:]], writes=[pB[:, 0:16]])
        p.op("vector", lambda e, d=dstbc, ps=pB: e.tensor_copy(out=d[:], in_=ps[:, 0:16]),
             reads=[pB[:, 0:16]], writes=[dstbc[:]])
    phs = Phase(C, "l3s")
    xst = [phs.sb(f"xst{i}", [128, D], F32) for i in range(2)]
    sqt = [phs.sb(f"sqt{i}", [128, D], F32) for i in range(2)]
    stats10 = phs.sb("stats10", [128, 4 * NT], F32)
    gtmp = phs.sb("gtmp", [128, 10 * NEx], F32)
    for mi in range(NT):
        xs = xst[mi % 2]
        src = x_src[mi * 128:(mi + 1) * 128, :]
        p.op("sync", lambda e, s=xs, r=src: e.dma_start(out=s[:], in_=r),
             reads=[src], writes=[xs[:]], dma=True)
        mu, rstd = ln_stats(C, phs, xs[:], sqt[mi % 2][:], stats10, mi, D)
        svar = stats10[:, 4 * mi + 1:4 * mi + 2]
        nrm = gtmp[:, 9 * NEx:9 * NEx + 1]
        p.op("vector", lambda e, o=nrm, i_=svar: e.tensor_scalar(
            out=o, in0=i_, scalar1=-float(D), scalar2=None, op0=ALU.mult),
            reads=[svar], writes=[nrm])
        p.op("vector", lambda e, o=nrm, i1=rstd: e.scalar_tensor_tensor(
            out=o, in0=i1, scalar=rstd, in1=o, op0=ALU.mult, op1=ALU.mult),
            reads=[rstd, nrm], writes=[nrm])
        p.op("vector", lambda e, o=nrm: e.tensor_scalar(
            out=o, in0=o, scalar1=1e-24, scalar2=None, op0=ALU.max),
            reads=[nrm], writes=[nrm])
        p.op("scalar", lambda e, o=nrm: e.activation(out=o, in_=o, func=AF.Ln),
             reads=[nrm], writes=[nrm])
        p.op("scalar", lambda e, o=nrm: e.activation(out=o, in_=o, func=AF.Exp, scale=-0.5),
             reads=[nrm], writes=[nrm])
        pL = C.psf()
        mm_accum(C, pL[:, 0:NEx],
                 [xnT[:, k * T + mi * 128: k * T + mi * 128 + 128] for k in range(8)],
                 [simb[:, k * 16:(k + 1) * 16] for k in range(8)])
        lg = gtmp[:, 0:NEx]
        p.op("vector", lambda e, ps=pL, d=lg, s=nrm: e.tensor_scalar(
            out=d, in0=ps[:, 0:NEx], scalar1=s, scalar2=None, op0=ALU.mult),
            reads=[pL[:, 0:NEx], nrm], writes=[lg])
        p.op("vector", lambda e, d=lg: e.tensor_tensor(out=d, in0=d, in1=icn_bc[:],
                                                       op=ALU.mult),
             reads=[lg, icn_bc[:]], writes=[lg])
        p.op("vector", lambda e, d=lg: e.tensor_tensor(out=d, in0=d, in1=sg_bc[:],
                                                       op=ALU.subtract),
             reads=[lg, sg_bc[:]], writes=[lg])
        m1 = gtmp[:, NEx:NEx + 1]
        p.op("vector", lambda e, d=m1, i_=lg: e.reduce_max(out=d, in_=i_, axis=AX.X),
             reads=[lg], writes=[m1])
        eq = gtmp[:, 2 * NEx:3 * NEx]
        p.op("vector", lambda e, d=eq, i_=lg, s=m1: e.tensor_scalar(
            out=d, in0=i_, scalar1=s, scalar2=-1e30, op0=ALU.is_ge, op1=ALU.mult),
            reads=[lg, m1], writes=[eq])
        l2i = gtmp[:, 3 * NEx:4 * NEx]
        p.op("vector", lambda e, d=l2i, i0=lg, i1=eq: e.tensor_tensor(
            out=d, in0=i0, in1=i1, op=ALU.add),
            reads=[lg, eq], writes=[l2i])
        m2 = gtmp[:, NEx + 1:NEx + 2]
        p.op("vector", lambda e, d=m2, i_=l2i: e.reduce_max(out=d, in_=i_, axis=AX.X),
             reads=[l2i], writes=[m2])
        fb2 = gtmp[:, 4 * NEx:5 * NEx]
        p.op("vector", lambda e, d=fb2, i_=lg, s=m2: e.tensor_scalar(
            out=d, in0=i_, scalar1=s, scalar2=None, op0=ALU.is_ge),
            reads=[lg, m2], writes=[fb2])
        am = gtmp[:, 5 * NEx:6 * NEx]
        p.op("vector", lambda e, d=am, i_=lg: e.tensor_scalar(
            out=d, in0=i_, scalar1=0.0, scalar2=None, op0=ALU.is_gt),
            reads=[lg], writes=[am])
        ia = gtmp[:, NEx + 2:NEx + 3]
        p.op("vector", lambda e, d=ia, i_=m1: e.tensor_scalar(
            out=d, in0=i_, scalar1=0.0, scalar2=None, op0=ALU.is_le),
            reads=[m1], writes=[ia])
        p.op("vector", lambda e, d=fb2, s=ia: e.tensor_scalar(
            out=d, in0=d, scalar1=s, scalar2=None, op0=ALU.mult),
            reads=[fb2, ia], writes=[fb2])
        p.op("vector", lambda e, d=am, i1=fb2: e.tensor_tensor(out=d, in0=d, in1=i1,
                                                               op=ALU.max),
             reads=[am, fb2], writes=[am])
        gated = gtmp[:, 6 * NEx:7 * NEx]
        p.op("vector", lambda e, d=gated, i_=lg: e.tensor_scalar(
            out=d, in0=i_, scalar1=0.0, scalar2=None, op0=ALU.max),
            reads=[lg], writes=[gated])
        p.op("vector", lambda e, d=gated, i1=am: e.tensor_tensor(out=d, in0=d, in1=i1,
                                                                 op=ALU.mult),
             reads=[gated, am], writes=[gated])
        pen = gtmp[:, 7 * NEx:8 * NEx]
        p.op("vector", lambda e, d=pen, i_=am: e.tensor_scalar(
            out=d, in0=i_, scalar1=1e30, scalar2=-1e30, op0=ALU.mult, op1=ALU.add),
            reads=[am], writes=[pen])
        p.op("vector", lambda e, d=gated, i1=pen: e.tensor_tensor(out=d, in0=d, in1=i1,
                                                                  op=ALU.add),
             reads=[gated, pen], writes=[gated])
        nmx = gtmp[:, NEx + 3:NEx + 4]
        ssum = gtmp[:, NEx + 4:NEx + 5]
        rsx = gtmp[:, NEx + 5:NEx + 6]
        p.op("vector", lambda e, d=nmx, i_=gated: e.reduce_max(out=d, in_=i_, axis=AX.X,
                                                               negate=True),
             reads=[gated], writes=[nmx])
        dstw = rw[:, mi * NEx:(mi + 1) * NEx]
        p.op("scalar", lambda e, d=dstw, i_=gated, b=nmx, a=ssum: e.activation(
            out=d, in_=i_, func=AF.Exp, bias=b, scale=1.0, accum_out=a),
            reads=[gated, nmx], writes=[dstw, ssum])
        p.op("vector", lambda e, o=rsx, i_=ssum: e.reciprocal(out=o, in_=i_),
             reads=[ssum], writes=[rsx])
        p.op("vector", lambda e, d=dstw, s=rsx: e.tensor_scalar_mul(out=d, in0=d,
                                                                    scalar1=s),
             reads=[dstw, rsx], writes=[dstw])
    phs.close()
    qm = phg.sb("qm", [128, NT * 3 * DH], BF16)
    apb = [phg.sb(f"apb{i}", [128, 8 * 512], BF16) for i in range(2)]
    mix = phg.sb("mix", [128, 512], F32)
    for ch in range(6):
        wb = apb[ch % 2]
        wload(C, wb, aps["L3_allP"], D, 512, col0=ch * 512)
        comp = ch // 2
        eh0 = (ch % 2) * 8
        for mi in range(NT):
            pA = C.psf()
            mm_accum(C, pA[:, 0:512],
                     [xnT[:, k * T + mi * 128: k * T + mi * 128 + 128] for k in range(8)],
                     [wb[:, k * 512:(k + 1) * 512] for k in range(8)])
            dst = qm[:, mi * 192 + comp * DH: mi * 192 + (comp + 1) * DH]
            for e8 in range(8):
                ei = eh0 + e8
                rcol = rw[:, mi * NEx + ei: mi * NEx + ei + 1]
                if e8 == 0:
                    p.op("vector", lambda e, ps=pA, d=mix[:, 0:DH], r=rcol, e8=e8:
                         e.tensor_scalar(out=d, in0=ps[:, e8 * DH:(e8 + 1) * DH],
                                         scalar1=r, scalar2=None, op0=ALU.mult),
                         reads=[pA[:, 0:512], rw[:]], writes=[mix[:, 0:DH]])
                else:
                    p.op("vector", lambda e, ps=pA, d=mix[:, 0:DH], r=rcol, e8=e8:
                         e.scalar_tensor_tensor(out=d, in0=ps[:, e8 * DH:(e8 + 1) * DH],
                                                scalar=r, in1=d, op0=ALU.mult,
                                                op1=ALU.add),
                         reads=[pA[:, 0:512], rw[:], mix[:, 0:DH]], writes=[mix[:, 0:DH]])
            if ch % 2 == 0:
                p.op("vector", lambda e, d=dst, s=mix[:, 0:DH]: e.tensor_copy(
                    out=d, in_=s), reads=[mix[:, 0:DH]], writes=[dst])
            else:
                p.op("vector", lambda e, d=dst, s=mix[:, 0:DH]: e.tensor_tensor(
                    out=d, in0=d, in1=s, op=ALU.add),
                    reads=[mix[:, 0:DH], dst], writes=[dst])
    qT3 = phg.sb("qT3", [64, T], BF16)
    kT3 = phg.sb("kT3", [64, T], BF16)
    sc3 = phg.sb("sc3", [128, T], F32)
    pb3 = phg.sb("pb3", [128, T], BF16)
    sfx4 = phg.sb("sfx4", [128, 6], F32)
    tri = phg.sb("tri", [128, 128], F32)
    p.op("sync", lambda e: e.dma_start(out=tri[:], in_=aps["diagbias"][:]),
         writes=[tri[:]], dma=True)
    ctx3 = phg.sb("ctx3", [128, NT * DH], BF16)
    for mi in range(NT):
        for which, dstT in ((0, qT3), (1, kT3)):
            src_b = qm[:, mi * 192 + which * DH: mi * 192 + (which + 1) * DH]
            pbt = C.psb()
            pb = pbt[0:64, 0:128]
            C.p.op("tensor", lambda e, s=src_b, o=pb: e.transpose(
                out=o, in_=s, identity=C.identb[:]),
                reads=[src_b, C.identb[:]], writes=[pbt[:, 0:128]])
            p.op("vector", lambda e, s=pb, d=dstT[0:64, mi * 128:(mi + 1) * 128]:
                 e.tensor_copy(out=d, in_=s),
                 reads=[pbt[:, 0:128]], writes=[dstT[0:64, mi * 128:(mi + 1) * 128]])
    p.op("vector", lambda e: e.memset(pb3[:], 0.0), writes=[pb3[:]])
    for mi in range(NT):
        W = (mi + 1) * 128
        ncj = (mi + 4) // 4
        for cj in range(ncj):
            wv = min(512, W - cj * 512)
            pA = C.psf()
            lh = qT3[0:64, mi * 128:(mi + 1) * 128]
            rh = kT3[0:64, cj * 512:cj * 512 + wv]
            C.p.op("tensor", lambda e, o=pA[:, 0:wv], lh=lh, rh=rh: e.matmul(
                out=o, lhsT=lh, rhs=rh, start=True, stop=True),
                reads=[lh, rh], writes=[pA[:, 0:wv]])
            dsc = sc3[:, cj * 512:cj * 512 + wv]
            p.op("scalar", lambda e, ps=pA, d=dsc, wv=wv: e.copy(out=d, in_=ps[:, 0:wv]),
                 reads=[pA[:, 0:wv]], writes=[dsc])
        dg = sc3[:, mi * 128:W]
        p.op("vector", lambda e, d=dg: e.tensor_tensor(out=d, in0=d, in1=tri[:],
                                                       op=ALU.add),
             reads=[dg, tri[:]], writes=[dg])
        scw = sc3[:, 0:W]
        pbw = pb3[:, 0:W]
        nmx = sfx4[:, 0:1]
        ssum = sfx4[:, 1:2]
        rsx = sfx4[:, 2:3]
        p.op("vector", lambda e, i_=scw, o=nmx: e.reduce_max(out=o, in_=i_, axis=AX.X,
                                                             negate=True),
             reads=[scw], writes=[nmx])
        p.op("vector", lambda e, o=nmx: e.tensor_scalar_mul(out=o, in0=o, scalar1=0.125),
             reads=[nmx], writes=[nmx])
        p.op("scalar", lambda e, o=pbw, i_=scw, b=nmx, a=ssum: e.activation(
            out=o, in_=i_, func=AF.Exp, bias=b, scale=0.125, accum_out=a),
            reads=[scw, nmx], writes=[pbw, ssum])
        p.op("vector", lambda e, o=rsx, i_=ssum: e.reciprocal(out=o, in_=i_),
             reads=[ssum], writes=[rsx])
        p.op("vector", lambda e, o=pbw, s=rsx: e.tensor_scalar_mul(out=o, in0=o, scalar1=s),
             reads=[pbw, rsx], writes=[pbw])
        for g4 in range((mi + 4) // 4):
            nb = min(4, mi + 1 - g4 * 4)
            transpose_batch(C, [pb3[:, (g4 * 4 + i2) * 128:(g4 * 4 + i2 + 1) * 128]
                                for i2 in range(nb)],
                            pb3[:, g4 * 512:g4 * 512 + nb * 128], BF16)
        pcx = C.psf()
        for kb in range(mi + 1):
            lh = qm[:, kb * 192 + 2 * DH: kb * 192 + 3 * DH]
            rh = pb3[:, kb * 128:(kb + 1) * 128]
            C.p.op("tensor", lambda e, o=pcx[0:DH, 0:128], lh=lh, rh=rh,
                   s=(kb == 0), t=(kb == mi): e.matmul(out=o, lhsT=lh, rhs=rh,
                                                       start=s, stop=t),
                   reads=[lh, rh], writes=[pcx[:, 0:128]])
        cst = ctx3[:, mi * DH:(mi + 1) * DH]
        ctmp = phg.sb(f"ctmp{mi}", [64, 128], BF16) if False else None
        ct64 = sc3[:, 0:64]
        ctb = pb3[:, T - 128:T]
        p.op("vector", lambda e, d=ctb[0:64, 0:128], s=pcx[0:DH, 0:128]: e.tensor_copy(
            out=d, in_=s), reads=[pcx[:, 0:128]], writes=[ctb])
        pbt2 = C.psb()
        C.p.op("tensor", lambda e, s=ctb[0:64, 0:128], o=pbt2[0:128, 0:64]: e.transpose(
            out=o, in_=s, identity=C.identb[0:64, 0:64]),
            reads=[ctb, C.identb[:]], writes=[pbt2[:, 0:128]])
        p.op("vector", lambda e, d=cst, s=pbt2[0:128, 0:64]: e.tensor_copy(out=d, in_=s),
             reads=[pbt2[:, 0:128]], writes=[cst])
    phz = Phase(C, "l3z")
    wmix = [phz.sb(f"wmix{i}", [128, NEx * DH], BF16) for i in range(2)]
    wmt = [phz.sb(f"wmt{i}", [128, 8 * 128], BF16) for i in range(2)]
    opb = [phz.sb(f"opb{i}", [128, 8 * 512], BF16) for i in range(2)]
    xst2 = [phz.sb(f"xu{i}", [128, D], F32) for i in range(2)]
    tmps = [phz.sb(f"tm{i}", [128, D], F32) for i in range(2)]
    for half in range(2):
        wload(C, opb[half], aps["L3_OP"], D, 512, col0=half * 512)
    for mi in range(NT):
        wm = wmix[mi % 2]
        for ei in range(NEx):
            rcol = rw[:, mi * NEx + ei: mi * NEx + ei + 1]
            dst = wm[:, ei * DH:(ei + 1) * DH]
            p.op("vector", lambda e, d=dst, s=ctx3[:, mi * DH:(mi + 1) * DH], r=rcol:
                 e.tensor_scalar(out=d, in0=s, scalar1=r, scalar2=None, op0=ALU.mult),
                 reads=[ctx3[:, mi * DH:(mi + 1) * DH], rw[:]], writes=[dst])
        wt = wmt[mi % 2]
        for fb in range(ND):
            src_b = wm[:, fb * 128:(fb + 1) * 128]
            dst_b = wt[:, fb * 128:(fb + 1) * 128]
            transpose128(C, phz, src_b, dst_b, BF16)
        xs = xst2[mi % 2]
        tmp = tmps[mi % 2]
        src = x_src[mi * 128:(mi + 1) * 128, :]
        p.op("sync", lambda e, s=xs, r=src: e.dma_start(out=s[:], in_=r),
             reads=[src], writes=[xs[:]], dma=True)
        for nj in range(2):
            pA = C.psf()
            mm_accum(C, pA[:, 0:512],
                     [wt[:, k * 128:(k + 1) * 128] for k in range(8)],
                     [opb[nj][:, k * 512:(k + 1) * 512] for k in range(8)])
            dsl = tmp[:, nj * 512:(nj + 1) * 512]
            p.op("scalar", lambda e, ps=pA, d=dsl: e.activation(
                out=d, in_=ps[:, 0:512], func=AF.Silu),
                reads=[pA[:, 0:512]], writes=[dsl])
        ssl = s_tok[:, mi * D:(mi + 1) * D]
        p.op("vector", lambda e, g=tmp, s=ssl: e.tensor_tensor(out=g[:], in0=g[:], in1=s,
                                                               op=ALU.mult),
             reads=[tmp[:], ssl], writes=[tmp[:]])
        p.op("vector", lambda e, x=xs, t_=tmp: e.tensor_tensor(out=x[:], in0=x[:],
                                                               in1=t_[:], op=ALU.add),
             reads=[tmp[:], xs[:]], writes=[xs[:]])
        dst = x_dst[mi * 128:(mi + 1) * 128, :]
        p.op("sync", lambda e, s=xs, d=dst: e.dma_start(out=d, in_=s[:]),
             reads=[xs[:]], writes=[dst], dma=True)
    phz.close()
    phg.close()
    phx.close()
    ph.close()


def _col128(v):
    return np.ascontiguousarray(v.reshape(-1, 128).T)


def prep_host(inputs):
    f = np.float32
    A = {}
    A["identb"] = np.eye(128, dtype=np.float32)
    A["identf"] = np.eye(128, dtype=np.float32)
    import ml_dtypes
    A["identb"] = np.eye(128, dtype=ml_dtypes.bfloat16)
    A["L0_lnw_col"] = _col128(np.asarray(inputs["L0_attn_ln_w"], f))
    A["L0_lnb_col"] = _col128(np.asarray(inputs["L0_attn_ln_b"], f))
    A["L0_wqkvT"] = np.ascontiguousarray(np.asarray(inputs["L0_attn_w_qkv"], f).T)
    A["L0_woutT"] = np.ascontiguousarray(np.asarray(inputs["L0_attn_w_out"], f).T)
    A["L0_w1T"] = np.ascontiguousarray(np.asarray(inputs["L0_ffn_w1_w"], f).T)
    A["L0_w2T"] = np.ascontiguousarray(np.asarray(inputs["L0_ffn_w2_w"], f).T)
    A["L0_h1T"] = np.ascontiguousarray(np.asarray(inputs["L0_ffn_h1_w"], f).T)
    A["L0_h2T"] = np.ascontiguousarray(np.asarray(inputs["L0_ffn_h2_w"], f).T)
    A["L0_w3T"] = np.ascontiguousarray(np.asarray(inputs["L0_ffn_w3_w"], f).T)
    A["L0_s1c"] = _col128(np.asarray(inputs["L0_ffn_w1_s"], f))
    A["L0_s2c"] = _col128(np.asarray(inputs["L0_ffn_w2_s"], f))
    A["L0_sh1c"] = _col128(np.asarray(inputs["L0_ffn_h1_s"], f))
    A["L0_sh2c"] = _col128(np.asarray(inputs["L0_ffn_h2_s"], f))
    A["L0_s3c"] = _col128(np.asarray(inputs["L0_ffn_w3_s"], f))
    dh = 64
    inv_freq = 1.0 / (10000.0 ** (np.arange(0, dh, 2, dtype=np.float64) / dh))
    tpos = np.arange(T, dtype=np.float64)
    fr = tpos[:, None] * inv_freq[None, :]
    fr2 = np.concatenate([fr, fr], axis=1)
    A["L0_cos64"] = np.cos(fr2).astype(f)
    sinm = np.sin(fr2).astype(f)
    sinm[:, 0:32] *= -1.0
    A["L0_sinm64"] = np.ascontiguousarray(sinm)
    di = np.arange(128)[:, None]
    dj = np.arange(160)[None, :]
    mg = ((dj > di) & (dj <= di + 32))
    m0 = (mg & (dj >= 32))
    A["L0_maskg"] = np.where(mg, 0.0, -1e30).astype(f)
    A["L0_mask0"] = np.where(m0, 0.0, -1e30).astype(f)
    dj2 = np.arange(128)[None, :]
    A["diagbias"] = np.where(dj2 <= di, 0.0, -1e30).astype(f)
    A["L1_wqT"] = np.ascontiguousarray(np.asarray(inputs["L1_attn_Wq"], f).T)
    A["L1_wkT"] = np.ascontiguousarray(np.asarray(inputs["L1_attn_Wk"], f).T)
    A["L1_wvT"] = np.ascontiguousarray(np.asarray(inputs["L1_attn_Wv"], f).T)
    A["L1_bqc"] = _col128(np.asarray(inputs["L1_attn_bq"], f))
    A["L1_bkc"] = _col128(np.asarray(inputs["L1_attn_bk"], f))
    A["L1_bv_row"] = np.asarray(inputs["L1_attn_bv"], f).reshape(1, D)
    A["L1_wqkv"] = np.ascontiguousarray(np.asarray(inputs["L1_ffn_Wqkv"], f))
    A["L1_wr"] = np.ascontiguousarray(np.asarray(inputs["L1_ffn_Wr"], f))
    pos = np.arange(T - 1, -1, -1, dtype=np.float64)
    invf = 1.0 / (10000.0 ** (np.arange(0, D, 2, dtype=np.float64) / D))
    sino = pos[:, None] * invf[None, :]
    pe = np.concatenate([np.sin(sino), np.cos(sino)], axis=1).astype(f)
    A["L1_peT"] = np.ascontiguousarray(pe.T)
    A["L1_uc"] = _col128(np.asarray(inputs["L1_ffn_u_bias"], f).reshape(-1))
    A["L1_vc"] = _col128(np.asarray(inputs["L1_ffn_v_bias"], f).reshape(-1))
    A["L1_wo"] = np.ascontiguousarray(np.asarray(inputs["L1_ffn_Wo"], f))
    for nm, src_ in [("ln1g", "L1_ffn_ln1_g"), ("ln1b", "L1_ffn_ln1_b"),
                     ("ln2g", "L1_ffn_ln2_g"), ("ln2b", "L1_ffn_ln2_b"),
                     ("ln3g", "L1_ffn_ln3_g"), ("ln3b", "L1_ffn_ln3_b")]:
        A[f"L1_{nm}_row"] = np.asarray(inputs[src_], f).reshape(1, D)
    A["L1_wff1"] = np.ascontiguousarray(np.asarray(inputs["L1_ffn_Wff1"], f))
    A["L1_bff1c"] = _col128(np.asarray(inputs["L1_ffn_bff1"], f))
    A["L1_wff2"] = np.ascontiguousarray(np.asarray(inputs["L1_ffn_Wff2"], f))
    A["L1_bff2_row"] = np.asarray(inputs["L1_ffn_bff2"], f).reshape(1, D)
    A["L1_wg"] = np.ascontiguousarray(np.asarray(inputs["L1_ffn_Wg"], f))
    A["L1_we1"] = np.ascontiguousarray(np.asarray(inputs["L1_ffn_We1"], f).reshape(8 * 1024, 4096))
    A["L1_be1c"] = np.concatenate(
        [_col128(np.asarray(inputs["L1_ffn_be1"], f)[e]) for e in range(8)], axis=1)
    A["L1_we2"] = np.ascontiguousarray(np.asarray(inputs["L1_ffn_We2"], f).reshape(8 * 4096, 1024))
    A["L1_be2_row"] = np.asarray(inputs["L1_ffn_be2"], f).reshape(1, 8 * 1024)
    A["L1_wout"] = np.ascontiguousarray(np.asarray(inputs["L1_ffn_Wout"], f))
    A["L2_wqT"] = np.ascontiguousarray(np.asarray(inputs["L2_attn_W_q"], f).T)
    A["L2_wkT"] = np.ascontiguousarray(np.asarray(inputs["L2_attn_W_k"], f).T)
    A["L2_wvT"] = np.ascontiguousarray(np.asarray(inputs["L2_attn_W_v"], f).T)
    A["L2_woT"] = np.ascontiguousarray(np.asarray(inputs["L2_attn_W_o"], f).T)
    A["L2_bqc"] = _col128(np.asarray(inputs["L2_attn_b_q"], f))
    A["L2_bk_row"] = np.asarray(inputs["L2_attn_b_k"], f).reshape(1, D)
    A["L2_bv_row"] = np.asarray(inputs["L2_attn_b_v"], f).reshape(1, D)
    A["L2_bo_row"] = np.asarray(inputs["L2_attn_b_o"], f).reshape(1, D)
    dec = np.exp(-np.arange(T, dtype=np.float64) / T)
    A["L2_dec"] = _col128(dec.astype(f))
    A["L2_epsdec"] = _col128((1e-6 / dec).astype(f))
    A["L2_wgate"] = np.ascontiguousarray(np.asarray(inputs["L2_ffn_w_gate"], f))
    A["L2_wup"] = np.ascontiguousarray(np.asarray(inputs["L2_ffn_w_up"], f))
    A["L2_wdown"] = np.ascontiguousarray(np.asarray(inputs["L2_ffn_w_down"], f))
    A["L3_qkvT"] = np.ascontiguousarray(np.asarray(inputs["L3_attn_qkv_weight"], f).T)
    A["L3_projT"] = np.ascontiguousarray(np.asarray(inputs["L3_attn_proj_weight"], f).T)
    A["L3_pb_row"] = np.asarray(inputs["L3_attn_proj_bias"], f).reshape(1, D)
    A["L3_sim"] = np.ascontiguousarray(np.asarray(inputs["L3_ffn_sim_matrix"], f))
    A["L3_gates_row"] = np.asarray(inputs["L3_ffn_gates"], f).reshape(1, 16)
    qp = np.asarray(inputs["L3_ffn_q_proj"], f).transpose(1, 0, 2).reshape(1024, 1024)
    kp = np.asarray(inputs["L3_ffn_k_proj"], f).transpose(1, 0, 2).reshape(1024, 1024)
    vp = np.asarray(inputs["L3_ffn_v_proj"], f).transpose(1, 0, 2).reshape(1024, 1024)
    A["L3_allP"] = np.ascontiguousarray(np.concatenate([qp, kp, vp], axis=1))
    A["L3_OP"] = np.ascontiguousarray(np.asarray(inputs["L3_ffn_o_proj"], f).reshape(1024, 1024))
    return A


def build_program(layers=(0, 1, 2, 3)):
    nc = bass.Bass("TRN2", target_bir_lowering=False, debug=False)
    aps = {}
    aps["x_in"] = nc.dram_tensor("x_in", [T, D], F32, kind="ExternalInput").ap()
    aps["x_out"] = nc.dram_tensor("x_out", [T, D], F32, kind="ExternalOutput").ap()
    host = _HOST_SPEC
    for name, (shape, dt) in host.items():
        if _need(name, layers):
            aps[name] = nc.dram_tensor(name, list(shape), dt, kind="ExternalInput").ap()
    aps["scr_a"] = nc.dram_tensor("scr_a", [4096 * T], BF16, kind="Internal").ap()
    aps["scr_b"] = nc.dram_tensor("scr_b", [4096 * T], BF16, kind="Internal").ap()
    aps["xbuf"] = nc.dram_tensor("xbuf", [T, D], F32, kind="Internal").ap()
    aps["s_scr"] = nc.dram_tensor("s_scr", [T * D], BF16, kind="Internal").ap()
    aps["scr_sh0"] = nc.dram_tensor("scr_sh0", [129 * T + 128], BF16, kind="Internal").ap()
    aps["scr_sh1"] = nc.dram_tensor("scr_sh1", [129 * T + 128], BF16, kind="Internal").ap()
    aps["ctx_scr"] = nc.dram_tensor("ctx_scr", [128 * ND * T], BF16, kind="Internal").ap()
    aps["l2wg_q"] = nc.dram_tensor("l2wg_q", [1024 * 2816], BF16, kind="Internal").ap()
    aps["l2wu_q"] = nc.dram_tensor("l2wu_q", [1024 * 2816], BF16, kind="Internal").ap()
    aps["l2wd_q"] = nc.dram_tensor("l2wd_q", [2816 * 1024], BF16, kind="Internal").ap()

    with ExitStack() as stack:
        p = Prog(nc, stack)
        block_cm = nc.Block()
        C = Ctx(nc, p, stack, aps)
        builders = {0: build_layer0, 1: build_layer1, 2: build_layer2,
                    3: build_layer3}
        cur = aps["x_in"]
        for i, l in enumerate(layers):
            dst = aps["x_out"] if i == len(layers) - 1 else aps["xbuf"]
            builders[l](C, cur, dst)
            cur = aps["xbuf"]
        p.raw_wait("sync", [p.barrier_token(e) for e in ENGS])
        with block_cm as block:
            p.replay(block)
    return nc


_HOST_SPEC = {
    "identb": ((128, 128), BF16),
    "identf": ((128, 128), F32),
    "L0_lnw_col": ((128, 8), F32), "L0_lnb_col": ((128, 8), F32),
    "L0_wqkvT": ((1024, 3072), F32), "L0_woutT": ((1024, 1024), F32),
    "L0_w1T": ((1024, 4096), F32), "L0_w2T": ((1024, 4096), F32),
    "L0_h1T": ((4096, 4096), F32), "L0_h2T": ((4096, 4096), F32),
    "L0_w3T": ((4096, 1024), F32),
    "L0_s1c": ((128, 32), F32), "L0_s2c": ((128, 32), F32),
    "L0_sh1c": ((128, 32), F32), "L0_sh2c": ((128, 32), F32),
    "L0_s3c": ((128, 8), F32),
    "L0_cos64": ((T, 64), F32), "L0_sinm64": ((T, 64), F32),
    "L0_maskg": ((128, 160), F32), "L0_mask0": ((128, 160), F32),
    "diagbias": ((128, 128), F32),
    "L1_wqT": ((1024, 1024), F32), "L1_wkT": ((1024, 1024), F32),
    "L1_wvT": ((1024, 1024), F32),
    "L1_bqc": ((128, 8), F32), "L1_bkc": ((128, 8), F32), "L1_bv_row": ((1, 1024), F32),
    "L1_wqkv": ((1024, 3072), F32), "L1_wr": ((1024, 1024), F32),
    "L1_peT": ((1024, 2048), F32),
    "L1_uc": ((128, 8), F32), "L1_vc": ((128, 8), F32),
    "L1_wo": ((1024, 1024), F32),
    "L1_ln1g_row": ((1, 1024), F32), "L1_ln1b_row": ((1, 1024), F32),
    "L1_ln2g_row": ((1, 1024), F32), "L1_ln2b_row": ((1, 1024), F32),
    "L1_ln3g_row": ((1, 1024), F32), "L1_ln3b_row": ((1, 1024), F32),
    "L1_wff1": ((1024, 4096), F32), "L1_bff1c": ((128, 32), F32),
    "L1_wff2": ((4096, 1024), F32), "L1_bff2_row": ((1, 1024), F32),
    "L1_wg": ((1024, 8), F32),
    "L1_we1": ((8192, 4096), F32), "L1_be1c": ((128, 256), F32),
    "L1_we2": ((32768, 1024), F32), "L1_be2_row": ((1, 8192), F32),
    "L1_wout": ((1024, 1024), F32),
    "L2_wqT": ((1024, 1024), F32), "L2_wkT": ((1024, 1024), F32),
    "L2_wvT": ((1024, 1024), F32), "L2_woT": ((1024, 1024), F32),
    "L2_bqc": ((128, 8), F32),
    "L2_bk_row": ((1, 1024), F32), "L2_bv_row": ((1, 1024), F32),
    "L2_bo_row": ((1, 1024), F32),
    "L2_dec": ((128, 16), F32), "L2_epsdec": ((128, 16), F32),
    "L2_wgate": ((2816, 1024), F32), "L2_wup": ((2816, 1024), F32),
    "L2_wdown": ((1024, 2816), F32),
    "L3_qkvT": ((1024, 3072), F32), "L3_projT": ((1024, 1024), F32),
    "L3_pb_row": ((1, 1024), F32),
    "L3_sim": ((1024, 16), F32), "L3_gates_row": ((1, 16), F32),
    "L3_allP": ((1024, 3072), F32), "L3_OP": ((1024, 1024), F32),
}


def _need(name, layers):
    if name.startswith("ident") or name == "diagbias":
        return True
    for l in layers:
        if name.startswith(f"L{l}_"):
            return True
    return False


def kernel(**inputs):
    A = prep_host(inputs)
    x = np.asarray(inputs["L0_attn_x"], np.float32)
    nc = build_program((0, 1, 2, 3))
    in_maps = []
    for c in range(NCORES):
        m = dict(A)
        m["x_in"] = np.ascontiguousarray(x[c])
        in_maps.append(m)
    res = run_bass_kernel_spmd(nc, in_maps, core_ids=list(range(NCORES)))
    return np.stack([np.asarray(res.results[c]["x_out"]) for c in range(NCORES)], axis=0)


# version 02
# speedup vs baseline: 1.0202x; 1.0202x over previous
import numpy as np
import concourse.bass as bass
import concourse.mybir as mybir
from concourse.bass_utils import run_bass_kernel_spmd
from contextlib import ExitStack

F32 = mybir.dt.float32
BF16 = mybir.dt.bfloat16
U8 = mybir.dt.uint8
AF = mybir.ActivationFunctionType
ALU = mybir.AluOpType
AX = mybir.AxisListType

T = 2048
D = 1024
NT = T // 128
ND = D // 128
NCORES = 8
EPS = 1e-5

ENGS = ("sync", "gpsimd", "vector", "scalar", "tensor")
_EIDX = {e: i for i, e in enumerate(ENGS)}


def _ap_box(ap):
    t = ap.tensor
    esz = np.dtype(mybir.dt.np(ap.dtype)).itemsize
    lo = 0
    hi = 0
    for step, num in ap.ap:
        d = step * (num - 1)
        if d >= 0:
            hi += d
        else:
            lo += d
    off = ap.offset
    return (t.name, (off + lo) * esz, (off + hi) * esz + esz)


class Prog:

    def __init__(self, nc, stack):
        self.nc = nc
        self.stack = stack
        self.items = []
        self.sems = []
        self.counts = []
        self.is_dma_sem = []
        self.dma_snap = []
        for e in ENGS:
            self.sems.append(stack.enter_context(nc.semaphore(f"sem_{e}")))
            self.counts.append(0)
            self.is_dma_sem.append(False)
            self.dma_snap.append(None)
        self.dma_sid = {}
        self.sem_pool = {"sw": [], "hw": []}
        self.reg = {}
        self.clocks = {e: {} for e in ENGS}
        self.snaps = {}
        self.fence_toks = {}
        self.n_waits = 0
        self.n_ops = 0

    def _dma_sid_for(self, name, kind):
        sid = self.dma_sid.get((name, kind))
        if sid is not None:
            return sid
        if self.sem_pool[kind]:
            sid = self.sem_pool[kind].pop()
        else:
            sid = len(self.sems)
            self.sems.append(self.stack.enter_context(
                self.nc.semaphore(f"dsem{sid}")))
            self.counts.append(0)
            self.is_dma_sem.append(True)
            self.dma_snap.append({})
        self.dma_sid[(name, kind)] = sid
        return sid

    def release_dma_sems(self, names):
        for n in names:
            for kind in ("sw", "hw"):
                sid = self.dma_sid.pop((n, kind), None)
                if sid is not None:
                    self.sem_pool[kind].append(sid)

    def fence(self):
        for sid in range(len(self.sems)):
            c = self.counts[sid]
            if c > 0:
                self.fence_toks[sid] = max(self.fence_toks.get(sid, 0), c)

    def barrier(self):
        toks = [(sid, c) for sid, c in enumerate(self.counts) if c > 0]
        sems = self.sems
        for e in ENGS:
            clk = self.clocks[e]
            mywaits = [(sid, c) for sid, c in toks if clk.get(sid, 0) < c]

            def g(engine, mywaits=mywaits):
                for sid, v in mywaits:
                    engine.wait_ge(sems[sid], v)
            self.items.append((e, g))
            for sid, c in mywaits:
                clk[sid] = c
        self.fence()

    def _collect_deps(self, eng, reads, writes):
        need = {}
        my = _EIDX[eng]

        def consider(sid, cnt):
            if sid == my == _EIDX["tensor"]:
                return
            if self.is_dma_sem[sid]:
                cnt = self.counts[sid]
            if cnt > need.get(sid, 0):
                need[sid] = cnt

        for sid, c in self.fence_toks.items():
            consider(sid, c)
        for ap in reads:
            name, lo, hi = _ap_box(ap)
            for ent in self.reg.get(name, ()):
                if ent[4] and ent[0] < hi and lo < ent[1]:
                    consider(ent[2], ent[3])
        for ap in writes:
            name, lo, hi = _ap_box(ap)
            for ent in self.reg.get(name, ()):
                if ent[0] < hi and lo < ent[1]:
                    consider(ent[2], ent[3])
        clk = self.clocks[eng]
        waits = []
        for sid, cnt in need.items():
            if cnt > clk.get(sid, 0):
                waits.append((sid, cnt))
        return waits

    def _register(self, sid, cnt, reads, writes):
        for ap in writes:
            name, lo, hi = _ap_box(ap)
            lst = self.reg.setdefault(name, [])
            lst[:] = [e for e in lst if not (lo <= e[0] and e[1] <= hi)]
            lst.append([lo, hi, sid, cnt, True])
            if len(lst) > 256:
                self._compact(name)
        for ap in reads:
            name, lo, hi = _ap_box(ap)
            lst = self.reg.setdefault(name, [])
            lst.append([lo, hi, sid, cnt, False])
            if len(lst) > 256:
                self._compact(name)

    def _compact(self, name):
        lst = self.reg[name]
        out = []
        for is_w in (True, False):
            sub = [e for e in lst if e[4] == is_w]
            if not sub:
                continue
            lo = min(e[0] for e in sub)
            hi = max(e[1] for e in sub)
            best = {}
            for e in sub:
                if e[3] > best.get(e[2], 0):
                    best[e[2]] = e[3]
            out.extend([lo, hi, sid, c, is_w] for sid, c in best.items())
        self.reg[name] = out

    def op(self, eng, fn, reads=(), writes=(), extra_wait=(), dma=False):
        reads = [r for r in reads if r is not None]
        writes = [w for w in writes if w is not None]
        waits = self._collect_deps(eng, reads, writes)
        my = _EIDX[eng]
        clk = self.clocks[eng]
        for en, cv in extra_wait or ():
            sid = _EIDX[en] if isinstance(en, str) else en
            if cv > clk.get(sid, 0) and not (sid == my == _EIDX["tensor"]):
                waits.append((sid, cv))
        wm = {}
        for sid, cv in waits:
            if cv > wm.get(sid, 0):
                wm[sid] = cv
        waits = sorted(wm.items())
        for sid, cv in waits:
            if cv > clk.get(sid, 0):
                clk[sid] = cv
            snap = self.dma_snap[sid] if self.is_dma_sem[sid] else \
                self.snaps.get((sid, cv))
            if snap is not None:
                for k, v in snap.items():
                    if v > clk.get(k, 0):
                        clk[k] = v
        if dma:
            dest = writes[0].tensor.name
            kind = "hw" if eng in ("sync", "scalar") else "sw"
            sid_inc = self._dma_sid_for(dest, kind)
            d = 16
        else:
            sid_inc = my
            d = 1
        self.counts[sid_inc] += d
        cnt = self.counts[sid_inc]
        snap_now = dict(clk)
        snap_now[sid_inc] = cnt
        if not dma:
            snap_now[my] = cnt
        if self.is_dma_sem[sid_inc]:
            acc = self.dma_snap[sid_inc]
            for k, v in snap_now.items():
                if v > acc.get(k, 0):
                    acc[k] = v
        else:
            self.snaps[(sid_inc, cnt)] = snap_now
        self._register(sid_inc, cnt, reads, writes)
        sems = self.sems
        self.n_waits += len(waits)
        self.n_ops += 1

        def g(engine, waits=waits, fn=fn, sid_inc=sid_inc, d=d):
            for sid, v in waits:
                engine.wait_ge(sems[sid], v)
            inst = fn(engine)
            inst.then_inc(sems[sid_inc], d)
        self.items.append((eng, g))
        return (sid_inc, cnt)

    def raw_wait(self, eng, tokens=None):
        sems = self.sems
        if tokens is None:
            toks = [(sid, c) for sid, c in enumerate(self.counts) if c > 0]
        else:
            toks = [t for t in tokens if t[1] > 0]

        def g(engine, toks=toks):
            for sid, v in toks:
                engine.wait_ge(sems[sid], v)
        self.items.append((eng, g))

    def barrier_token(self, eng):
        sid = _EIDX[eng]
        return (sid, self.counts[sid])

    def replay(self, block):
        items = self.items

        def runner(name):
            def run(e):
                for n, f in items:
                    if n == name:
                        f(e)
            return run
        block.sync(runner("sync"))
        block.gpsimd(runner("gpsimd"))
        block.vector(runner("vector"))
        block.scalar(runner("scalar"))
        block.tensor(runner("tensor"))


class Ctx:

    def __init__(self, nc, p, stack, aps):
        self.nc = nc
        self.p = p
        self.stack = stack
        self.aps = aps
        self.ps_f = []
        self.ps_b = []
        for i in range(6):
            self.ps_f.append(stack.enter_context(nc.psum_tensor(f"psf{i}", [128, 512], F32)))
        for i in range(2):
            self.ps_b.append(stack.enter_context(nc.psum_tensor(f"psb{i}", [128, 1024], BF16)))
        self._psf_i = 0
        self._psb_i = 0
        self.identb = stack.enter_context(nc.sbuf_tensor("c_identb", [128, 128], BF16))
        self.identf = stack.enter_context(nc.sbuf_tensor("c_identf", [128, 128], F32))
        self.ones_b = stack.enter_context(nc.sbuf_tensor("ones_b", [1, 128], BF16))
        self.neg_s = stack.enter_context(nc.sbuf_tensor("neg_s", [128, 1], F32))
        self.eps_s = stack.enter_context(nc.sbuf_tensor("eps_s", [128, 1], F32))
        p.op("sync", lambda e: e.dma_start(out=self.identb[:], in_=aps["identb"][:]),
             writes=[self.identb[:]], dma=True)
        p.op("sync", lambda e: e.dma_start(out=self.identf[:], in_=aps["identf"][:]),
             writes=[self.identf[:]], dma=True)
        p.op("vector", lambda e: e.memset(self.ones_b[:], 1.0), writes=[self.ones_b[:]])
        self.ones_f = stack.enter_context(nc.sbuf_tensor("ones_f", [1, 128], F32))
        p.op("vector", lambda e: e.memset(self.neg_s[:], -1e30), writes=[self.neg_s[:]])
        p.op("vector", lambda e: e.memset(self.eps_s[:], EPS), writes=[self.eps_s[:]])
        p.op("vector", lambda e: e.memset(self.ones_f[:], 1.0), writes=[self.ones_f[:]])

    def psf(self):
        t = self.ps_f[self._psf_i % len(self.ps_f)]
        self._psf_i += 1
        return t

    def psb(self):
        t = self.ps_b[self._psb_i % len(self.ps_b)]
        self._psb_i += 1
        return t

    def ps8(self, i):
        if i < 6:
            return self.ps_f[i][:]
        return self.ps_b[i - 6][:].bitcast(F32)

    def negbc(self, n):
        return self.neg_s[:].broadcast_to((128, n))


class Phase:
    def __init__(self, C, name):
        self.C = C
        self.name = name
        self.stack = ExitStack()
        self.names = []

    def sb(self, name, shape, dt):
        full = self.name + "_" + name
        self.names.append(full)
        return self.stack.enter_context(self.C.nc.sbuf_tensor(full, shape, dt))

    def close(self):
        self.stack.close()
        self.C.p.barrier()
        self.C.p.release_dma_sems(self.names)


def wload(C, dst_sb, dram_ap, rows, cols, row0=0, col0=0, kb=128):
    nr = rows // 128
    src = dram_ap[row0:row0 + rows, col0:col0 + cols].rearrange("(r p) c -> p r c", p=128)
    dst = dst_sb[:, 0:nr * cols].rearrange("p (r c) -> p r c", c=cols)
    return C.p.op("gpsimd", lambda e: e.dma_start(out=dst, in_=src),
                  reads=[src], writes=[dst_sb[:, 0:nr * cols]], dma=True)


def transpose128(C, ph, src_ap, dst_ap, dtype, w=128):
    pb = C.psb() if dtype == BF16 else C.psf()
    ident = C.identb if dtype == BF16 else C.identf
    pslice = pb[:, 0:128] if dtype == BF16 else pb[:, 0:128]
    C.p.op("tensor", lambda e: e.transpose(out=pb[0:w, 0:128], in_=src_ap, identity=ident[:]),
           reads=[src_ap, ident[:]], writes=[pb[0:w, 0:128]])
    C.p.op("vector", lambda e: e.tensor_copy(out=dst_ap, in_=pb[0:w, 0:128]),
           reads=[pb[0:w, 0:128]], writes=[dst_ap])


def softmax_rows(C, ph, sc_f32, width, p_bf16, stats, col):
    nmx = stats[:, 3 * col:3 * col + 1]
    ssum = stats[:, 3 * col + 1:3 * col + 2]
    rs = stats[:, 3 * col + 2:3 * col + 3]
    C.p.op("vector", lambda e: e.reduce_max(out=nmx, in_=sc_f32, axis=AX.X, negate=True),
           reads=[sc_f32], writes=[nmx])
    C.p.op("scalar", lambda e: e.activation(out=p_bf16, in_=sc_f32, func=AF.Exp,
                                            bias=nmx, scale=1.0, accum_out=ssum),
           reads=[sc_f32, nmx], writes=[p_bf16, ssum])
    C.p.op("vector", lambda e: e.reciprocal(out=rs, in_=ssum), reads=[ssum], writes=[rs])
    C.p.op("vector", lambda e: e.tensor_scalar_mul(out=p_bf16, in0=p_bf16, scalar1=rs),
           reads=[p_bf16, rs], writes=[p_bf16])


def ln_stats(C, ph, src_ap, sq_tmp, stats, col, nfeat):
    ssum = stats[:, 4 * col + 0:4 * col + 1]
    svar = stats[:, 4 * col + 1:4 * col + 2]
    mu = stats[:, 4 * col + 2:4 * col + 3]
    rstd = stats[:, 4 * col + 3:4 * col + 4]
    p = C.p
    p.op("scalar", lambda e: e.activation(out=sq_tmp, in_=src_ap, func=AF.Square,
                                          accum_out=svar),
         reads=[src_ap], writes=[sq_tmp, svar])
    p.op("vector", lambda e: e.reduce_sum(out=ssum, in_=src_ap, axis=AX.X),
         reads=[src_ap], writes=[ssum])
    p.op("vector", lambda e: e.tensor_scalar_mul(out=mu, in0=ssum, scalar1=1.0 / nfeat),
         reads=[ssum], writes=[mu])
    p.op("vector", lambda e: e.tensor_scalar_mul(out=svar, in0=svar, scalar1=1.0 / nfeat),
         reads=[svar], writes=[svar])
    p.op("vector", lambda e: e.scalar_tensor_tensor(out=svar, in0=mu, scalar=mu, in1=svar,
                                                    op0=ALU.mult, op1=ALU.subtract),
         reads=[mu, svar], writes=[svar])
    p.op("scalar", lambda e: e.activation(out=rstd, in_=svar, func=AF.Ln, bias=C.eps_s[:],
                                          scale=-1.0),
         reads=[svar, C.eps_s[:]], writes=[rstd])
    p.op("scalar", lambda e: e.activation(out=rstd, in_=rstd, func=AF.Exp, scale=-0.5),
         reads=[rstd], writes=[rstd])
    return mu, rstd


def mm_accum(C, psum_ap, lhsT_list, rhs_list, extra=None):
    n = len(lhsT_list)
    for i in range(n):
        lh, rh = lhsT_list[i], rhs_list[i]
        C.p.op("tensor",
               lambda e, lh=lh, rh=rh, s=(i == 0), t=(i == n - 1): e.matmul(
                   out=psum_ap, lhsT=lh, rhs=rh, start=s, stop=t),
               reads=[lh, rh], writes=[psum_ap])


def transpose_batch(C, srcs, dst_ap, dtype):
    n = len(srcs)
    if dtype == BF16:
        pbt = C.psb()
        ident = C.identb
        pb = pbt[:, 0:n * 128]
        slots = [pbt[:, j * 128:(j + 1) * 128] for j in range(n)]
    else:
        pbt = C.psf()
        ident = C.identf
        pb = pbt[:, 0:n * 128]
        slots = [pbt[:, j * 128:(j + 1) * 128] for j in range(n)]
    for j, s in enumerate(srcs):
        C.p.op("tensor", lambda e, s=s, o=slots[j]: e.transpose(out=o, in_=s,
                                                                identity=ident[:]),
               reads=[s, ident[:]], writes=[slots[j]])
    C.p.op("vector", lambda e: e.tensor_copy(out=dst_ap, in_=pb),
           reads=[pb], writes=[dst_ap])


def make_bcast(C, ph, name, row_dram_ap, w=1024):
    row = ph.sb(name + "_r", [1, w], F32)
    out = ph.sb(name + "_bc", [128, w], F32)
    C.p.op("sync", lambda e: e.dma_start(out=row[:], in_=row_dram_ap),
           reads=[row_dram_ap], writes=[row[:]], dma=True)
    for nj in range((w + 511) // 512):
        wv = min(512, w - nj * 512)
        pA = C.psf()
        C.p.op("tensor", lambda e, r=row[0:1, nj * 512:nj * 512 + wv], o=pA[:, 0:wv]: e.matmul(
            out=o, lhsT=C.ones_f[:], rhs=r, start=True, stop=True),
            reads=[row[:], C.ones_f[:]], writes=[pA[:, 0:wv]])
        C.p.op("vector", lambda e, o=out[:, nj * 512:nj * 512 + wv], s=pA[:, 0:wv]:
               e.tensor_copy(out=o, in_=s), reads=[pA[:, 0:wv]], writes=[out[:]])
    return out


def ln_tile(C, ph, src_ap, dst_ap, sq_tmp, stats, col, affine=None):
    mu, rstd = ln_stats(C, ph, src_ap, sq_tmp, stats, col, D)
    C.p.op("vector", lambda e: e.tensor_scalar(
        out=dst_ap, in0=src_ap, scalar1=mu, scalar2=rstd, op0=ALU.subtract, op1=ALU.mult),
        reads=[src_ap, mu, rstd], writes=[dst_ap])
    if affine is not None:
        gbc, bbc = affine
        C.p.op("vector", lambda e: e.tensor_tensor(out=dst_ap, in0=dst_ap, in1=gbc, op=ALU.mult),
               reads=[dst_ap, gbc], writes=[dst_ap])
        C.p.op("vector", lambda e: e.tensor_tensor(out=dst_ap, in0=dst_ap, in1=bbc, op=ALU.add),
               reads=[dst_ap, bbc], writes=[dst_ap])


def make_xn(C, x_dram, ph, xnT, xlnT=None, lnwb=None):
    p = C.p
    xst = [ph.sb(f"xst{i}", [128, D], F32) for i in range(2)]
    sqt = [ph.sb(f"sqt{i}", [128, D], F32) for i in range(2)]
    xnb = [ph.sb(f"xnb{i}", [128, D], BF16) for i in range(2)]
    stats = ph.sb("stats", [128, 4 * NT], F32)
    stats2 = ph.sb("stats2", [128, 4 * NT], F32)
    xlst = [ph.sb(f"xlst{i}", [128, D], BF16) for i in range(2)]
    for ti in range(NT):
        st = xst[ti % 2]
        sq = sqt[ti % 2]
        xb = xnb[ti % 2]
        src = x_dram[ti * 128:(ti + 1) * 128, :]
        p.op("sync", lambda e, s=st, r=src: e.dma_start(out=s[:], in_=r),
             reads=[src], writes=[st[:]], dma=True)
        mu, rstd = ln_stats(C, ph, st[:], sq[:], stats, ti, D)
        p.op("vector", lambda e, s=st, o=xb, m=mu, r=rstd: e.tensor_scalar(
            out=o[:], in0=s[:], scalar1=m, scalar2=r, op0=ALU.subtract, op1=ALU.mult),
            reads=[st[:], mu, rstd], writes=[xb[:]])
        for fb in range(ND):
            src_b = xb[:, fb * 128:(fb + 1) * 128]
            dst_b = xnT[:, fb * T + ti * 128: fb * T + (ti + 1) * 128]
            transpose128(C, ph, src_b, dst_b, BF16)
        if xlnT is not None:
            lnw, lnb = lnwb
            xl = xlst[ti % 2]
            mu2, rstd2 = ln_stats(C, ph, xb[:], sq[:], stats2, ti, D)
            p.op("vector", lambda e, o=xl, i=xb, m=mu2, r=rstd2: e.tensor_scalar(
                out=o[:], in0=i[:], scalar1=m, scalar2=r, op0=ALU.subtract, op1=ALU.mult),
                reads=[xb[:], mu2, rstd2], writes=[xl[:]])
            for fb in range(ND):
                pb = C.psb()
                src_b = xl[:, fb * 128:(fb + 1) * 128]
                dst_b = xlnT[:, fb * T + ti * 128: fb * T + (ti + 1) * 128]
                p.op("tensor", lambda e, s=src_b, pb=pb: e.transpose(
                    out=pb[:, 0:128], in_=s, identity=C.identb[:]),
                    reads=[src_b, C.identb[:]], writes=[pb[:, 0:128]])
                p.op("vector", lambda e, pb=pb, d=dst_b, f=fb: e.tensor_scalar(
                    out=d, in0=pb[:, 0:128], scalar1=lnw[:, f:f + 1], scalar2=lnb[:, f:f + 1],
                    op0=ALU.mult, op1=ALU.add),
                    reads=[pb[:, 0:128], lnw[:], lnb[:]], writes=[dst_b])


def build_layer0(C, x_src, x_dst):
    p = C.p
    aps = C.aps
    ph = Phase(C, "l0")
    xlnT = ph.sb("xlnT", [128, ND * T], BF16)
    g_tok = ph.sb("g_tok", [128, NT * D], BF16)
    lnw = ph.sb("lnw", [128, ND], F32)
    lnb = ph.sb("lnb", [128, ND], F32)
    p.op("sync", lambda e: e.dma_start(out=lnw[:], in_=aps["L0_lnw_col"][:]),
         writes=[lnw[:]], dma=True)
    p.op("sync", lambda e: e.dma_start(out=lnb[:], in_=aps["L0_lnb_col"][:]),
         writes=[lnb[:]], dma=True)
    phx = Phase(C, "l0x")
    xnT = phx.sb("xnT", [128, ND * T], BF16)
    phpre = Phase(C, "l0pre")
    make_xn(C, x_src, phpre, xnT, xlnT, (lnw, lnb))
    phpre.close()

    phf = Phase(C, "l0ffn")
    s1c = phf.sb("s1c", [128, 32], F32)
    s2c = phf.sb("s2c", [128, 32], F32)
    sh1c = phf.sb("sh1c", [128, 32], F32)
    sh2c = phf.sb("sh2c", [128, 32], F32)
    s3c = phf.sb("s3c", [128, 8], F32)
    for nm, t in [("L0_s1c", s1c), ("L0_s2c", s2c), ("L0_sh1c", sh1c),
                  ("L0_sh2c", sh2c), ("L0_s3c", s3c)]:
        p.op("sync", lambda e, t=t, nm=nm: e.dma_start(out=t[:], in_=aps[nm][:]),
             writes=[t[:]], dma=True)
    scr_a = aps["scr_a"]
    scr_b = aps["scr_b"]
    scrA = scr_a.rearrange("(r c) -> r c", c=T)
    scrB = scr_b.rearrange("(r c) -> r c", c=T)

    tA = [phf.sb(f"tA{i}", [128, 512], F32) for i in range(2)]
    tB = [phf.sb(f"tB{i}", [128, 512], F32) for i in range(2)]
    ast0 = phf.sb("ast0", [128, T], BF16)
    ast = [ast0, ast0]
    FH = 4096
    phsA = Phase(C, "l0sa")
    w1b = [phsA.sb(f"w1b{i}", [128, 8 * 512], BF16) for i in range(2)]
    w2b = [phsA.sb(f"w2b{i}", [128, 8 * 512], BF16) for i in range(2)]
    for mg in range(8):
        wl1 = wload(C, w1b[mg % 2], aps["L0_w1T"], D, 512, col0=mg * 512)
        wl2 = wload(C, w2b[mg % 2], aps["L0_w2T"], D, 512, col0=mg * 512)
        for mi in range(4):
            gm = mg * 4 + mi
            a_stage = ast[gm % 2]
            for nj in range(4):
                pA = C.psf()
                pB = C.psf()
                mm_accum(C, pA[:, 0:512],
                         [w1b[mg % 2][:, k * 512 + mi * 128: k * 512 + mi * 128 + 128]
                          for k in range(8)],
                         [xnT[:, k * T + nj * 512: k * T + (nj + 1) * 512] for k in range(8)])
                mm_accum(C, pB[:, 0:512],
                         [w2b[mg % 2][:, k * 512 + mi * 128: k * 512 + mi * 128 + 128]
                          for k in range(8)],
                         [xnT[:, k * T + nj * 512: k * T + (nj + 1) * 512] for k in range(8)])
                t1, t2 = tA[nj % 2], tB[nj % 2]
                scol = s1c[:, gm:gm + 1]
                p.op("vector", lambda e, pA=pA, t1=t1, sc=scol: e.tensor_scalar_mul(
                    out=t1[:], in0=pA[:, 0:512], scalar1=sc),
                    reads=[pA[:, 0:512], s1c[:]], writes=[t1[:]])
                p.op("scalar", lambda e, t1=t1: e.activation(out=t1[:], in_=t1[:], func=AF.Silu),
                     reads=[t1[:]], writes=[t1[:]])
                scol2 = s2c[:, gm:gm + 1]
                p.op("vector", lambda e, pB=pB, t2=t2, sc=scol2: e.tensor_scalar_mul(
                    out=t2[:], in0=pB[:, 0:512], scalar1=sc),
                    reads=[pB[:, 0:512], s2c[:]], writes=[t2[:]])
                dst = a_stage[:, nj * 512:(nj + 1) * 512]
                p.op("vector", lambda e, t1=t1, t2=t2, d=dst: e.tensor_tensor(
                    out=d, in0=t1[:], in1=t2[:], op=ALU.mult),
                    reads=[t1[:], t2[:]], writes=[dst])
            dstdram = scrA[gm * 128:(gm + 1) * 128, :]
            p.op("sync", lambda e, s=a_stage, d=dstdram: e.dma_start(out=d, in_=s[:]),
                 reads=[a_stage[:]], writes=[dstdram], dma=True)
    phsA.close()

    for stage, (wname, scale_c, src_scr, dst_scr) in enumerate(
            [("L0_h1T", sh1c, scrA, scrB), ("L0_h2T", sh2c, scrB, scrA)]):
        phs = Phase(C, f"l0st{stage}")
        whb = phs.sb("whb", [128, 32 * 512], BF16)
        abuf2 = [phs.sb(f"abuf{i}", [128, 32 * 512], BF16) for i in range(2)]
        for mg in range(8):
            wl = wload(C, whb, aps[wname], FH, 512, col0=mg * 512)
            for nj in range(4):
                abuf = abuf2[nj % 2]
                asrc = src_scr[:, nj * 512:(nj + 1) * 512].rearrange("(r p) c -> p r c", p=128)
                abv = abuf[:, 0:32 * 512].rearrange("p (r c) -> p r c", c=512)
                p.op("gpsimd", lambda e, d=abv, s=asrc: e.dma_start(out=d, in_=s),
                     reads=[asrc], writes=[abuf[:]], dma=True)
                for mi in range(4):
                    gm = mg * 4 + mi
                    pA = C.psf()
                    mm_accum(C, pA[:, 0:512],
                             [whb[:, k * 512 + mi * 128: k * 512 + mi * 128 + 128]
                              for k in range(32)],
                             [abuf[:, k * 512:(k + 1) * 512] for k in range(32)])
                    t1 = tA[mi % 2]
                    scol = scale_c[:, gm:gm + 1]
                    p.op("vector", lambda e, pA=pA, t1=t1, sc=scol: e.tensor_scalar_mul(
                        out=t1[:], in0=pA[:, 0:512], scalar1=sc),
                        reads=[pA[:, 0:512], scale_c[:]], writes=[t1[:]])
                    dstd = dst_scr[gm * 128:(gm + 1) * 128, nj * 512:(nj + 1) * 512]
                    gst = ast[mi % 2]
                    gsl = gst[:, nj * 512:(nj + 1) * 512]
                    p.op("scalar", lambda e, t1=t1, d=gsl: e.activation(
                        out=d, in_=t1[:], func=AF.Gelu),
                        reads=[t1[:]], writes=[gsl])
                    p.op("sync", lambda e, s=gsl, d=dstd: e.dma_start(out=d, in_=s),
                         reads=[gsl], writes=[dstd], dma=True)
        phs.close()

    phsD = Phase(C, "l0sd")
    w3b = [phsD.sb(f"w3b{i}", [128, 32 * 128], BF16) for i in range(2)]
    cbuf = phsD.sb("cbuf", [128, 32 * 512], BF16)
    oTst = [phsD.sb(f"oT{i}", [128, 512], F32) for i in range(2)]
    for nj in range(4):
        csrc = scrA[:, nj * 512:(nj + 1) * 512].rearrange("(r p) c -> p r c", p=128)
        cb = cbuf
        cbv = cb[:, 0:32 * 512].rearrange("p (r c) -> p r c", c=512)
        p.op("gpsimd", lambda e, d=cbv, s=csrc: e.dma_start(out=d, in_=s),
             reads=[csrc], writes=[cb[:]], dma=True)
        for mi in range(8):
            w3m = w3b[mi % 2]
            wload(C, w3m, aps["L0_w3T"], FH, 128, col0=mi * 128)
            pA = C.psf()
            mm_accum(C, pA[:, 0:512],
                     [w3m[:, k * 128:(k + 1) * 128] for k in range(32)],
                     [cb[:, k * 512:(k + 1) * 512] for k in range(32)])
            ot = oTst[mi % 2]
            scol = s3c[:, mi:mi + 1]
            p.op("vector", lambda e, pA=pA, ot=ot, sc=scol: e.tensor_scalar_mul(
                out=ot[:], in0=pA[:, 0:512], scalar1=sc),
                reads=[pA[:, 0:512], s3c[:]], writes=[ot[:]])
            for tt in range(4):
                pb = C.psf()
                src_b = ot[:, tt * 128:(tt + 1) * 128]
                p.op("tensor", lambda e, s=src_b, pb=pb: e.transpose(
                    out=pb[:, 0:128], in_=s, identity=C.identf[:]),
                    reads=[src_b, C.identf[:]], writes=[pb[:, 0:128]])
                ti = nj * 4 + tt
                dst = g_tok[:, ti * D + mi * 128: ti * D + (mi + 1) * 128]
                p.op("scalar", lambda e, pb=pb, d=dst: e.activation(
                    out=d, in_=pb[:, 0:128], func=AF.Silu),
                    reads=[pb[:, 0:128]], writes=[dst])
    phsD.close()
    phf.close()
    phx.close()

    pha = Phase(C, "l0att")
    cos_sb = pha.sb("cos", [128, NT * 64], F32)
    sin_sb = pha.sb("sin", [128, NT * 64], F32)
    p.op("sync", lambda e: e.dma_start(
        out=cos_sb[:].rearrange("p (m j) -> p m j", j=64),
        in_=aps["L0_cos64"].rearrange("(m p) j -> p m j", p=128)),
        reads=[aps["L0_cos64"][:]], writes=[cos_sb[:]], dma=True)
    p.op("sync", lambda e: e.dma_start(
        out=sin_sb[:].rearrange("p (m j) -> p m j", j=64),
        in_=aps["L0_sinm64"].rearrange("(m p) j -> p m j", p=128)),
        reads=[aps["L0_sinm64"][:]], writes=[sin_sb[:]], dma=True)
    mk_g = pha.sb("mask_g", [128, 160], F32)
    mk_0 = pha.sb("mask_0", [128, 160], F32)
    p.op("sync", lambda e: e.dma_start(out=mk_g[:], in_=aps["L0_maskg"][:]),
         writes=[mk_g[:]], dma=True)
    p.op("sync", lambda e: e.dma_start(out=mk_0[:], in_=aps["L0_mask0"][:]),
         writes=[mk_0[:]], dma=True)

    ctxT = pha.sb("ctxT", [128, ND * T], BF16)
    stats3 = pha.sb("sfx", [128, 3 * 4], F32)
    for hg in range(4):
        phg = Phase(C, f"l0hg{hg}")
        wq = phg.sb("wq", [128, 8 * 256], BF16)
        wk = phg.sb("wk", [128, 8 * 256], BF16)
        wv = phg.sb("wv", [128, 8 * 256], BF16)
        wload(C, wq, aps["L0_wqkvT"], D, 256, col0=hg * 256)
        wload(C, wk, aps["L0_wqkvT"], D, 256, col0=1024 + hg * 256)
        wload(C, wv, aps["L0_wqkvT"], D, 256, col0=2048 + hg * 256)
        qg = phg.sb("qg", [128, NT * 256], BF16)
        kg = phg.sb("kg", [128, NT * 256], BF16)
        vg = phg.sb("vg", [128, NT * 256], BF16)
        vsh = phg.sb("vsh", [128, NT * 256], BF16)
        qst = [phg.sb(f"qst{i}", [128, 256], F32) for i in range(2)]
        sh_t = [phg.sb(f"sht{i}", [128, 256], F32) for i in range(2)]
        t1_t = [phg.sb(f"t1t{i}", [128, 256], F32) for i in range(2)]

        def rope_evac(which, psum_ap, mi, out_tensor):
            st = qst[mi % 2]
            sh = sh_t[mi % 2]
            t1 = t1_t[mi % 2]
            p.op("vector", lambda e, ps=psum_ap, st=st: e.tensor_copy(out=st[:], in_=ps),
                 reads=[psum_ap], writes=[st[:]])
            sv = st[:].rearrange("p (h t j) -> p h t j", h=4, t=2)
            shv = sh[:].rearrange("p (h t j) -> p h t j", h=4, t=2)
            p.op("vector", lambda e, sv=sv, shv=shv: e.tensor_copy(
                out=shv[:, :, 0:1, :], in_=sv[:, :, 1:2, :]),
                reads=[st[:]], writes=[sh[:]])
            p.op("vector", lambda e, sv=sv, shv=shv: e.tensor_copy(
                out=shv[:, :, 1:2, :], in_=sv[:, :, 0:1, :]),
                reads=[st[:]], writes=[sh[:]])
            cosb = cos_sb[:, mi * 64:(mi + 1) * 64].unsqueeze(1).broadcast_to((128, 4, 64))
            sinb = sin_sb[:, mi * 64:(mi + 1) * 64].unsqueeze(1).broadcast_to((128, 4, 64))
            stv = st[:].rearrange("p (h j) -> p h j", h=4)
            shv2 = sh[:].rearrange("p (h j) -> p h j", h=4)
            t1v = t1[:].rearrange("p (h j) -> p h j", h=4)
            p.op("vector", lambda e: e.tensor_tensor(out=t1v, in0=stv, in1=cosb, op=ALU.mult),
                 reads=[st[:], cos_sb[:]], writes=[t1[:]])
            p.op("vector", lambda e: e.tensor_tensor(out=shv2, in0=shv2, in1=sinb, op=ALU.mult),
                 reads=[sh[:], sin_sb[:]], writes=[sh[:]])
            dst = out_tensor[:, mi * 256:(mi + 1) * 256]
            p.op("vector", lambda e, d=dst: e.tensor_tensor(out=d, in0=t1[:], in1=sh[:],
                                                            op=ALU.add),
                 reads=[t1[:], sh[:]], writes=[dst])

        for mi in range(NT):
            for which, wsb, out_t in [("q", wq, qg), ("k", wk, kg), ("v", wv, vg)]:
                pA = C.psf()
                mm_accum(C, pA[:, 0:256],
                         [xlnT[:, k * T + mi * 128: k * T + mi * 128 + 128] for k in range(8)],
                         [wsb[:, k * 256:(k + 1) * 256] for k in range(8)])
                if which == "v":
                    dst = vg[:, mi * 256:(mi + 1) * 256]
                    p.op("vector", lambda e, ps=pA, d=dst: e.tensor_copy(out=d, in_=ps[:, 0:256]),
                         reads=[pA[:, 0:256]], writes=[dst])
                else:
                    rope_evac(which, pA[:, 0:256], mi, out_t)
        p.op("gpsimd", lambda e, d=vsh[32:128, 0:NT * 256], s=vg[0:96, 0:NT * 256]:
             e.dma_start(out=d, in_=s),
             reads=[vg[:]], writes=[vsh[:]], dma=True)
        p.op("gpsimd", lambda e, d=vsh[0:32, 256:NT * 256], s=vg[96:128, 0:(NT - 1) * 256]:
             e.dma_start(out=d, in_=s),
             reads=[vg[:]], writes=[vsh[:]], dma=True)
        qT = phg.sb("qT", [128, 2 * T], BF16)
        kT = phg.sb("kT", [128, 2 * T], BF16)
        for mi in range(NT):
            for b2 in range(2):
                src_q = qg[:, mi * 256 + b2 * 128: mi * 256 + (b2 + 1) * 128]
                transpose128(C, pha, src_q, qT[:, b2 * T + mi * 128:b2 * T + (mi + 1) * 128], BF16)
                src_k = kg[:, mi * 256 + b2 * 128: mi * 256 + (b2 + 1) * 128]
                transpose128(C, pha, src_k, kT[:, b2 * T + mi * 128:b2 * T + (mi + 1) * 128], BF16)
        scf = [phg.sb(f"scf{i}", [128, 160], F32) for i in range(2)]
        pbf = [phg.sb(f"pbf{i}", [128, 160], BF16) for i in range(2)]
        ptp = [phg.sb(f"ptp{i}", [128, 128], BF16) for i in range(2)]
        ptc = [phg.sb(f"ptc{i}", [128, 128], BF16) for i in range(2)]
        for h4 in range(4):
            h = hg * 4 + h4
            rb = (h4 % 2) * 64
            blk = h4 // 2
            for mi in range(NT):
                ps = C.psf()
                lh = qT[rb:rb + 64, blk * T + mi * 128: blk * T + (mi + 1) * 128]
                if mi == 0:
                    rh = kT[rb:rb + 64, blk * T + 0: blk * T + 128]
                    C.p.op("tensor", lambda e, o=ps[:, 32:160], lh=lh, rh=rh: e.matmul(
                        out=o, lhsT=lh, rhs=rh, start=True, stop=True),
                        reads=[lh, rh], writes=[ps[:, 0:160]])
                else:
                    rh = kT[rb:rb + 64, blk * T + mi * 128 - 32: blk * T + (mi + 1) * 128]
                    C.p.op("tensor", lambda e, o=ps[:, 0:160], lh=lh, rh=rh: e.matmul(
                        out=o, lhsT=lh, rhs=rh, start=True, stop=True),
                        reads=[lh, rh], writes=[ps[:, 0:160]])
                sc = scf[mi % 2]
                pb16 = pbf[mi % 2]
                p.op("scalar", lambda e, ps=ps, sc=sc: e.activation(
                    out=sc[:], in_=ps[:, 0:160], func=AF.Identity, scale=0.125),
                    reads=[ps[:, 0:160]], writes=[sc[:]])
                if mi == 0:
                    p.op("vector", lambda e, sc=sc: e.memset(sc[:, 0:32], -1e30),
                         reads=[], writes=[sc[:, 0:32]])
                mk = mk_0 if mi == 0 else mk_g
                p.op("vector", lambda e, sc=sc, mk=mk: e.tensor_tensor(
                    out=sc[:], in0=sc[:], in1=mk[:], op=ALU.add),
                    reads=[sc[:], mk[:]], writes=[sc[:]])
                softmax_rows(C, phg, sc[:], 160, pb16[:], stats3, mi % 4)
                ppv = ptp[mi % 2]
                pcv = ptc[mi % 2]
                pbb = C.psb()
                p.op("tensor", lambda e, s=pb16, pb=pbb: e.transpose(
                    out=pb[0:32, 0:128], in_=s[:, 0:32], identity=C.identb[:]),
                    reads=[pb16[:], C.identb[:]], writes=[pbb[:, 0:128]])
                p.op("vector", lambda e, pb=pbb, d=ppv: e.tensor_copy(
                    out=d[0:32, :], in_=pb[0:32, 0:128]),
                    reads=[pbb[:, 0:128]], writes=[ppv[:]])
                pbb2 = C.psb()
                p.op("tensor", lambda e, s=pb16, pb=pbb2: e.transpose(
                    out=pb[:, 0:128], in_=s[:, 32:160], identity=C.identb[:]),
                    reads=[pb16[:], C.identb[:]], writes=[pbb2[:, 0:128]])
                p.op("vector", lambda e, pb=pbb2, d=pcv: e.tensor_copy(
                    out=d[:], in_=pb[:, 0:128]),
                    reads=[pbb2[:, 0:128]], writes=[pcv[:]])
                pc = C.psf()
                if mi > 0:
                    lh1 = vsh[0:32, mi * 256 + h4 * 64: mi * 256 + (h4 + 1) * 64]
                    C.p.op("tensor", lambda e, o=pc[0:64, 0:128], lh=lh1, rh=ppv: e.matmul(
                        out=o, lhsT=lh, rhs=rh[0:32, :], start=True, stop=False),
                        reads=[lh1, ppv[:]], writes=[pc[:, 0:128]])
                lh2 = vg[:, mi * 256 + h4 * 64: mi * 256 + (h4 + 1) * 64]
                C.p.op("tensor", lambda e, o=pc[0:64, 0:128], lh=lh2, rh=pcv,
                       s=(mi == 0): e.matmul(out=o, lhsT=lh, rhs=rh[:], start=s, stop=True),
                       reads=[lh2, pcv[:]], writes=[pc[:, 0:128]])
                rbase = (h % 2) * 64
                dst = ctxT[rbase:rbase + 64, (h // 2) * T + mi * 128:(h // 2) * T + (mi + 1) * 128]
                p.op("vector", lambda e, ps=pc, d=dst: e.tensor_copy(out=d, in_=ps[0:64, 0:128]),
                     reads=[pc[:, 0:128]], writes=[dst])
        phg.close()

    wob = pha.sb("wob", [128, 8 * 1024], BF16)
    wload(C, wob, aps["L0_woutT"], D, 1024)
    sst = [pha.sb(f"sst{i}", [128, D], F32) for i in range(2)]
    xst2 = [pha.sb(f"xst2{i}", [128, D], F32) for i in range(2)]
    xnt2 = [pha.sb(f"xnt2{i}", [128, D], F32) for i in range(2)]
    sqt2 = [pha.sb(f"sqt2{i}", [128, D], F32) for i in range(2)]
    stats4 = pha.sb("stats4", [128, 4 * NT], F32)
    for mi in range(NT):
        xs = xst2[mi % 2]
        src = x_src[mi * 128:(mi + 1) * 128, :]
        p.op("sync", lambda e, s=xs, r=src: e.dma_start(out=s[:], in_=r),
             reads=[src], writes=[xs[:]], dma=True)
        xn = xnt2[mi % 2]
        ln_tile(C, pha, xs[:], xn[:], sqt2[mi % 2][:], stats4, mi)
        ss = sst[mi % 2]
        for nj in range(2):
            pA = C.psf()
            mm_accum(C, pA[:, 0:512],
                     [ctxT[:, k * T + mi * 128: k * T + mi * 128 + 128] for k in range(8)],
                     [wob[:, k * 1024 + nj * 512: k * 1024 + (nj + 1) * 512] for k in range(8)])
            dsl = ss[:, nj * 512:(nj + 1) * 512]
            xnsl = xn[:, nj * 512:(nj + 1) * 512]
            p.op("vector", lambda e, ps=pA, d=dsl, x2=xnsl: e.tensor_tensor(
                out=d, in0=ps[:, 0:512], in1=x2, op=ALU.add),
                reads=[pA[:, 0:512], xnsl], writes=[dsl])
        gsl = g_tok[:, mi * D:(mi + 1) * D]
        p.op("vector", lambda e, s=ss, g=gsl: e.tensor_tensor(out=s[:], in0=s[:], in1=g,
                                                              op=ALU.mult),
             reads=[ss[:], gsl], writes=[ss[:]])
        p.op("vector", lambda e, s=ss, x=xs: e.tensor_tensor(out=x[:], in0=x[:], in1=s[:],
                                                             op=ALU.add),
             reads=[ss[:], xs[:]], writes=[xs[:]])
        dst = x_dst[mi * 128:(mi + 1) * 128, :]
        p.op("sync", lambda e, s=xs, d=dst: e.dma_start(out=d, in_=s[:]),
             reads=[xs[:]], writes=[dst], dma=True)
    pha.close()
    ph.close()


L1_DBG = None


def _dump_flat(C, src_ap, x_dst, nelem):
    dst = x_dst.rearrange("a b -> (a b)")[0:nelem]
    C.p.op("gpsimd", lambda e: e.dma_start(out=dst, in_=src_ap),
           reads=[src_ap], writes=[dst], dma=True)


def build_layer1(C, x_src, x_dst):
    p = C.p
    aps = C.aps
    NH, DH, FF, NE = 16, 64, 4096, 8
    s_scr = aps["s_scr"].rearrange("(r c) -> r c", c=D)
    ctx2d = aps["ctx_scr"].rearrange("(r c) -> r c", c=ND * T)

    ph = Phase(C, "l1")
    phH = Phase(C, "l1H")
    h_tok = phH.sb("h_tok", [128, NT * D], BF16)
    phB = Phase(C, "l1B")
    vbuf = phB.sb("vbuf", [128, NT * D], BF16)
    arenaA = phB.sb("arenaA", [128, ND * T], BF16)
    arenaB = phB.sb("arenaB", [128, ND * T], BF16)
    phx = Phase(C, "l1x")
    xnT = phx.sb("xnT", [128, ND * T], BF16)
    phpre = Phase(C, "l1pre")
    make_xn(C, x_src, phpre, xnT)
    phpre.close()

    pha = Phase(C, "l1att")
    qT = arenaA
    kT = arenaB
    v_tok = vbuf
    wbuf = pha.sb("wbuf", [128, 8 * 1024], BF16)
    bcol = pha.sb("bcol", [128, 2 * ND], F32)
    p.op("sync", lambda e: e.dma_start(out=bcol[:, 0:ND], in_=aps["L1_bqc"][:]),
         writes=[bcol[:, 0:ND]], dma=True)
    p.op("sync", lambda e: e.dma_start(out=bcol[:, ND:2 * ND], in_=aps["L1_bkc"][:]),
         writes=[bcol[:, ND:2 * ND]], dma=True)
    bvrow = pha.sb("bvrow", [1, D], BF16)
    p.op("gpsimd", lambda e: e.dma_start(out=bvrow[:], in_=aps["L1_bv_row"][:]),
         reads=[aps["L1_bv_row"][:]], writes=[bvrow[:]], dma=True)
    for wi, (wname, dstT, bc0) in enumerate([("L1_wqT", qT, 0), ("L1_wkT", kT, ND)]):
        wload(C, wbuf, aps[wname], D, 1024)
        for m in range(ND):
            for nj in range(4):
                pA = C.psf()
                mm_accum(C, pA[:, 0:512],
                         [wbuf[:, k * 1024 + m * 128: k * 1024 + m * 128 + 128]
                          for k in range(8)],
                         [xnT[:, k * T + nj * 512: k * T + (nj + 1) * 512] for k in range(8)])
                dst = dstT[:, m * T + nj * 512: m * T + (nj + 1) * 512]
                bc = bcol[:, bc0 + m: bc0 + m + 1]
                p.op("vector", lambda e, ps=pA, d=dst, b=bc: e.tensor_scalar(
                    out=d, in0=ps[:, 0:512], scalar1=b, scalar2=None, op0=ALU.add),
                    reads=[pA[:, 0:512], bcol[:]], writes=[dst])
    wload(C, wbuf, aps["L1_wvT"], D, 1024)
    for mi in range(NT):
        for nj in range(2):
            pA = C.psf()
            lhs = [xnT[:, k * T + mi * 128: k * T + mi * 128 + 128] for k in range(8)]
            rhs = [wbuf[:, k * 1024 + nj * 512: k * 1024 + (nj + 1) * 512] for k in range(8)]
            lhs.append(C.ones_b[0:1, 0:128])
            rhs.append(bvrow[0:1, nj * 512:(nj + 1) * 512])
            mm_accum(C, pA[:, 0:512], lhs, rhs)
            dst = v_tok[:, mi * D + nj * 512: mi * D + (nj + 1) * 512]
            p.op("vector", lambda e, ps=pA, d=dst: e.tensor_copy(out=d, in_=ps[:, 0:512]),
                 reads=[pA[:, 0:512]], writes=[dst])
    sc2 = [pha.sb(f"sc{i}", [128, T], F32) for i in range(2)]
    pbf2 = [pha.sb(f"pbf{i}", [128, T], BF16) for i in range(2)]
    sfx = pha.sb("sfx", [128, 12], F32)
    sst = [pha.sb(f"sst{i}", [128, D], BF16) for i in range(2)]
    for mi in range(NT):
        sc = sc2[mi % 2]
        pbf = pbf2[mi % 2]
        for nj in range(4):
            pA = C.psf()
            mm_accum(C, pA[:, 0:512],
                     [qT[:, k * T + mi * 128: k * T + mi * 128 + 128] for k in range(8)],
                     [kT[:, k * T + nj * 512: k * T + (nj + 1) * 512] for k in range(8)])
            dst = sc[:, nj * 512:(nj + 1) * 512]
            p.op("scalar", lambda e, ps=pA, d=dst: e.copy(out=d, in_=ps[:, 0:512]),
                 reads=[pA[:, 0:512]], writes=[dst])
        softmax_rows(C, pha, sc[:], T, pbf[:], sfx, mi % 2)
        for g4 in range(4):
            transpose_batch(C, [pbf[:, (g4 * 4 + j) * 128:(g4 * 4 + j + 1) * 128]
                                for j in range(4)],
                            pbf[:, g4 * 512:(g4 + 1) * 512], BF16)
        ssl = sst[mi % 2]
        for nj in range(2):
            pA = C.psf()
            mm_accum(C, pA[:, 0:512],
                     [pbf[:, kb * 128:(kb + 1) * 128] for kb in range(16)],
                     [v_tok[:, kb * D + nj * 512: kb * D + (nj + 1) * 512] for kb in range(16)])
            dst = ssl[:, nj * 512:(nj + 1) * 512]
            p.op("vector", lambda e, ps=pA, d=dst: e.tensor_copy(out=d, in_=ps[:, 0:512]),
                 reads=[pA[:, 0:512]], writes=[dst])
        ddst = s_scr[mi * 128:(mi + 1) * 128, :]
        p.op("sync", lambda e, s=ssl, d=ddst: e.dma_start(out=d, in_=s[:]),
             reads=[ssl[:]], writes=[ddst], dma=True)
    pha.close()
    if L1_DBG == "s":
        _dump_flat(C, aps["s_scr"][0:T * D], x_dst, T * D)
        phx.close(); phB.close(); phH.close(); ph.close()
        return

    phq = Phase(C, "l1qkv")
    fv = vbuf
    wqkvb = [arenaB[:, 4 * T:4 * T + 4096]]
    for cg in range(2):
        wb2 = wqkvb[0]
        wload(C, wb2, aps["L1_wqkv"], D, 512, col0=2048 + cg * 512)
        for mi in range(NT):
            pA = C.psf()
            mm_accum(C, pA[:, 0:512],
                     [xnT[:, k * T + mi * 128: k * T + mi * 128 + 128] for k in range(8)],
                     [wb2[:, k * 512:(k + 1) * 512] for k in range(8)])
            dst = fv[:, mi * D + cg * 512: mi * D + (cg + 1) * 512]
            p.op("vector", lambda e, ps=pA, d=dst: e.tensor_copy(out=d, in_=ps[:, 0:512]),
                 reads=[pA[:, 0:512]], writes=[dst])
    scrbuf = phq.sb("scrbuf", [128, 8 * 512], BF16)
    peb = arenaB[:, 4 * T + 4096:4 * T + 8192]
    ucol = phq.sb("ucol", [128, ND], F32)
    vcol = phq.sb("vcol", [128, ND], F32)
    p.op("sync", lambda e: e.dma_start(out=ucol[:], in_=aps["L1_uc"][:]),
         writes=[ucol[:]], dma=True)
    p.op("sync", lambda e: e.dma_start(out=vcol[:], in_=aps["L1_vc"][:]),
         writes=[vcol[:]], dma=True)
    tri = phq.sb("tri", [128, 128], F32)
    p.op("sync", lambda e: e.dma_start(out=tri[:], in_=aps["diagbias"][:]),
         writes=[tri[:]], dma=True)
    phz0 = Phase(C, "l1z0")
    zz = phz0.sb("zz", [128, 2065], BF16)
    p.op("vector", lambda e: e.memset(zz[:], 0.0), writes=[zz[:]])
    for shd0 in (aps["scr_sh0"], aps["scr_sh1"]):
        dstz = shd0[0:128 * 2065].rearrange("(r c) -> r c", c=2065)
        p.op("gpsimd", lambda e, d=dstz: e.dma_start(out=d, in_=zz[:]),
             reads=[zz[:]], writes=[dstz], dma=True)
    phz0.close()
    qu0 = phq.sb("qu0", [128, T], BF16)
    qv0 = phq.sb("qv0", [128, T], BF16)
    qu2 = [qu0, qu0]
    qv2 = [qv0, qv0]
    scrbuf2 = phq.sb("scrbuf2", [128, 4 * 512], BF16)
    scc2 = [phq.sb(f"scc{i}", [128, T], F32) for i in range(2)]
    pbfc2 = [phq.sb(f"pbfc{i}", [128, T], BF16) for i in range(2)]
    sfx2 = phq.sb("sfx2", [128, 12], F32)
    ctxs = [phq.sb(f"ctxs{i}", [64, 128], BF16) for i in range(2)]
    for t_ in (qu2[0], qv2[0], scc2[0], scc2[1], pbfc2[0], pbfc2[1]):
        p.op("vector", lambda e, t_=t_: e.memset(t_[:], 0.0), writes=[t_[:]])
    shsc = [aps["scr_sh0"], aps["scr_sh1"]]
    NHG = 4
    for g2 in range(2):
        qTg = arenaA[:, 0:4 * T]
        kTg = arenaA[:, 4 * T:8 * T]
        rTg = arenaB[:, 0:4 * T]
        for (dstT, dram, c0) in [(qTg, "L1_wqkv", g2 * 512),
                                 (kTg, "L1_wqkv", 1024 + g2 * 512),
                                 (rTg, "L1_wr", g2 * 512)]:
            wb2 = wqkvb[0]
            wload(C, wb2, aps[dram], D, 512, col0=c0)
            if dram == "L1_wr":
                for nj in range(4):
                    pe_sl = aps["L1_peT"][:, nj * 512:(nj + 1) * 512].rearrange(
                        "(r p) c -> p r c", p=128)
                    pev = peb[:].rearrange("p (r c) -> p r c", c=512)
                    p.op("gpsimd", lambda e, d=pev, s=pe_sl: e.dma_start(out=d, in_=s),
                         reads=[aps["L1_peT"][:, nj * 512:(nj + 1) * 512]],
                         writes=[peb[:]], dma=True)
                    for m4 in range(4):
                        pA = C.psf()
                        mm_accum(C, pA[:, 0:512],
                                 [wb2[:, k * 512 + m4 * 128: k * 512 + m4 * 128 + 128]
                                  for k in range(8)],
                                 [peb[:, k * 512:(k + 1) * 512] for k in range(8)])
                        dst = dstT[:, m4 * T + nj * 512: m4 * T + (nj + 1) * 512]
                        p.op("vector", lambda e, ps=pA, d=dst: e.tensor_copy(
                            out=d, in_=ps[:, 0:512]),
                            reads=[pA[:, 0:512]], writes=[dst])
            else:
                for m4 in range(4):
                    for nj in range(4):
                        pA = C.psf()
                        mm_accum(C, pA[:, 0:512],
                                 [wb2[:, k * 512 + m4 * 128: k * 512 + m4 * 128 + 128]
                                  for k in range(8)],
                                 [xnT[:, k * T + nj * 512: k * T + (nj + 1) * 512]
                                  for k in range(8)])
                        dst = dstT[:, m4 * T + nj * 512: m4 * T + (nj + 1) * 512]
                        p.op("vector", lambda e, ps=pA, d=dst: e.tensor_copy(
                            out=d, in_=ps[:, 0:512]),
                            reads=[pA[:, 0:512]], writes=[dst])
        if L1_DBG in ("qg", "kg", "rg") and g2 == 0:
            srcT = {"qg": qTg, "kg": kTg, "rg": rTg}[L1_DBG]
            _dump_flat(C, srcT, x_dst, 128 * 4 * T)
            phq.close(); phx.close(); phB.close(); phH.close(); ph.close()
            return
        for j in range(8):
            h = g2 * 8 + j
            rb = (j % 2) * 64
            blk = j // 2
            qu = qu2[j % 2]
            qv = qv2[j % 2]
            usl = ucol[(h % 2) * 64:(h % 2) * 64 + 64, (h // 2):(h // 2) + 1]
            vsl = vcol[(h % 2) * 64:(h % 2) * 64 + 64, (h // 2):(h // 2) + 1]
            qsl = qTg[rb:rb + 64, blk * T:(blk + 1) * T]
            p.op("vector", lambda e, d=qu[rb:rb + 64, :], s=qsl, b=usl: e.tensor_scalar(
                out=d, in0=s, scalar1=b, scalar2=None, op0=ALU.add),
                reads=[qsl, ucol[:]], writes=[qu[:]])
            p.op("vector", lambda e, d=qv[rb:rb + 64, :], s=qsl, b=vsl: e.tensor_scalar(
                out=d, in0=s, scalar1=b, scalar2=None, op0=ALU.add),
                reads=[qsl, vcol[:]], writes=[qv[:]])
            for mi in range(NT):
                W = (mi + 1) * 128
                c0 = (T - W) // 512
                ncj = (mi + 4) // 4
                shd = shsc[mi % 2]
                rawst = scrbuf[:, 0:T]
                shf = scrbuf[:, T:2 * T] if mi % 2 == 0 else scrbuf2[:, 0:T]
                scc = scc2[mi % 2]
                pbfc = pbfc2[mi % 2]
                for cj in range(c0, 4):
                    pA = C.psf()
                    lh = qv[rb:rb + 64, mi * 128:(mi + 1) * 128]
                    rh = rTg[rb:rb + 64, blk * T + cj * 512: blk * T + (cj + 1) * 512]
                    C.p.op("tensor", lambda e, o=pA[:, 0:512], lh=lh, rh=rh: e.matmul(
                        out=o, lhsT=lh, rhs=rh, start=True, stop=True),
                        reads=[lh, rh], writes=[pA[:, 0:512]])
                    dst = rawst[:, cj * 512:(cj + 1) * 512]
                    p.op("scalar", lambda e, ps=pA, d=dst: e.copy(out=d, in_=ps[:, 0:512]),
                         reads=[pA[:, 0:512]], writes=[dst])
                wr = aps_pitch_write_view(shd, c0)
                p.op("gpsimd", lambda e, d=wr, s=rawst[:, c0 * 512:T]: e.dma_start(
                    out=d, in_=s),
                    reads=[rawst], writes=[wr], dma=True)
                rd = shd[T - mi * 128: T - mi * 128 + 128 * T].rearrange(
                    "(r c) -> r c", c=T)[:, 0:W]
                p.op("gpsimd", lambda e, d=shf[:, 0:W], s=rd: e.dma_start(out=d, in_=s),
                     reads=[rd], writes=[shf], dma=True)
                for cj in range(ncj):
                    wv = min(512, W - cj * 512)
                    pA = C.psf()
                    lh = qu[rb:rb + 64, mi * 128:(mi + 1) * 128]
                    rh = kTg[rb:rb + 64, blk * T + cj * 512: blk * T + cj * 512 + wv]
                    C.p.op("tensor", lambda e, o=pA[:, 0:wv], lh=lh, rh=rh: e.matmul(
                        out=o, lhsT=lh, rhs=rh, start=True, stop=True),
                        reads=[lh, rh], writes=[pA[:, 0:wv]])
                    dsc = scc[:, cj * 512:cj * 512 + wv]
                    p.op("vector", lambda e, ps=pA[:, 0:wv], d=dsc,
                         b=shf[:, cj * 512:cj * 512 + wv]: e.tensor_tensor(
                             out=d, in0=ps, in1=b, op=ALU.add),
                         reads=[pA[:, 0:wv], shf], writes=[dsc])
                if L1_DBG == "sh15" and h == 0 and mi == 15:
                    _dump_flat(C, shf[:, 0:T], x_dst, 128 * T)
                    phq.close(); phx.close(); phB.close(); phH.close(); ph.close()
                    return
                dg = scc[:, mi * 128:W]
                p.op("vector", lambda e, d=dg: e.tensor_tensor(out=d, in0=d, in1=tri[:],
                                                               op=ALU.add),
                     reads=[dg, tri[:]], writes=[dg])
                if L1_DBG == "sc15" and h == 0 and mi == 15:
                    _dump_flat(C, scc[:, 0:T], x_dst, 128 * T)
                    phq.close(); phx.close(); phB.close(); phH.close(); ph.close()
                    return
                nmx = sfx2[:, 0:1]
                ssum = sfx2[:, 1:2]
                rsx = sfx2[:, 2:3]
                scw = scc[:, 0:W]
                pbw = pbfc[:, 0:W]
                p.op("vector", lambda e, i_=scw, o=nmx: e.reduce_max(out=o, in_=i_, axis=AX.X,
                                                                     negate=True),
                     reads=[scw], writes=[nmx])
                p.op("vector", lambda e, o=nmx: e.tensor_scalar_mul(out=o, in0=o, scalar1=0.125),
                     reads=[nmx], writes=[nmx])
                p.op("scalar", lambda e, o=pbw, i_=scw, b=nmx, a=ssum: e.activation(
                    out=o, in_=i_, func=AF.Exp, bias=b, scale=0.125, accum_out=a),
                     reads=[scw, nmx], writes=[pbw, ssum])
                p.op("vector", lambda e, o=rsx, i_=ssum: e.reciprocal(out=o, in_=i_),
                     reads=[ssum], writes=[rsx])
                p.op("vector", lambda e, o=pbw, s=rsx: e.tensor_scalar_mul(out=o, in0=o,
                                                                           scalar1=s),
                     reads=[pbw, rsx], writes=[pbw])
                if L1_DBG == "p15" and h == 0 and mi == 15:
                    _dump_flat(C, pbfc[:, 0:T], x_dst, 128 * T)
                    phq.close(); phx.close(); phB.close(); phH.close(); ph.close()
                    return
                for g4 in range((mi + 4) // 4):
                    nb = min(4, mi + 1 - g4 * 4)
                    transpose_batch(C, [pbfc[:, (g4 * 4 + i2) * 128:(g4 * 4 + i2 + 1) * 128]
                                        for i2 in range(nb)],
                                    pbfc[:, g4 * 512:g4 * 512 + nb * 128], BF16)
                if L1_DBG == "pt15" and h == 0 and mi == 15:
                    _dump_flat(C, pbfc[:, 0:T], x_dst, 128 * T)
                    phq.close(); phx.close(); phB.close(); phH.close(); ph.close()
                    return
                if L1_DBG == "fv0" and h == 0 and mi == 0:
                    _dump_flat(C, fv[:, 0:NT * D], x_dst, T * D)
                    phq.close(); phx.close(); phB.close(); phH.close(); ph.close()
                    return
                pc = C.psf()
                for kb in range(mi + 1):
                    lh = fv[:, kb * D + h * 64: kb * D + (h + 1) * 64]
                    rh = pbfc[:, kb * 128:(kb + 1) * 128]
                    C.p.op("tensor", lambda e, o=pc[0:64, 0:128], lh=lh, rh=rh,
                           s=(kb == 0), t=(kb == mi): e.matmul(out=o, lhsT=lh, rhs=rh,
                                                               start=s, stop=t),
                           reads=[lh, rh], writes=[pc[:, 0:128]])
                cstg = ctxs[mi % 2]
                p.op("vector", lambda e, ps=pc, d=cstg[0:64, 0:128]: e.tensor_copy(
                    out=d, in_=ps[0:64, 0:128]),
                    reads=[pc[:, 0:128]], writes=[cstg[:]])
                grb = (h % 2) * 64
                ddst = ctx2d[grb:grb + 64,
                             (h // 2) * T + mi * 128:(h // 2) * T + (mi + 1) * 128]
                p.op("sync", lambda e, s=cstg[0:64, 0:128], d=ddst: e.dma_start(
                    out=d, in_=s),
                    reads=[cstg[:]], writes=[ddst], dma=True)
    phq.close()
    phx.close()
    phB.close()
    p.fence()
    if L1_DBG == "ctx":
        _dump_flat(C, aps["ctx_scr"][0:128 * ND * T], x_dst, 128 * ND * T)
        phH.close(); ph.close()
        return

    phh = Phase(C, "l1h")
    g1bc = make_bcast(C, phh, "g1", aps["L1_ln1g_row"][:])
    b1bc = make_bcast(C, phh, "b1", aps["L1_ln1b_row"][:])
    wob = phh.sb("wob", [128, 8 * 1024], BF16)
    wload(C, wob, aps["L1_wo"], D, 1024)
    ctt = [phh.sb(f"ctt{i}", [128, ND * 128], BF16) for i in range(2)]
    xst = [phh.sb(f"xst{i}", [128, D], F32) for i in range(2)]
    sqt = [phh.sb(f"sqt{i}", [128, D], F32) for i in range(2)]
    xnt = [phh.sb(f"xnt{i}", [128, D], F32) for i in range(2)]
    stats5 = phh.sb("stats5", [128, 4 * NT], F32)
    stats6 = phh.sb("stats6", [128, 4 * NT], F32)
    for mi in range(NT):
        ct = ctt[mi % 2]
        csrc = ctx2d[:, :].rearrange("p (b t) -> p b t", t=T)[:, :, mi * 128:(mi + 1) * 128]
        cdv = ct[:].rearrange("p (b t) -> p b t", t=128)
        p.op("sync", lambda e, d=cdv, s=csrc: e.dma_start(out=d, in_=s),
             reads=[ctx2d[:, :]], writes=[ct[:]], dma=True)
        xs = xst[mi % 2]
        src = x_src[mi * 128:(mi + 1) * 128, :]
        p.op("sync", lambda e, s=xs, r=src: e.dma_start(out=s[:], in_=r),
             reads=[src], writes=[xs[:]], dma=True)
        xn = xnt[mi % 2]
        ln_tile(C, phh, xs[:], xn[:], sqt[mi % 2][:], stats5, mi)
        for nj in range(2):
            pA = C.psf()
            mm_accum(C, pA[:, 0:512],
                     [ct[:, k * 128:(k + 1) * 128] for k in range(8)],
                     [wob[:, k * 1024 + nj * 512: k * 1024 + (nj + 1) * 512]
                      for k in range(8)])
            dsl = xn[:, nj * 512:(nj + 1) * 512]
            p.op("vector", lambda e, ps=pA, d=dsl: e.tensor_tensor(
                out=d, in0=ps[:, 0:512], in1=d, op=ALU.add),
                reads=[pA[:, 0:512], dsl], writes=[dsl])
        hsl = h_tok[:, mi * D:(mi + 1) * D]
        ln_tile(C, phh, xn[:], hsl, sqt[mi % 2][:], stats6, mi, affine=(g1bc[:], b1bc[:]))
    phh.close()
    p.fence()
    if L1_DBG == "h":
        _dump_flat(C, h_tok[:], x_dst, T * D)
        phH.close(); ph.close()
        return

    phe = Phase(C, "l1ff")
    hT = phe.sb("hT", [128, ND * T], BF16)
    for mi in range(NT):
        for fb in range(ND):
            src_b = h_tok[:, mi * D + fb * 128: mi * D + (fb + 1) * 128]
            pbt = C.psb()
            pb = pbt[:, 0:128]
            C.p.op("tensor", lambda e, s=src_b, o=pb: e.transpose(
                out=o, in_=s, identity=C.identb[:]),
                reads=[src_b, C.identb[:]], writes=[pb])
            dst = hT[:, fb * T + mi * 128: fb * T + (mi + 1) * 128]
            p.op("vector", lambda e, s=pb, d=dst: e.tensor_copy(out=d, in_=s),
                 reads=[pb], writes=[dst])
    g2bc = make_bcast(C, phe, "g2", aps["L1_ln2g_row"][:])
    b2bc = make_bcast(C, phe, "b2", aps["L1_ln2b_row"][:])
    bff1c = phe.sb("bff1c", [128, 32], F32)
    p.op("sync", lambda e: e.dma_start(out=bff1c[:], in_=aps["L1_bff1c"][:]),
         writes=[bff1c[:]], dma=True)
    bff2row = phe.sb("bff2row", [1, D], BF16)
    p.op("gpsimd", lambda e: e.dma_start(out=bff2row[:], in_=aps["L1_bff2_row"][:]),
         reads=[aps["L1_bff2_row"][:]], writes=[bff2row[:]], dma=True)
    f1c = phe.sb("f1c", [128, 32 * 512], BF16)
    wf1b = [phe.sb(f"wf1b{i}", [128, 8 * 512], BF16) for i in range(2)]
    wf2b = [phe.sb(f"wf2b{i}", [128, 1024], BF16) for i in range(2)]
    stats7 = phe.sb("stats7", [128, 4 * NT], F32)
    sq2 = [phe.sb(f"sq2{i}", [128, D], F32) for i in range(2)]
    tmpf = [phe.sb(f"tmpf{i}", [128, D], F32) for i in range(2)]
    for tc in range(4):
        t0 = tc * 512
        for mg in range(8):
            wb3 = wf1b[mg % 2]
            wload(C, wb3, aps["L1_wff1"], D, 512, col0=mg * 512)
            for m4 in range(4):
                fch = mg * 4 + m4
                pA = C.psf()
                mm_accum(C, pA[:, 0:512],
                         [wb3[:, k * 512 + m4 * 128: k * 512 + m4 * 128 + 128]
                          for k in range(8)],
                         [hT[:, k * T + t0: k * T + t0 + 512] for k in range(8)])
                dst = f1c[:, fch * 512:(fch + 1) * 512]
                bsl = bff1c[:, fch:fch + 1]
                p.op("scalar", lambda e, ps=pA, d=dst, b=bsl: e.activation(
                    out=d, in_=ps[:, 0:512], func=AF.Relu, bias=b),
                    reads=[pA[:, 0:512], bff1c[:]], writes=[dst])
        for k in range(32):
            wb4 = wf2b[k % 2]
            src = aps["L1_wff2"][k * 128:(k + 1) * 128, :]
            p.op("gpsimd", lambda e, d=wb4[:], s=src: e.dma_start(out=d, in_=s),
                 reads=[src], writes=[wb4[:]], dma=True)
            for m4 in range(4):
                for oc in range(2):
                    bank = C.ps8(m4 * 2 + oc)
                    lh = f1c[:, k * 512 + m4 * 128: k * 512 + m4 * 128 + 128]
                    rh = wb4[:, oc * 512:(oc + 1) * 512]
                    C.p.op("tensor", lambda e, o=bank, lh=lh, rh=rh, s=(k == 0):
                           e.matmul(out=o, lhsT=lh, rhs=rh, start=s, stop=False),
                           reads=[lh, rh], writes=[bank])
        for m4 in range(4):
            for oc in range(2):
                bank = C.ps8(m4 * 2 + oc)
                C.p.op("tensor", lambda e, o=bank, rh=bff2row[0:1, oc * 512:(oc + 1) * 512]:
                       e.matmul(out=o, lhsT=C.ones_b[0:1, 0:128], rhs=rh,
                                start=False, stop=True),
                       reads=[C.ones_b[:], bff2row[:]], writes=[bank])
        for m4 in range(4):
            mi = tc * 4 + m4
            tmp = tmpf[m4 % 2]
            for oc in range(2):
                bank = C.ps8(m4 * 2 + oc)
                dsl = tmp[:, oc * 512:(oc + 1) * 512]
                hsl = h_tok[:, mi * D + oc * 512: mi * D + oc * 512 + 512]
                p.op("vector", lambda e, ps=bank, d=dsl, h=hsl: e.tensor_tensor(
                    out=d, in0=ps, in1=h, op=ALU.add),
                    reads=[bank, hsl], writes=[dsl])
            hsl2 = h_tok[:, mi * D:(mi + 1) * D]
            ln_tile(C, phe, tmp[:], hsl2, sq2[m4 % 2][:], stats7, mi,
                    affine=(g2bc[:], b2bc[:]))
    phe.close()
    p.fence()
    if L1_DBG == "h2":
        _dump_flat(C, h_tok[:], x_dst, T * D)
        phH.close(); ph.close()
        return

    phg = Phase(C, "l1g")
    moe = phg.sb("moe", [128, NT * D], BF16)
    p.op("vector", lambda e: e.memset(moe[:], 0.0), writes=[moe[:]])
    disp = phg.sb("disp", [128, NT * NE], F32)
    gcol = phg.sb("gcol", [128, NT], F32)
    phm = Phase(C, "l1m")
    h2T = phm.sb("h2T", [128, ND * T], BF16)
    for mi in range(NT):
        for fb in range(ND):
            src_b = h_tok[:, mi * D + fb * 128: mi * D + (fb + 1) * 128]
            pbt = C.psb()
            pb = pbt[:, 0:128]
            C.p.op("tensor", lambda e, s=src_b, o=pb: e.transpose(
                out=o, in_=s, identity=C.identb[:]),
                reads=[src_b, C.identb[:]], writes=[pb])
            dst = h2T[:, fb * T + mi * 128: fb * T + (mi + 1) * 128]
            p.op("vector", lambda e, s=pb, d=dst: e.tensor_copy(out=d, in_=s),
                 reads=[pb], writes=[dst])
    wgb = phm.sb("wgb", [128, 8 * 8], BF16)
    wload(C, wgb, aps["L1_wg"], D, 8)
    prb = [phm.sb(f"prb{i}", [128, NE], F32) for i in range(2)]
    sfx3 = phm.sb("sfx3", [128, 6], F32)
    for mi in range(NT):
        pA = C.psf()
        mm_accum(C, pA[:, 0:NE],
                 [h2T[:, k * T + mi * 128: k * T + mi * 128 + 128] for k in range(8)],
                 [wgb[:, k * 8:(k + 1) * 8] for k in range(8)])
        pr = prb[mi % 2]
        nmx = sfx3[:, 0:1]
        ssum = sfx3[:, 1:2]
        rsx = sfx3[:, 2:3]
        p.op("vector", lambda e, ps=pA, o=nmx: e.reduce_max(out=o, in_=ps[:, 0:NE],
                                                            axis=AX.X, negate=True),
             reads=[pA[:, 0:NE]], writes=[nmx])
        p.op("scalar", lambda e, ps=pA, d=pr, b=nmx, a=ssum: e.activation(
            out=d[:], in_=ps[:, 0:NE], func=AF.Exp, bias=b, scale=1.0, accum_out=a),
             reads=[pA[:, 0:NE], nmx], writes=[pr[:], ssum])
        p.op("vector", lambda e, o=rsx, i_=ssum: e.reciprocal(out=o, in_=i_),
             reads=[ssum], writes=[rsx])
        p.op("vector", lambda e, d=pr, s=rsx: e.tensor_scalar_mul(out=d[:], in0=d[:],
                                                                  scalar1=s),
             reads=[pr[:], rsx], writes=[pr[:]])
        gsl = gcol[:, mi:mi + 1]
        p.op("vector", lambda e, s=pr, d=gsl: e.reduce_max(out=d, in_=s[:], axis=AX.X),
             reads=[pr[:]], writes=[gsl])
        dsl = disp[:, mi * NE:(mi + 1) * NE]
        p.op("vector", lambda e, s=pr, d=dsl, g=gsl: e.tensor_scalar(
            out=d, in0=s[:], scalar1=g, scalar2=None, op0=ALU.is_ge),
            reads=[pr[:], gsl], writes=[dsl])
    hid = phm.sb("hid", [128, 32 * 1024], BF16)
    we1b = [phm.sb(f"we1b{i}", [128, 8 * 512], BF16) for i in range(2)]
    we2b = [phm.sb(f"we2b{i}", [128, 512], BF16) for i in range(2)]
    be1c = phm.sb("be1c", [128, NE * 32], F32)
    p.op("sync", lambda e: e.dma_start(out=be1c[:], in_=aps["L1_be1c"][:]),
         writes=[be1c[:]], dma=True)
    be2r = phm.sb("be2r", [1, NE * D], BF16)
    p.op("gpsimd", lambda e: e.dma_start(out=be2r[:], in_=aps["L1_be2_row"][:]),
         reads=[aps["L1_be2_row"][:]], writes=[be2r[:]], dma=True)
    eoutf = [phm.sb(f"eoutf{i}", [128, 512], F32) for i in range(2)]
    for e_i in range(NE):
        for tc in range(2):
            t0 = tc * 1024
            for mg in range(8):
                wb5 = we1b[mg % 2]
                wload(C, wb5, aps["L1_we1"], D, 512, row0=e_i * D, col0=mg * 512)
                for m4 in range(4):
                    fch = mg * 4 + m4
                    for nj in range(2):
                        pA = C.psf()
                        mm_accum(C, pA[:, 0:512],
                                 [wb5[:, k * 512 + m4 * 128: k * 512 + m4 * 128 + 128]
                                  for k in range(8)],
                                 [h2T[:, k * T + t0 + nj * 512: k * T + t0 + (nj + 1) * 512]
                                  for k in range(8)])
                        dst = hid[:, fch * 1024 + nj * 512: fch * 1024 + (nj + 1) * 512]
                        bsl = be1c[:, e_i * 32 + fch: e_i * 32 + fch + 1]
                        p.op("scalar", lambda e, ps=pA, d=dst, b=bsl: e.activation(
                            out=d, in_=ps[:, 0:512], func=AF.Relu, bias=b),
                            reads=[pA[:, 0:512], be1c[:]], writes=[dst])
            for oh in range(2):
                for k in range(32):
                    wb6 = we2b[k % 2]
                    src = aps["L1_we2"][e_i * FF + k * 128: e_i * FF + (k + 1) * 128,
                                        oh * 512:(oh + 1) * 512]
                    p.op("gpsimd", lambda e, d=wb6[:], s=src: e.dma_start(out=d, in_=s),
                         reads=[src], writes=[wb6[:]], dma=True)
                    for m8 in range(8):
                        bank = C.ps8(m8)
                        lh = hid[:, k * 1024 + m8 * 128: k * 1024 + m8 * 128 + 128]
                        C.p.op("tensor", lambda e, o=bank, lh=lh, rh=wb6[:],
                               s=(k == 0): e.matmul(out=o, lhsT=lh, rhs=rh,
                                                    start=s, stop=False),
                               reads=[lh, wb6[:]], writes=[bank])
                for m8 in range(8):
                    bank = C.ps8(m8)
                    rh = be2r[0:1, e_i * D + oh * 512: e_i * D + (oh + 1) * 512]
                    C.p.op("tensor", lambda e, o=bank, rh=rh: e.matmul(
                        out=o, lhsT=C.ones_b[0:1, 0:128], rhs=rh, start=False, stop=True),
                        reads=[C.ones_b[:], be2r[:]], writes=[bank])
                for m8 in range(8):
                    mi = tc * 8 + m8
                    bank = C.ps8(m8)
                    ef = eoutf[m8 % 2]
                    p.op("vector", lambda e, ps=bank, d=ef: e.tensor_copy(out=d[:], in_=ps),
                         reads=[bank], writes=[ef[:]])
                    msl = moe[:, mi * D + oh * 512: mi * D + (oh + 1) * 512]
                    dcol = disp[:, mi * NE + e_i: mi * NE + e_i + 1]
                    p.op("vector", lambda e, s=ef, d=msl, dc=dcol: e.scalar_tensor_tensor(
                        out=d, in0=s[:], scalar=dc, in1=d, op0=ALU.mult, op1=ALU.add),
                        reads=[ef[:], disp[:], msl], writes=[msl])
    phm.close()
    p.fence()
    if L1_DBG == "moe":
        _dump_flat(C, moe[:], x_dst, T * D)
        phg.close(); phH.close(); ph.close()
        return

    phz = Phase(C, "l1z")
    g3bc = make_bcast(C, phz, "g3", aps["L1_ln3g_row"][:])
    b3bc = make_bcast(C, phz, "b3", aps["L1_ln3b_row"][:])
    woutb = phz.sb("woutb", [128, 8 * 1024], BF16)
    wload(C, woutb, aps["L1_wout"], D, 1024)
    h3t = [phz.sb(f"h3t{i}", [128, ND * 128], BF16) for i in range(2)]
    tmp3 = [phz.sb(f"tmp3{i}", [128, D], F32) for i in range(2)]
    h3f = [phz.sb(f"h3f{i}", [128, D], F32) for i in range(2)]
    sq3 = [phz.sb(f"sq3{i}", [128, D], F32) for i in range(2)]
    xupd = [phz.sb(f"xupd{i}", [128, D], F32) for i in range(2)]
    sld = [phz.sb(f"sld{i}", [128, D], BF16) for i in range(2)]
    stats8 = phz.sb("stats8", [128, 4 * NT], F32)
    for mi in range(NT):
        tmp = tmp3[mi % 2]
        msl = moe[:, mi * D:(mi + 1) * D]
        h2sl = h_tok[:, mi * D:(mi + 1) * D]
        gsl = gcol[:, mi:mi + 1]
        p.op("vector", lambda e, m=msl, h=h2sl, g=gsl, d=tmp: e.scalar_tensor_tensor(
            out=d[:], in0=m, scalar=g, in1=h, op0=ALU.mult, op1=ALU.add),
            reads=[msl, gcol[:], h2sl], writes=[tmp[:]])
        h3 = h3f[mi % 2]
        ln_tile(C, phz, tmp[:], h3[:], sq3[mi % 2][:], stats8, mi, affine=(g3bc[:], b3bc[:]))
        h3T = h3t[mi % 2]
        for g4 in range(2):
            transpose_batch(C, [h3[:, (g4 * 4 + i2) * 128:(g4 * 4 + i2 + 1) * 128]
                                for i2 in range(4)],
                            h3T[:, g4 * 512:(g4 + 1) * 512], F32)
        xs = xupd[mi % 2]
        src = x_src[mi * 128:(mi + 1) * 128, :]
        p.op("sync", lambda e, s=xs, r=src: e.dma_start(out=s[:], in_=r),
             reads=[src], writes=[xs[:]], dma=True)
        sl = sld[mi % 2]
        ssrc = s_scr[mi * 128:(mi + 1) * 128, :]
        p.op("sync", lambda e, s=sl, r=ssrc: e.dma_start(out=s[:], in_=r),
             reads=[ssrc], writes=[sl[:]], dma=True)
        for nj in range(2):
            pA = C.psf()
            mm_accum(C, pA[:, 0:512],
                     [h3T[:, k * 128:(k + 1) * 128] for k in range(8)],
                     [woutb[:, k * 1024 + nj * 512: k * 1024 + (nj + 1) * 512]
                      for k in range(8)])
            gsl2 = tmp[:, nj * 512:(nj + 1) * 512]
            p.op("scalar", lambda e, ps=pA, d=gsl2: e.activation(
                out=d, in_=ps[:, 0:512], func=AF.Silu),
                reads=[pA[:, 0:512]], writes=[gsl2])
        if L1_DBG == "gout":
            gdst = ctx2d[0:128, mi * D:(mi + 1) * D]
            p.op("gpsimd", lambda e, s=tmp, d=gdst: e.dma_start(out=d, in_=s[:]),
                 reads=[tmp[:]], writes=[gdst], dma=True)
        if L1_DBG == "sin":
            gdst = ctx2d[0:128, mi * D:(mi + 1) * D]
            p.op("gpsimd", lambda e, s=sl, d=gdst: e.dma_start(out=d, in_=s[:]),
                 reads=[sl[:]], writes=[gdst], dma=True)
        p.op("vector", lambda e, s=sl, g=tmp: e.tensor_tensor(out=g[:], in0=g[:], in1=s[:],
                                                              op=ALU.mult),
             reads=[sl[:], tmp[:]], writes=[tmp[:]])
        p.op("vector", lambda e, g=tmp, x=xs: e.tensor_tensor(out=x[:], in0=x[:], in1=g[:],
                                                              op=ALU.add),
             reads=[tmp[:], xs[:]], writes=[xs[:]])
        dst = x_dst[mi * 128:(mi + 1) * 128, :]
        p.op("sync", lambda e, s=xs, d=dst: e.dma_start(out=d, in_=s[:]),
             reads=[xs[:]], writes=[dst], dma=True)
    if L1_DBG in ("gout", "sin"):
        _dump_flat(C, aps["ctx_scr"][0:T * D], x_dst, T * D)
    phz.close()
    phg.close()
    phH.close()
    ph.close()


def aps_pitch_write_view(shd, c0):
    return shd[0:128 * (T + 1)].rearrange("(r c) -> r c", c=T + 1)[:, 1 + c0 * 512: 1 + T]


def quant_tile(C, src_f32, dst_bf16, W, tq):
    p = C.p
    nb = W // 16
    a = tq["a"][:, 0:W]
    q = tq["q"][:, 0:W]
    sg = tq["sg"][:, 0:W]
    bm = tq["bm"][:, 0:nb]
    iv = tq["iv"][:, 0:nb]
    p.op("scalar", lambda e, s=src_f32, o=a: e.activation(out=o, in_=s, func=AF.Abs),
         reads=[src_f32], writes=[a])
    cur = a
    width = W
    for step in range(4):
        half = width // 2
        nxt = tq["b1"][:, 0:half]
        cv = cur.rearrange("p (n two) -> p n two", two=2)
        p.op("vector", lambda e, i0=cv[:, :, 0:1].squeeze(), i1=cv[:, :, 1:2].squeeze(),
             o=nxt: e.tensor_tensor(out=o, in0=i0, in1=i1, op=ALU.max),
            reads=[tq["a"][:, 0:W], tq["b1"][:, 0:W // 2]], writes=[nxt])
        cur = nxt
        width = half
    p.op("vector", lambda e, i_=cur, o=bm: e.tensor_scalar(
        out=o, in0=i_, scalar1=1.0 / 6.0, scalar2=1e-12, op0=ALU.mult, op1=ALU.max),
        reads=[cur], writes=[bm])
    p.op("vector", lambda e, o=iv, i_=bm: e.reciprocal(out=o, in_=i_),
         reads=[bm], writes=[iv])
    ivb = iv.unsqueeze(2).broadcast_to((128, nb, 16))
    av = a.rearrange("p (n k) -> p n k", k=16)
    p.op("vector", lambda e, o=av, i0=av, i1=ivb: e.tensor_tensor(out=o, in0=i0, in1=i1,
                                                                  op=ALU.mult),
         reads=[a, iv], writes=[a])
    p.op("vector", lambda e, i_=a, o=q: e.tensor_scalar(
        out=o, in0=i_, scalar1=0.25, scalar2=0.5, op0=ALU.is_gt, op1=ALU.mult),
        reads=[a], writes=[q])
    for thr, inc in ((0.75, 0.5), (1.25, 0.5), (1.75, 0.5), (2.5, 1.0), (3.5, 1.0),
                     (5.0, 2.0)):
        p.op("vector", lambda e, i_=a, o=sg, t_=thr, ic=inc: e.tensor_scalar(
            out=o, in0=i_, scalar1=t_, scalar2=ic, op0=ALU.is_gt, op1=ALU.mult),
            reads=[a], writes=[sg])
        p.op("vector", lambda e, o=q, i1=sg: e.tensor_tensor(out=o, in0=o, in1=i1,
                                                             op=ALU.add),
             reads=[q, sg], writes=[q])
    p.op("vector", lambda e, s=src_f32, o=sg: e.tensor_scalar(
        out=o, in0=s, scalar1=0.0, scalar2=2.0, op0=ALU.is_gt, op1=ALU.mult),
        reads=[src_f32], writes=[sg])
    p.op("vector", lambda e, o=sg: e.tensor_scalar(
        out=o, in0=o, scalar1=-1.0, scalar2=None, op0=ALU.add),
        reads=[sg], writes=[sg])
    p.op("vector", lambda e, o=q, i1=sg: e.tensor_tensor(out=o, in0=o, in1=i1, op=ALU.mult),
         reads=[q, sg], writes=[q])
    qv = q.rearrange("p (n k) -> p n k", k=16)
    bmb = bm.unsqueeze(2).broadcast_to((128, nb, 16))
    dv = dst_bf16.rearrange("p (n k) -> p n k", k=16)
    p.op("vector", lambda e, o=dv, i0=qv, i1=bmb: e.tensor_tensor(out=o, in0=i0, in1=i1,
                                                                  op=ALU.mult),
         reads=[q, bm], writes=[dst_bf16])


def quant_wide(C, src_ap, dst_ap, W, tq):
    for c0 in range(0, W, 512):
        w = min(512, W - c0)
        quant_tile(C, src_ap[:, c0:c0 + w], dst_ap[:, c0:c0 + w], w, tq)


def build_layer2(C, x_src, x_dst):
    p = C.p
    aps = C.aps
    II = 2816
    ph = Phase(C, "l2")
    s_tok = ph.sb("s_tok", [128, NT * D], BF16)
    phx = Phase(C, "l2x")
    xnT = phx.sb("xnT", [128, ND * T], BF16)
    phpre = Phase(C, "l2pre")
    make_xn(C, x_src, phpre, xnT)
    phpre.close()

    pha = Phase(C, "l2a")
    qT = pha.sb("qT", [128, ND * T], BF16)
    k_tok = pha.sb("k_tok", [128, NT * D], BF16)
    v_tok = pha.sb("v_tok", [128, NT * D], BF16)
    wbuf = pha.sb("wbuf", [128, 8 * 512], BF16)
    bqc = pha.sb("bqc", [128, ND], F32)
    p.op("sync", lambda e: e.dma_start(out=bqc[:], in_=aps["L2_bqc"][:]),
         writes=[bqc[:]], dma=True)
    brow = pha.sb("brow", [1, 3 * D], BF16)
    for i, nm in enumerate(["L2_bk_row", "L2_bv_row", "L2_bo_row"]):
        p.op("gpsimd", lambda e, i=i, nm=nm: e.dma_start(
            out=brow[0:1, i * D:(i + 1) * D], in_=aps[nm][:]),
            reads=[aps[nm][:]], writes=[brow[0:1, i * D:(i + 1) * D]], dma=True)
    dec_c = pha.sb("dec_c", [128, NT], F32)
    eps_c = pha.sb("eps_c", [128, NT], F32)
    p.op("sync", lambda e: e.dma_start(out=dec_c[:], in_=aps["L2_dec"][:]),
         writes=[dec_c[:]], dma=True)
    p.op("sync", lambda e: e.dma_start(out=eps_c[:], in_=aps["L2_epsdec"][:]),
         writes=[eps_c[:]], dma=True)
    onec = pha.sb("onec", [128, 1], BF16)
    p.op("vector", lambda e: e.memset(onec[:], 1.0), writes=[onec[:]])
    for half in range(2):
        wload(C, wbuf, aps["L2_wqT"], D, 512, col0=half * 512)
        for m4 in range(4):
            m = half * 4 + m4
            for nj in range(4):
                pA = C.psf()
                mm_accum(C, pA[:, 0:512],
                         [wbuf[:, k * 512 + m4 * 128: k * 512 + m4 * 128 + 128]
                          for k in range(8)],
                         [xnT[:, k * T + nj * 512: k * T + (nj + 1) * 512]
                          for k in range(8)])
                dst = qT[:, m * T + nj * 512: m * T + (nj + 1) * 512]
                p.op("scalar", lambda e, ps=pA, d=dst, b=bqc[:, m:m + 1]: e.activation(
                    out=d, in_=ps[:, 0:512], func=AF.Relu, bias=b),
                    reads=[pA[:, 0:512], bqc[:]], writes=[dst])
    for wi, (wname, b0, isk) in enumerate([("L2_wkT", 0, True), ("L2_wvT", D, False)]):
        for nj in range(2):
            wload(C, wbuf, aps[wname], D, 512, col0=nj * 512)
            for mi in range(NT):
                pA = C.psf()
                lhs = [xnT[:, k * T + mi * 128: k * T + mi * 128 + 128] for k in range(8)]
                rhs = [wbuf[:, k * 512:(k + 1) * 512] for k in range(8)]
                lhs.append(C.ones_b[0:1, 0:128])
                rhs.append(brow[0:1, b0 + nj * 512: b0 + (nj + 1) * 512])
                mm_accum(C, pA[:, 0:512], lhs, rhs)
                dstt = (k_tok if isk else v_tok)[:, mi * D + nj * 512:
                                                 mi * D + (nj + 1) * 512]
                if isk:
                    p.op("vector", lambda e, ps=pA, d=dstt, dc=dec_c[:, mi:mi + 1]:
                         e.tensor_scalar(out=d, in0=ps[:, 0:512], scalar1=0.0, scalar2=dc,
                                         op0=ALU.max, op1=ALU.mult),
                         reads=[pA[:, 0:512], dec_c[:]], writes=[dstt])
                else:
                    p.op("vector", lambda e, ps=pA, d=dstt: e.tensor_copy(
                        out=d, in_=ps[:, 0:512]),
                        reads=[pA[:, 0:512]], writes=[dstt])
    bdr = pha.sb("bdr", [128, 8 * 130], BF16)
    dnt = pha.sb("dnt", [128, 4], F32)
    p.op("vector", lambda e: e.memset(bdr[:], 0.0), writes=[bdr[:]])
    for hg in range(8):
        pk = C.psf()
        for kb in range(NT):
            lh = k_tok[:, kb * D + hg * 128: kb * D + (hg + 1) * 128]
            rh = v_tok[:, kb * D + hg * 128: kb * D + (hg + 1) * 128]
            C.p.op("tensor", lambda e, o=pk[:, 0:128], lh=lh, rh=rh, s=(kb == 0),
                   t=(kb == NT - 1): e.matmul(out=o, lhsT=lh, rhs=rh, start=s, stop=t),
                   reads=[lh, rh], writes=[pk[:, 0:128]])
            C.p.op("tensor", lambda e, o=pk[:, 128:129], lh=lh, s=(kb == 0),
                   t=(kb == NT - 1): e.matmul(out=o, lhsT=lh, rhs=onec[:],
                                              start=s, stop=t),
                   reads=[lh, onec[:]], writes=[pk[:, 128:129]])
        for j2 in range(2):
            src1 = pk[j2 * 64:(j2 + 1) * 64, j2 * 64:(j2 + 1) * 64]
            dst1 = bdr[j2 * 64:(j2 + 1) * 64, hg * 130 + j2 * 65:hg * 130 + j2 * 65 + 64]
            p.op("vector", lambda e, s=src1, d=dst1: e.tensor_copy(out=d, in_=s),
                 reads=[pk[:, 0:129]], writes=[dst1])
            src2 = pk[j2 * 64:(j2 + 1) * 64, 128:129]
            dst2 = bdr[j2 * 64:(j2 + 1) * 64,
                       hg * 130 + j2 * 65 + 64:hg * 130 + j2 * 65 + 65]
            p.op("vector", lambda e, s=src2, d=dst2: e.tensor_copy(out=d, in_=s),
                 reads=[pk[:, 0:129]], writes=[dst2])
    ao_tok = k_tok
    for hg in range(8):
        for mi in range(NT):
            po = C.psf()
            lh = qT[:, hg * T + mi * 128: hg * T + (mi + 1) * 128]
            C.p.op("tensor", lambda e, o=po[:, 0:130], lh=lh,
                   rh=bdr[:, hg * 130:(hg + 1) * 130]: e.matmul(
                out=o, lhsT=lh, rhs=rh, start=True, stop=True),
                reads=[lh, bdr[:]], writes=[po[:, 0:130]])
            for j2 in range(2):
                h = hg * 2 + j2
                dn = dnt[:, j2:j2 + 1]
                p.op("vector", lambda e, ps=po, o=dn, ec=eps_c[:, mi:mi + 1], j2=j2:
                     e.tensor_tensor(out=o, in0=ps[:, j2 * 65 + 64:j2 * 65 + 65],
                                     in1=ec, op=ALU.add),
                     reads=[po[:, 0:130], eps_c[:]], writes=[dn])
                dn2 = dnt[:, 2 + j2:3 + j2]
                p.op("vector", lambda e, o=dn2, i_=dn: e.reciprocal(out=o, in_=i_),
                     reads=[dn], writes=[dn2])
                dst = ao_tok[:, mi * D + h * 64: mi * D + (h + 1) * 64]
                p.op("vector", lambda e, ps=po, d=dst, r=dn2, j2=j2: e.tensor_scalar(
                    out=d, in0=ps[:, j2 * 65:j2 * 65 + 64], scalar1=r, scalar2=None,
                    op0=ALU.mult),
                    reads=[po[:, 0:130], dn2], writes=[dst])
    aoT = qT
    for mi in range(NT):
        for fb in range(ND):
            src_b = ao_tok[:, mi * D + fb * 128: mi * D + (fb + 1) * 128]
            dst_b = aoT[:, fb * T + mi * 128: fb * T + (mi + 1) * 128]
            transpose128(C, pha, src_b, dst_b, BF16)
    for nj in range(2):
        wload(C, wbuf, aps["L2_woT"], D, 512, col0=nj * 512)
        for mi in range(NT):
            pA = C.psf()
            lhs = [aoT[:, k * T + mi * 128: k * T + mi * 128 + 128] for k in range(8)]
            rhs = [wbuf[:, k * 512:(k + 1) * 512] for k in range(8)]
            lhs.append(C.ones_b[0:1, 0:128])
            rhs.append(brow[0:1, 2 * D + nj * 512: 2 * D + (nj + 1) * 512])
            mm_accum(C, pA[:, 0:512], lhs, rhs)
            dst = s_tok[:, mi * D + nj * 512: mi * D + (nj + 1) * 512]
            p.op("vector", lambda e, ps=pA, d=dst: e.tensor_copy(out=d, in_=ps[:, 0:512]),
                 reads=[pA[:, 0:512]], writes=[dst])
    pha.close()
    phx.close()
    g_tok = ph.sb("g_tok", [128, NT * D], BF16)

    wg2d = aps["l2wg_q"].rearrange("(r c) -> r c", c=II)
    wu2d = aps["l2wu_q"].rearrange("(r c) -> r c", c=II)
    wd2d = aps["l2wd_q"].rearrange("(r c) -> r c", c=1024)
    phw = Phase(C, "l2w")
    tq = {"a": phw.sb("qa", [128, 512], F32), "b1": phw.sb("qb1", [128, 256], F32),
          "q": phw.sb("qq", [128, 512], BF16), "sg": phw.sb("qsg", [128, 512], BF16),
          "bm": phw.sb("qbm", [128, 32], F32),
          "iv": phw.sb("qiv", [128, 32], F32)}
    phw1 = Phase(C, "l2w1")
    wst = [phw1.sb(f"wst{i}", [128, II], F32) for i in range(2)]
    wqb = [phw1.sb(f"wqb{i}", [128, II], BF16) for i in range(2)]
    wtt = [phw1.sb(f"wtt{i}", [128, 22 * 128], BF16) for i in range(2)]
    for wi, (wname, scr2d, rows, cols) in enumerate(
            [("L2_wgate", wg2d, II, 1024), ("L2_wup", wu2d, II, 1024),
             ("L2_wdown", wd2d, 1024, II)]):
        nrc = rows // 128
        nfb = cols // 128
        for rg in range(nrc):
            st = wst[rg % 2]
            qb = wqb[rg % 2]
            tt = wtt[rg % 2]
            src = aps[wname][rg * 128:(rg + 1) * 128, :]
            p.op("sync", lambda e, d=st[:, 0:cols], s=src: e.dma_start(out=d, in_=s),
                 reads=[src], writes=[st[:, 0:cols]], dma=True)
            quant_wide(C, st[:, 0:cols], qb[:, 0:cols], cols, tq)
            for fb in range(nfb):
                pbt = C.psb()
                pb = pbt[:, 0:128]
                C.p.op("tensor", lambda e, s=qb[:, fb * 128:(fb + 1) * 128], o=pb:
                       e.transpose(out=o, in_=s, identity=C.identb[:]),
                       reads=[qb[:, 0:cols], C.identb[:]], writes=[pb])
                p.op("vector", lambda e, s=pb, d=tt[:, fb * 128:(fb + 1) * 128]:
                     e.tensor_copy(out=d, in_=s),
                     reads=[pb], writes=[tt[:, fb * 128:(fb + 1) * 128]])
            dstv = scr2d[0:cols, rg * 128:(rg + 1) * 128].rearrange(
                "(fb p) j -> p fb j", p=128)
            ttv = tt[:, 0:nfb * 128].rearrange("p (fb j) -> p fb j", j=128)
            p.op("sync", lambda e, d=dstv, s=ttv: e.dma_start(out=d, in_=s),
                 reads=[tt[:, 0:nfb * 128]], writes=[dstv], dma=True)
    phw1.close()
    xq_tok = phw.sb("xq_tok", [128, NT * D], BF16)
    xqT = phw.sb("xqT", [128, ND * T], BF16)
    phxq = Phase(C, "l2xq")
    xst = [phxq.sb(f"xst{i}", [128, D], F32) for i in range(2)]
    sqt = [phxq.sb(f"sqt{i}", [128, D], F32) for i in range(2)]
    xnt = [phxq.sb(f"xnt{i}", [128, D], F32) for i in range(2)]
    stats9 = phxq.sb("stats9", [128, 4 * NT], F32)
    for mi in range(NT):
        xs = xst[mi % 2]
        src = x_src[mi * 128:(mi + 1) * 128, :]
        p.op("sync", lambda e, s=xs, r=src: e.dma_start(out=s[:], in_=r),
             reads=[src], writes=[xs[:]], dma=True)
        xn = xnt[mi % 2]
        ln_tile(C, phxq, xs[:], xn[:], sqt[mi % 2][:], stats9, mi)
        dstq = xq_tok[:, mi * D:(mi + 1) * D]
        quant_wide(C, xn[:], dstq, D, tq)
        for fb in range(ND):
            src_b = xq_tok[:, mi * D + fb * 128: mi * D + (fb + 1) * 128]
            dst_b = xqT[:, fb * T + mi * 128: fb * T + (mi + 1) * 128]
            transpose128(C, phxq, src_b, dst_b, BF16)
    phxq.close()
    hch = phw.sb("hch", [128, 4 * II], BF16)
    wsb = [phw.sb(f"wsb{i}", [128, 8 * 512], BF16) for i in range(2)]
    wdb = [phw.sb(f"wdb{i}", [128, 1024], BF16) for i in range(2)]
    t1b = [phw.sb(f"t1b{i}", [128, 512], F32) for i in range(2)]
    t2b = [phw.sb(f"t2b{i}", [128, 512], F32) for i in range(2)]
    hqt = phw.sb("hqt", [128, 4 * 128], BF16)
    NOG = [(0, 512), (512, 1024), (1024, 1536), (1536, 2048), (2048, 2560), (2560, 2816)]
    for tc in range(4):
        t0 = tc * 512
        for (o0, o1) in NOG:
            ow = o1 - o0
            for gi, scr in enumerate((wg2d, wu2d)):
                wsl = wsb[gi]
                srcw = scr[:, o0:o1].rearrange("(r p) c -> p r c", p=128)
                wv = wsl[:, 0:8 * ow].rearrange("p (r c) -> p r c", c=ow)
                p.op("gpsimd", lambda e, d=wv, s=srcw: e.dma_start(out=d, in_=s),
                     reads=[scr[:, o0:o1]], writes=[wsl[:, 0:8 * ow]], dma=True)
            for m4 in range(4):
                pG = C.psf()
                pU = C.psf()
                for gi, psx in ((0, pG), (1, pU)):
                    mm_accum(C, psx[:, 0:ow],
                             [xqT[:, k * T + t0 + m4 * 128: k * T + t0 + m4 * 128 + 128]
                              for k in range(8)],
                             [wsb[gi][:, k * ow:(k + 1) * ow] for k in range(8)])
                t1 = t1b[m4 % 2]
                t2 = t2b[m4 % 2]
                p.op("scalar", lambda e, ps=pG, o=t1[:, 0:ow], ow=ow: e.activation(
                    out=o, in_=ps[:, 0:ow], func=AF.Silu),
                    reads=[pG[:, 0:512]], writes=[t1[:, 0:512]])
                p.op("vector", lambda e, ps=pU, o=t2[:, 0:ow], ow=ow: e.tensor_copy(
                    out=o, in_=ps[:, 0:ow]),
                    reads=[pU[:, 0:512]], writes=[t2[:, 0:512]])
                dsth = hch[:, m4 * II + o0: m4 * II + o1]
                p.op("vector", lambda e, d=dsth, i0=t1[:, 0:ow], i1=t2[:, 0:ow]:
                     e.tensor_tensor(out=d, in0=i0, in1=i1, op=ALU.mult),
                     reads=[t1[:, 0:512], t2[:, 0:512]], writes=[dsth])
        for m4 in range(4):
            hsl = hch[:, m4 * II:(m4 + 1) * II]
            quant_wide(C, hsl, hsl, II, tq)
        nkc = II // 128
        for kc in range(nkc):
            wdl = wdb[kc % 2]
            srcw = wd2d[kc * 128:(kc + 1) * 128, :]
            p.op("gpsimd", lambda e, d=wdl[:], s=srcw: e.dma_start(out=d, in_=s),
                 reads=[srcw], writes=[wdl[:]], dma=True)
            for m4 in range(4):
                src_b = hch[:, m4 * II + kc * 128: m4 * II + (kc + 1) * 128]
                dst_b = hqt[:, m4 * 128:(m4 + 1) * 128]
                transpose128(C, phw, src_b, dst_b, BF16)
            for m4 in range(4):
                lh = hqt[:, m4 * 128:(m4 + 1) * 128]
                for oc in range(2):
                    bank = C.ps8(m4 * 2 + oc)
                    rh = wdl[:, oc * 512:(oc + 1) * 512]
                    C.p.op("tensor", lambda e, o=bank, lh=lh, rh=rh, s=(kc == 0),
                           t=(kc == nkc - 1): e.matmul(out=o, lhsT=lh, rhs=rh,
                                                       start=s, stop=t),
                           reads=[lh, rh], writes=[bank])
        for m4 in range(4):
            mi = tc * 4 + m4
            for oc in range(2):
                bank = C.ps8(m4 * 2 + oc)
                dst = g_tok[:, mi * D + oc * 512: mi * D + (oc + 1) * 512]
                p.op("scalar", lambda e, ps=bank, d=dst: e.activation(
                    out=d, in_=ps, func=AF.Silu),
                    reads=[bank], writes=[dst])
    phw.close()

    phu = Phase(C, "l2u")
    xst2 = [phu.sb(f"xu{i}", [128, D], F32) for i in range(2)]
    tmps = [phu.sb(f"tm{i}", [128, D], F32) for i in range(2)]
    for mi in range(NT):
        xs = xst2[mi % 2]
        tmp = tmps[mi % 2]
        src = x_src[mi * 128:(mi + 1) * 128, :]
        p.op("sync", lambda e, s=xs, r=src: e.dma_start(out=s[:], in_=r),
             reads=[src], writes=[xs[:]], dma=True)
        ssl = s_tok[:, mi * D:(mi + 1) * D]
        gsl = g_tok[:, mi * D:(mi + 1) * D]
        p.op("vector", lambda e, d=tmp, i0=ssl, i1=gsl: e.tensor_tensor(
            out=d[:], in0=i0, in1=i1, op=ALU.mult),
            reads=[ssl, gsl], writes=[tmp[:]])
        p.op("vector", lambda e, x=xs, t_=tmp: e.tensor_tensor(
            out=x[:], in0=x[:], in1=t_[:], op=ALU.add),
            reads=[tmp[:], xs[:]], writes=[xs[:]])
        dst = x_dst[mi * 128:(mi + 1) * 128, :]
        p.op("sync", lambda e, s=xs, d=dst: e.dma_start(out=d, in_=s[:]),
             reads=[xs[:]], writes=[dst], dma=True)
    phu.close()
    ph.close()


def build_layer3(C, x_src, x_dst):
    p = C.p
    aps = C.aps
    NHA, DA = 32, 32
    NEx, DH = 16, 64
    ph = Phase(C, "l3")
    s_tok = ph.sb("s_tok", [128, NT * D], BF16)
    phx = Phase(C, "l3x")
    xnT = phx.sb("xnT", [128, ND * T], BF16)
    phpre = Phase(C, "l3pre")
    make_xn(C, x_src, phpre, xnT)
    phpre.close()

    pha = Phase(C, "l3a")
    qT = pha.sb("qT", [128, ND * T], BF16)
    k_tok = pha.sb("k_tok", [128, NT * D], BF16)
    v_tok = pha.sb("v_tok", [128, NT * D], BF16)
    wbuf = pha.sb("wbuf", [128, 8 * 512], BF16)
    onec = pha.sb("onec", [128, 1], BF16)
    p.op("vector", lambda e: e.memset(onec[:], 1.0), writes=[onec[:]])
    pbrow = pha.sb("pbrow", [1, D], BF16)
    p.op("gpsimd", lambda e: e.dma_start(out=pbrow[:], in_=aps["L3_pb_row"][:]),
         reads=[aps["L3_pb_row"][:]], writes=[pbrow[:]], dma=True)
    for half in range(2):
        wload(C, wbuf, aps["L3_qkvT"], D, 512, col0=half * 512)
        for m4 in range(4):
            m = half * 4 + m4
            for nj in range(4):
                pA = C.psf()
                mm_accum(C, pA[:, 0:512],
                         [wbuf[:, k * 512 + m4 * 128: k * 512 + m4 * 128 + 128]
                          for k in range(8)],
                         [xnT[:, k * T + nj * 512: k * T + (nj + 1) * 512]
                          for k in range(8)])
                dst = qT[:, m * T + nj * 512: m * T + (nj + 1) * 512]
                p.op("scalar", lambda e, ps=pA, d=dst: e.activation(
                    out=d, in_=ps[:, 0:512], func=AF.Relu),
                    reads=[pA[:, 0:512]], writes=[dst])
    for wi, (c0, dstt) in enumerate([(1024, k_tok), (2048, v_tok)]):
        for nj in range(2):
            wload(C, wbuf, aps["L3_qkvT"], D, 512, col0=c0 + nj * 512)
            for mi in range(NT):
                pA = C.psf()
                mm_accum(C, pA[:, 0:512],
                         [xnT[:, k * T + mi * 128: k * T + mi * 128 + 128]
                          for k in range(8)],
                         [wbuf[:, k * 512:(k + 1) * 512] for k in range(8)])
                dst = dstt[:, mi * D + nj * 512: mi * D + (nj + 1) * 512]
                p.op("scalar", lambda e, ps=pA, d=dst: e.activation(
                    out=d, in_=ps[:, 0:512], func=AF.Relu),
                    reads=[pA[:, 0:512]], writes=[dst])
    bdr = pha.sb("bdr", [128, 8 * 132], BF16)
    dnt = pha.sb("dnt", [128, 4], F32)
    p.op("vector", lambda e: e.memset(bdr[:], 0.0), writes=[bdr[:]])
    for hg in range(8):
        pk = C.psf()
        for kb in range(NT):
            lh = k_tok[:, kb * D + hg * 128: kb * D + (hg + 1) * 128]
            rh = v_tok[:, kb * D + hg * 128: kb * D + (hg + 1) * 128]
            C.p.op("tensor", lambda e, o=pk[:, 0:128], lh=lh, rh=rh, s=(kb == 0),
                   t=(kb == NT - 1): e.matmul(out=o, lhsT=lh, rhs=rh, start=s, stop=t),
                   reads=[lh, rh], writes=[pk[:, 0:128]])
            C.p.op("tensor", lambda e, o=pk[:, 128:129], lh=lh, s=(kb == 0),
                   t=(kb == NT - 1): e.matmul(out=o, lhsT=lh, rhs=onec[:],
                                              start=s, stop=t),
                   reads=[lh, onec[:]], writes=[pk[:, 128:129]])
        for j4 in range(4):
            src1 = pk[j4 * 32:(j4 + 1) * 32, j4 * 32:(j4 + 1) * 32]
            dst1 = bdr[j4 * 32:(j4 + 1) * 32, hg * 132 + j4 * 33:hg * 132 + j4 * 33 + 32]
            p.op("vector", lambda e, s=src1, d=dst1: e.tensor_copy(out=d, in_=s),
                 reads=[pk[:, 0:129]], writes=[dst1])
            src2 = pk[j4 * 32:(j4 + 1) * 32, 128:129]
            dst2 = bdr[j4 * 32:(j4 + 1) * 32,
                       hg * 132 + j4 * 33 + 32:hg * 132 + j4 * 33 + 33]
            p.op("vector", lambda e, s=src2, d=dst2: e.tensor_copy(out=d, in_=s),
                 reads=[pk[:, 0:129]], writes=[dst2])
    ao_tok = k_tok
    for hg in range(8):
        for mi in range(NT):
            po = C.psf()
            lh = qT[:, hg * T + mi * 128: hg * T + (mi + 1) * 128]
            C.p.op("tensor", lambda e, o=po[:, 0:132], lh=lh,
                   rh=bdr[:, hg * 132:(hg + 1) * 132]: e.matmul(
                out=o, lhsT=lh, rhs=rh, start=True, stop=True),
                reads=[lh, bdr[:]], writes=[po[:, 0:132]])
            for j4 in range(4):
                h = hg * 4 + j4
                dn = dnt[:, 0:1]
                p.op("vector", lambda e, ps=po, o=dn, j4=j4: e.tensor_scalar(
                    out=o, in0=ps[:, j4 * 33 + 32:j4 * 33 + 33], scalar1=1e-15,
                    scalar2=None, op0=ALU.add),
                    reads=[po[:, 0:132]], writes=[dn])
                dn2 = dnt[:, 1:2]
                p.op("vector", lambda e, o=dn2, i_=dn: e.reciprocal(out=o, in_=i_),
                     reads=[dn], writes=[dn2])
                dst = ao_tok[:, mi * D + h * 32: mi * D + (h + 1) * 32]
                p.op("vector", lambda e, ps=po, d=dst, r=dn2, j4=j4: e.tensor_scalar(
                    out=d, in0=ps[:, j4 * 33:j4 * 33 + 32], scalar1=r, scalar2=None,
                    op0=ALU.mult),
                    reads=[po[:, 0:132], dn2], writes=[dst])
    aoT = qT
    for mi in range(NT):
        for fb in range(ND):
            src_b = ao_tok[:, mi * D + fb * 128: mi * D + (fb + 1) * 128]
            dst_b = aoT[:, fb * T + mi * 128: fb * T + (mi + 1) * 128]
            transpose128(C, pha, src_b, dst_b, BF16)
    for nj in range(2):
        wload(C, wbuf, aps["L3_projT"], D, 512, col0=nj * 512)
        for mi in range(NT):
            pA = C.psf()
            lhs = [aoT[:, k * T + mi * 128: k * T + mi * 128 + 128] for k in range(8)]
            rhs = [wbuf[:, k * 512:(k + 1) * 512] for k in range(8)]
            lhs.append(C.ones_b[0:1, 0:128])
            rhs.append(pbrow[0:1, nj * 512:(nj + 1) * 512])
            mm_accum(C, pA[:, 0:512], lhs, rhs)
            dst = s_tok[:, mi * D + nj * 512: mi * D + (nj + 1) * 512]
            p.op("vector", lambda e, ps=pA, d=dst: e.tensor_copy(out=d, in_=ps[:, 0:512]),
                 reads=[pA[:, 0:512]], writes=[dst])
    pha.close()

    phg = Phase(C, "l3g")
    rw = phg.sb("rw", [128, NT * NEx], F32)
    simb = phg.sb("simb", [128, 8 * 16], BF16)
    wload(C, simb, aps["L3_sim"], D, 16)
    simsq = phg.sb("simsq", [128, 8 * 16], F32)
    p.op("scalar", lambda e: e.activation(out=simsq[:], in_=simb[:], func=AF.Square),
         reads=[simb[:]], writes=[simsq[:]])
    onef = phg.sb("onef", [128, 1], F32)
    p.op("vector", lambda e: e.memset(onef[:], 1.0), writes=[onef[:]])
    pc = C.psf()
    for kb in range(8):
        C.p.op("tensor", lambda e, o=pc[0:1, 0:16], rh=simsq[:, kb * 16:(kb + 1) * 16],
               s=(kb == 0), t=(kb == 7): e.matmul(out=o, lhsT=onef[:], rhs=rh,
                                                  start=s, stop=t),
               reads=[onef[:], simsq[:]], writes=[pc[:, 0:16]])
    csum = phg.sb("csum", [1, 16], F32)
    p.op("vector", lambda e: e.tensor_copy(out=csum[:], in_=pc[0:1, 0:16]),
         reads=[pc[:, 0:16]], writes=[csum[:]])
    p.op("vector", lambda e: e.tensor_scalar(out=csum[:], in0=csum[:], scalar1=1e-24,
                                             scalar2=None, op0=ALU.max),
         reads=[csum[:]], writes=[csum[:]])
    p.op("scalar", lambda e: e.activation(out=csum[:], in_=csum[:], func=AF.Ln),
         reads=[csum[:]], writes=[csum[:]])
    p.op("scalar", lambda e: e.activation(out=csum[:], in_=csum[:], func=AF.Exp,
                                          scale=-0.5),
         reads=[csum[:]], writes=[csum[:]])
    grow = phg.sb("grow", [1, 16], F32)
    p.op("sync", lambda e: e.dma_start(out=grow[:], in_=aps["L3_gates_row"][:]),
         writes=[grow[:]], dma=True)
    p.op("scalar", lambda e: e.activation(out=grow[:], in_=grow[:], func=AF.Sigmoid),
         reads=[grow[:]], writes=[grow[:]])
    icn_bc = phg.sb("icn_bc", [128, 16], F32)
    sg_bc = phg.sb("sg_bc", [128, 16], F32)
    for rowap, dstbc in ((csum, icn_bc), (grow, sg_bc)):
        pB = C.psf()
        C.p.op("tensor", lambda e, o=pB[:, 0:16], rh=rowap[0:1, 0:16]: e.matmul(
            out=o, lhsT=C.ones_f[:], rhs=rh, start=True, stop=True),
            reads=[C.ones_f[:], rowap[:]], writes=[pB[:, 0:16]])
        p.op("vector", lambda e, d=dstbc, ps=pB: e.tensor_copy(out=d[:], in_=ps[:, 0:16]),
             reads=[pB[:, 0:16]], writes=[dstbc[:]])
    phs = Phase(C, "l3s")
    xst = [phs.sb(f"xst{i}", [128, D], F32) for i in range(2)]
    sqt = [phs.sb(f"sqt{i}", [128, D], F32) for i in range(2)]
    stats10 = phs.sb("stats10", [128, 4 * NT], F32)
    gtmp = phs.sb("gtmp", [128, 10 * NEx], F32)
    for mi in range(NT):
        xs = xst[mi % 2]
        src = x_src[mi * 128:(mi + 1) * 128, :]
        p.op("sync", lambda e, s=xs, r=src: e.dma_start(out=s[:], in_=r),
             reads=[src], writes=[xs[:]], dma=True)
        mu, rstd = ln_stats(C, phs, xs[:], sqt[mi % 2][:], stats10, mi, D)
        svar = stats10[:, 4 * mi + 1:4 * mi + 2]
        nrm = gtmp[:, 9 * NEx:9 * NEx + 1]
        p.op("vector", lambda e, o=nrm, i_=svar: e.tensor_scalar(
            out=o, in0=i_, scalar1=-float(D), scalar2=None, op0=ALU.mult),
            reads=[svar], writes=[nrm])
        p.op("vector", lambda e, o=nrm, i1=rstd: e.scalar_tensor_tensor(
            out=o, in0=i1, scalar=rstd, in1=o, op0=ALU.mult, op1=ALU.mult),
            reads=[rstd, nrm], writes=[nrm])
        p.op("vector", lambda e, o=nrm: e.tensor_scalar(
            out=o, in0=o, scalar1=1e-24, scalar2=None, op0=ALU.max),
            reads=[nrm], writes=[nrm])
        p.op("scalar", lambda e, o=nrm: e.activation(out=o, in_=o, func=AF.Ln),
             reads=[nrm], writes=[nrm])
        p.op("scalar", lambda e, o=nrm: e.activation(out=o, in_=o, func=AF.Exp, scale=-0.5),
             reads=[nrm], writes=[nrm])
        pL = C.psf()
        mm_accum(C, pL[:, 0:NEx],
                 [xnT[:, k * T + mi * 128: k * T + mi * 128 + 128] for k in range(8)],
                 [simb[:, k * 16:(k + 1) * 16] for k in range(8)])
        lg = gtmp[:, 0:NEx]
        p.op("vector", lambda e, ps=pL, d=lg, s=nrm: e.tensor_scalar(
            out=d, in0=ps[:, 0:NEx], scalar1=s, scalar2=None, op0=ALU.mult),
            reads=[pL[:, 0:NEx], nrm], writes=[lg])
        p.op("vector", lambda e, d=lg: e.tensor_tensor(out=d, in0=d, in1=icn_bc[:],
                                                       op=ALU.mult),
             reads=[lg, icn_bc[:]], writes=[lg])
        p.op("vector", lambda e, d=lg: e.tensor_tensor(out=d, in0=d, in1=sg_bc[:],
                                                       op=ALU.subtract),
             reads=[lg, sg_bc[:]], writes=[lg])
        m1 = gtmp[:, NEx:NEx + 1]
        p.op("vector", lambda e, d=m1, i_=lg: e.reduce_max(out=d, in_=i_, axis=AX.X),
             reads=[lg], writes=[m1])
        eq = gtmp[:, 2 * NEx:3 * NEx]
        p.op("vector", lambda e, d=eq, i_=lg, s=m1: e.tensor_scalar(
            out=d, in0=i_, scalar1=s, scalar2=-1e30, op0=ALU.is_ge, op1=ALU.mult),
            reads=[lg, m1], writes=[eq])
        l2i = gtmp[:, 3 * NEx:4 * NEx]
        p.op("vector", lambda e, d=l2i, i0=lg, i1=eq: e.tensor_tensor(
            out=d, in0=i0, in1=i1, op=ALU.add),
            reads=[lg, eq], writes=[l2i])
        m2 = gtmp[:, NEx + 1:NEx + 2]
        p.op("vector", lambda e, d=m2, i_=l2i: e.reduce_max(out=d, in_=i_, axis=AX.X),
             reads=[l2i], writes=[m2])
        fb2 = gtmp[:, 4 * NEx:5 * NEx]
        p.op("vector", lambda e, d=fb2, i_=lg, s=m2: e.tensor_scalar(
            out=d, in0=i_, scalar1=s, scalar2=None, op0=ALU.is_ge),
            reads=[lg, m2], writes=[fb2])
        am = gtmp[:, 5 * NEx:6 * NEx]
        p.op("vector", lambda e, d=am, i_=lg: e.tensor_scalar(
            out=d, in0=i_, scalar1=0.0, scalar2=None, op0=ALU.is_gt),
            reads=[lg], writes=[am])
        ia = gtmp[:, NEx + 2:NEx + 3]
        p.op("vector", lambda e, d=ia, i_=m1: e.tensor_scalar(
            out=d, in0=i_, scalar1=0.0, scalar2=None, op0=ALU.is_le),
            reads=[m1], writes=[ia])
        p.op("vector", lambda e, d=fb2, s=ia: e.tensor_scalar(
            out=d, in0=d, scalar1=s, scalar2=None, op0=ALU.mult),
            reads=[fb2, ia], writes=[fb2])
        p.op("vector", lambda e, d=am, i1=fb2: e.tensor_tensor(out=d, in0=d, in1=i1,
                                                               op=ALU.max),
             reads=[am, fb2], writes=[am])
        gated = gtmp[:, 6 * NEx:7 * NEx]
        p.op("vector", lambda e, d=gated, i_=lg: e.tensor_scalar(
            out=d, in0=i_, scalar1=0.0, scalar2=None, op0=ALU.max),
            reads=[lg], writes=[gated])
        p.op("vector", lambda e, d=gated, i1=am: e.tensor_tensor(out=d, in0=d, in1=i1,
                                                                 op=ALU.mult),
             reads=[gated, am], writes=[gated])
        pen = gtmp[:, 7 * NEx:8 * NEx]
        p.op("vector", lambda e, d=pen, i_=am: e.tensor_scalar(
            out=d, in0=i_, scalar1=1e30, scalar2=-1e30, op0=ALU.mult, op1=ALU.add),
            reads=[am], writes=[pen])
        p.op("vector", lambda e, d=gated, i1=pen: e.tensor_tensor(out=d, in0=d, in1=i1,
                                                                  op=ALU.add),
             reads=[gated, pen], writes=[gated])
        nmx = gtmp[:, NEx + 3:NEx + 4]
        ssum = gtmp[:, NEx + 4:NEx + 5]
        rsx = gtmp[:, NEx + 5:NEx + 6]
        p.op("vector", lambda e, d=nmx, i_=gated: e.reduce_max(out=d, in_=i_, axis=AX.X,
                                                               negate=True),
             reads=[gated], writes=[nmx])
        dstw = rw[:, mi * NEx:(mi + 1) * NEx]
        p.op("scalar", lambda e, d=dstw, i_=gated, b=nmx, a=ssum: e.activation(
            out=d, in_=i_, func=AF.Exp, bias=b, scale=1.0, accum_out=a),
            reads=[gated, nmx], writes=[dstw, ssum])
        p.op("vector", lambda e, o=rsx, i_=ssum: e.reciprocal(out=o, in_=i_),
             reads=[ssum], writes=[rsx])
        p.op("vector", lambda e, d=dstw, s=rsx: e.tensor_scalar_mul(out=d, in0=d,
                                                                    scalar1=s),
             reads=[dstw, rsx], writes=[dstw])
    phs.close()
    qm = phg.sb("qm", [128, NT * 3 * DH], BF16)
    apb = [phg.sb(f"apb{i}", [128, 8 * 512], BF16) for i in range(2)]
    mix = phg.sb("mix", [128, 512], F32)
    for ch in range(6):
        wb = apb[ch % 2]
        wload(C, wb, aps["L3_allP"], D, 512, col0=ch * 512)
        comp = ch // 2
        eh0 = (ch % 2) * 8
        for mi in range(NT):
            pA = C.psf()
            mm_accum(C, pA[:, 0:512],
                     [xnT[:, k * T + mi * 128: k * T + mi * 128 + 128] for k in range(8)],
                     [wb[:, k * 512:(k + 1) * 512] for k in range(8)])
            dst = qm[:, mi * 192 + comp * DH: mi * 192 + (comp + 1) * DH]
            for e8 in range(8):
                ei = eh0 + e8
                rcol = rw[:, mi * NEx + ei: mi * NEx + ei + 1]
                if e8 == 0:
                    p.op("vector", lambda e, ps=pA, d=mix[:, 0:DH], r=rcol, e8=e8:
                         e.tensor_scalar(out=d, in0=ps[:, e8 * DH:(e8 + 1) * DH],
                                         scalar1=r, scalar2=None, op0=ALU.mult),
                         reads=[pA[:, 0:512], rw[:]], writes=[mix[:, 0:DH]])
                else:
                    p.op("vector", lambda e, ps=pA, d=mix[:, 0:DH], r=rcol, e8=e8:
                         e.scalar_tensor_tensor(out=d, in0=ps[:, e8 * DH:(e8 + 1) * DH],
                                                scalar=r, in1=d, op0=ALU.mult,
                                                op1=ALU.add),
                         reads=[pA[:, 0:512], rw[:], mix[:, 0:DH]], writes=[mix[:, 0:DH]])
            if ch % 2 == 0:
                p.op("vector", lambda e, d=dst, s=mix[:, 0:DH]: e.tensor_copy(
                    out=d, in_=s), reads=[mix[:, 0:DH]], writes=[dst])
            else:
                p.op("vector", lambda e, d=dst, s=mix[:, 0:DH]: e.tensor_tensor(
                    out=d, in0=d, in1=s, op=ALU.add),
                    reads=[mix[:, 0:DH], dst], writes=[dst])
    qT3 = phg.sb("qT3", [64, T], BF16)
    kT3 = phg.sb("kT3", [64, T], BF16)
    sc32 = [phg.sb(f"sc3{i}", [128, T], F32) for i in range(2)]
    pb32 = [phg.sb(f"pb3{i}", [128, T], BF16) for i in range(2)]
    sfx4 = phg.sb("sfx4", [128, 12], F32)
    tri = phg.sb("tri", [128, 128], F32)
    p.op("sync", lambda e: e.dma_start(out=tri[:], in_=aps["diagbias"][:]),
         writes=[tri[:]], dma=True)
    ctx3 = phg.sb("ctx3", [128, NT * DH], BF16)
    for mi in range(NT):
        for which, dstT in ((0, qT3), (1, kT3)):
            src_b = qm[:, mi * 192 + which * DH: mi * 192 + (which + 1) * DH]
            pbt = C.psb()
            pb = pbt[0:64, 0:128]
            C.p.op("tensor", lambda e, s=src_b, o=pb: e.transpose(
                out=o, in_=s, identity=C.identb[:]),
                reads=[src_b, C.identb[:]], writes=[pbt[:, 0:128]])
            p.op("vector", lambda e, s=pb, d=dstT[0:64, mi * 128:(mi + 1) * 128]:
                 e.tensor_copy(out=d, in_=s),
                 reads=[pbt[:, 0:128]], writes=[dstT[0:64, mi * 128:(mi + 1) * 128]])
    p.op("vector", lambda e: e.memset(pb32[0][:], 0.0), writes=[pb32[0][:]])
    p.op("vector", lambda e: e.memset(pb32[1][:], 0.0), writes=[pb32[1][:]])
    for mi in range(NT):
        sc3 = sc32[mi % 2]
        pb3 = pb32[mi % 2]
        W = (mi + 1) * 128
        ncj = (mi + 4) // 4
        for cj in range(ncj):
            wv = min(512, W - cj * 512)
            pA = C.psf()
            lh = qT3[0:64, mi * 128:(mi + 1) * 128]
            rh = kT3[0:64, cj * 512:cj * 512 + wv]
            C.p.op("tensor", lambda e, o=pA[:, 0:wv], lh=lh, rh=rh: e.matmul(
                out=o, lhsT=lh, rhs=rh, start=True, stop=True),
                reads=[lh, rh], writes=[pA[:, 0:wv]])
            dsc = sc3[:, cj * 512:cj * 512 + wv]
            p.op("scalar", lambda e, ps=pA, d=dsc, wv=wv: e.copy(out=d, in_=ps[:, 0:wv]),
                 reads=[pA[:, 0:wv]], writes=[dsc])
        dg = sc3[:, mi * 128:W]
        p.op("vector", lambda e, d=dg: e.tensor_tensor(out=d, in0=d, in1=tri[:],
                                                       op=ALU.add),
             reads=[dg, tri[:]], writes=[dg])
        scw = sc3[:, 0:W]
        pbw = pb3[:, 0:W]
        nmx = sfx4[:, 6 * (mi % 2) + 0:6 * (mi % 2) + 1]
        ssum = sfx4[:, 6 * (mi % 2) + 1:6 * (mi % 2) + 2]
        rsx = sfx4[:, 6 * (mi % 2) + 2:6 * (mi % 2) + 3]
        p.op("vector", lambda e, i_=scw, o=nmx: e.reduce_max(out=o, in_=i_, axis=AX.X,
                                                             negate=True),
             reads=[scw], writes=[nmx])
        p.op("vector", lambda e, o=nmx: e.tensor_scalar_mul(out=o, in0=o, scalar1=0.125),
             reads=[nmx], writes=[nmx])
        p.op("scalar", lambda e, o=pbw, i_=scw, b=nmx, a=ssum: e.activation(
            out=o, in_=i_, func=AF.Exp, bias=b, scale=0.125, accum_out=a),
            reads=[scw, nmx], writes=[pbw, ssum])
        p.op("vector", lambda e, o=rsx, i_=ssum: e.reciprocal(out=o, in_=i_),
             reads=[ssum], writes=[rsx])
        p.op("vector", lambda e, o=pbw, s=rsx: e.tensor_scalar_mul(out=o, in0=o, scalar1=s),
             reads=[pbw, rsx], writes=[pbw])
        for g4 in range((mi + 4) // 4):
            nb = min(4, mi + 1 - g4 * 4)
            transpose_batch(C, [pb3[:, (g4 * 4 + i2) * 128:(g4 * 4 + i2 + 1) * 128]
                                for i2 in range(nb)],
                            pb3[:, g4 * 512:g4 * 512 + nb * 128], BF16)
        pcx = C.psf()
        for kb in range(mi + 1):
            lh = qm[:, kb * 192 + 2 * DH: kb * 192 + 3 * DH]
            rh = pb3[:, kb * 128:(kb + 1) * 128]
            C.p.op("tensor", lambda e, o=pcx[0:DH, 0:128], lh=lh, rh=rh,
                   s=(kb == 0), t=(kb == mi): e.matmul(out=o, lhsT=lh, rhs=rh,
                                                       start=s, stop=t),
                   reads=[lh, rh], writes=[pcx[:, 0:128]])
        cst = ctx3[:, mi * DH:(mi + 1) * DH]
        ctmp = phg.sb(f"ctmp{mi}", [64, 128], BF16) if False else None
        ct64 = sc3[:, 0:64]
        ctb = pb3[:, T - 128:T]
        p.op("vector", lambda e, d=ctb[0:64, 0:128], s=pcx[0:DH, 0:128]: e.tensor_copy(
            out=d, in_=s), reads=[pcx[:, 0:128]], writes=[ctb])
        pbt2 = C.psb()
        C.p.op("tensor", lambda e, s=ctb[0:64, 0:128], o=pbt2[0:128, 0:64]: e.transpose(
            out=o, in_=s, identity=C.identb[0:64, 0:64]),
            reads=[ctb, C.identb[:]], writes=[pbt2[:, 0:128]])
        p.op("vector", lambda e, d=cst, s=pbt2[0:128, 0:64]: e.tensor_copy(out=d, in_=s),
             reads=[pbt2[:, 0:128]], writes=[cst])
    phz = Phase(C, "l3z")
    wmix = [phz.sb(f"wmix{i}", [128, NEx * DH], BF16) for i in range(2)]
    wmt = [phz.sb(f"wmt{i}", [128, 8 * 128], BF16) for i in range(2)]
    opb = [phz.sb(f"opb{i}", [128, 8 * 512], BF16) for i in range(2)]
    xst2 = [phz.sb(f"xu{i}", [128, D], F32) for i in range(2)]
    tmps = [phz.sb(f"tm{i}", [128, D], F32) for i in range(2)]
    for half in range(2):
        wload(C, opb[half], aps["L3_OP"], D, 512, col0=half * 512)
    for mi in range(NT):
        wm = wmix[mi % 2]
        for ei in range(NEx):
            rcol = rw[:, mi * NEx + ei: mi * NEx + ei + 1]
            dst = wm[:, ei * DH:(ei + 1) * DH]
            p.op("vector", lambda e, d=dst, s=ctx3[:, mi * DH:(mi + 1) * DH], r=rcol:
                 e.tensor_scalar(out=d, in0=s, scalar1=r, scalar2=None, op0=ALU.mult),
                 reads=[ctx3[:, mi * DH:(mi + 1) * DH], rw[:]], writes=[dst])
        wt = wmt[mi % 2]
        for fb in range(ND):
            src_b = wm[:, fb * 128:(fb + 1) * 128]
            dst_b = wt[:, fb * 128:(fb + 1) * 128]
            transpose128(C, phz, src_b, dst_b, BF16)
        xs = xst2[mi % 2]
        tmp = tmps[mi % 2]
        src = x_src[mi * 128:(mi + 1) * 128, :]
        p.op("sync", lambda e, s=xs, r=src: e.dma_start(out=s[:], in_=r),
             reads=[src], writes=[xs[:]], dma=True)
        for nj in range(2):
            pA = C.psf()
            mm_accum(C, pA[:, 0:512],
                     [wt[:, k * 128:(k + 1) * 128] for k in range(8)],
                     [opb[nj][:, k * 512:(k + 1) * 512] for k in range(8)])
            dsl = tmp[:, nj * 512:(nj + 1) * 512]
            p.op("scalar", lambda e, ps=pA, d=dsl: e.activation(
                out=d, in_=ps[:, 0:512], func=AF.Silu),
                reads=[pA[:, 0:512]], writes=[dsl])
        ssl = s_tok[:, mi * D:(mi + 1) * D]
        p.op("vector", lambda e, g=tmp, s=ssl: e.tensor_tensor(out=g[:], in0=g[:], in1=s,
                                                               op=ALU.mult),
             reads=[tmp[:], ssl], writes=[tmp[:]])
        p.op("vector", lambda e, x=xs, t_=tmp: e.tensor_tensor(out=x[:], in0=x[:],
                                                               in1=t_[:], op=ALU.add),
             reads=[tmp[:], xs[:]], writes=[xs[:]])
        dst = x_dst[mi * 128:(mi + 1) * 128, :]
        p.op("sync", lambda e, s=xs, d=dst: e.dma_start(out=d, in_=s[:]),
             reads=[xs[:]], writes=[dst], dma=True)
    phz.close()
    phg.close()
    phx.close()
    ph.close()


def _col128(v):
    return np.ascontiguousarray(v.reshape(-1, 128).T)


def prep_host(inputs):
    f = np.float32
    A = {}
    A["identb"] = np.eye(128, dtype=np.float32)
    A["identf"] = np.eye(128, dtype=np.float32)
    import ml_dtypes
    A["identb"] = np.eye(128, dtype=ml_dtypes.bfloat16)
    A["L0_lnw_col"] = _col128(np.asarray(inputs["L0_attn_ln_w"], f))
    A["L0_lnb_col"] = _col128(np.asarray(inputs["L0_attn_ln_b"], f))
    A["L0_wqkvT"] = np.ascontiguousarray(np.asarray(inputs["L0_attn_w_qkv"], f).T)
    A["L0_woutT"] = np.ascontiguousarray(np.asarray(inputs["L0_attn_w_out"], f).T)
    A["L0_w1T"] = np.ascontiguousarray(np.asarray(inputs["L0_ffn_w1_w"], f).T)
    A["L0_w2T"] = np.ascontiguousarray(np.asarray(inputs["L0_ffn_w2_w"], f).T)
    A["L0_h1T"] = np.ascontiguousarray(np.asarray(inputs["L0_ffn_h1_w"], f).T)
    A["L0_h2T"] = np.ascontiguousarray(np.asarray(inputs["L0_ffn_h2_w"], f).T)
    A["L0_w3T"] = np.ascontiguousarray(np.asarray(inputs["L0_ffn_w3_w"], f).T)
    A["L0_s1c"] = _col128(np.asarray(inputs["L0_ffn_w1_s"], f))
    A["L0_s2c"] = _col128(np.asarray(inputs["L0_ffn_w2_s"], f))
    A["L0_sh1c"] = _col128(np.asarray(inputs["L0_ffn_h1_s"], f))
    A["L0_sh2c"] = _col128(np.asarray(inputs["L0_ffn_h2_s"], f))
    A["L0_s3c"] = _col128(np.asarray(inputs["L0_ffn_w3_s"], f))
    dh = 64
    inv_freq = 1.0 / (10000.0 ** (np.arange(0, dh, 2, dtype=np.float64) / dh))
    tpos = np.arange(T, dtype=np.float64)
    fr = tpos[:, None] * inv_freq[None, :]
    fr2 = np.concatenate([fr, fr], axis=1)
    A["L0_cos64"] = np.cos(fr2).astype(f)
    sinm = np.sin(fr2).astype(f)
    sinm[:, 0:32] *= -1.0
    A["L0_sinm64"] = np.ascontiguousarray(sinm)
    di = np.arange(128)[:, None]
    dj = np.arange(160)[None, :]
    mg = ((dj > di) & (dj <= di + 32))
    m0 = (mg & (dj >= 32))
    A["L0_maskg"] = np.where(mg, 0.0, -1e30).astype(f)
    A["L0_mask0"] = np.where(m0, 0.0, -1e30).astype(f)
    dj2 = np.arange(128)[None, :]
    A["diagbias"] = np.where(dj2 <= di, 0.0, -1e30).astype(f)
    A["L1_wqT"] = np.ascontiguousarray(np.asarray(inputs["L1_attn_Wq"], f).T)
    A["L1_wkT"] = np.ascontiguousarray(np.asarray(inputs["L1_attn_Wk"], f).T)
    A["L1_wvT"] = np.ascontiguousarray(np.asarray(inputs["L1_attn_Wv"], f).T)
    A["L1_bqc"] = _col128(np.asarray(inputs["L1_attn_bq"], f))
    A["L1_bkc"] = _col128(np.asarray(inputs["L1_attn_bk"], f))
    A["L1_bv_row"] = np.asarray(inputs["L1_attn_bv"], f).reshape(1, D)
    A["L1_wqkv"] = np.ascontiguousarray(np.asarray(inputs["L1_ffn_Wqkv"], f))
    A["L1_wr"] = np.ascontiguousarray(np.asarray(inputs["L1_ffn_Wr"], f))
    pos = np.arange(T - 1, -1, -1, dtype=np.float64)
    invf = 1.0 / (10000.0 ** (np.arange(0, D, 2, dtype=np.float64) / D))
    sino = pos[:, None] * invf[None, :]
    pe = np.concatenate([np.sin(sino), np.cos(sino)], axis=1).astype(f)
    A["L1_peT"] = np.ascontiguousarray(pe.T)
    A["L1_uc"] = _col128(np.asarray(inputs["L1_ffn_u_bias"], f).reshape(-1))
    A["L1_vc"] = _col128(np.asarray(inputs["L1_ffn_v_bias"], f).reshape(-1))
    A["L1_wo"] = np.ascontiguousarray(np.asarray(inputs["L1_ffn_Wo"], f))
    for nm, src_ in [("ln1g", "L1_ffn_ln1_g"), ("ln1b", "L1_ffn_ln1_b"),
                     ("ln2g", "L1_ffn_ln2_g"), ("ln2b", "L1_ffn_ln2_b"),
                     ("ln3g", "L1_ffn_ln3_g"), ("ln3b", "L1_ffn_ln3_b")]:
        A[f"L1_{nm}_row"] = np.asarray(inputs[src_], f).reshape(1, D)
    A["L1_wff1"] = np.ascontiguousarray(np.asarray(inputs["L1_ffn_Wff1"], f))
    A["L1_bff1c"] = _col128(np.asarray(inputs["L1_ffn_bff1"], f))
    A["L1_wff2"] = np.ascontiguousarray(np.asarray(inputs["L1_ffn_Wff2"], f))
    A["L1_bff2_row"] = np.asarray(inputs["L1_ffn_bff2"], f).reshape(1, D)
    A["L1_wg"] = np.ascontiguousarray(np.asarray(inputs["L1_ffn_Wg"], f))
    A["L1_we1"] = np.ascontiguousarray(np.asarray(inputs["L1_ffn_We1"], f).reshape(8 * 1024, 4096))
    A["L1_be1c"] = np.concatenate(
        [_col128(np.asarray(inputs["L1_ffn_be1"], f)[e]) for e in range(8)], axis=1)
    A["L1_we2"] = np.ascontiguousarray(np.asarray(inputs["L1_ffn_We2"], f).reshape(8 * 4096, 1024))
    A["L1_be2_row"] = np.asarray(inputs["L1_ffn_be2"], f).reshape(1, 8 * 1024)
    A["L1_wout"] = np.ascontiguousarray(np.asarray(inputs["L1_ffn_Wout"], f))
    A["L2_wqT"] = np.ascontiguousarray(np.asarray(inputs["L2_attn_W_q"], f).T)
    A["L2_wkT"] = np.ascontiguousarray(np.asarray(inputs["L2_attn_W_k"], f).T)
    A["L2_wvT"] = np.ascontiguousarray(np.asarray(inputs["L2_attn_W_v"], f).T)
    A["L2_woT"] = np.ascontiguousarray(np.asarray(inputs["L2_attn_W_o"], f).T)
    A["L2_bqc"] = _col128(np.asarray(inputs["L2_attn_b_q"], f))
    A["L2_bk_row"] = np.asarray(inputs["L2_attn_b_k"], f).reshape(1, D)
    A["L2_bv_row"] = np.asarray(inputs["L2_attn_b_v"], f).reshape(1, D)
    A["L2_bo_row"] = np.asarray(inputs["L2_attn_b_o"], f).reshape(1, D)
    dec = np.exp(-np.arange(T, dtype=np.float64) / T)
    A["L2_dec"] = _col128(dec.astype(f))
    A["L2_epsdec"] = _col128((1e-6 / dec).astype(f))
    A["L2_wgate"] = np.ascontiguousarray(np.asarray(inputs["L2_ffn_w_gate"], f))
    A["L2_wup"] = np.ascontiguousarray(np.asarray(inputs["L2_ffn_w_up"], f))
    A["L2_wdown"] = np.ascontiguousarray(np.asarray(inputs["L2_ffn_w_down"], f))
    A["L3_qkvT"] = np.ascontiguousarray(np.asarray(inputs["L3_attn_qkv_weight"], f).T)
    A["L3_projT"] = np.ascontiguousarray(np.asarray(inputs["L3_attn_proj_weight"], f).T)
    A["L3_pb_row"] = np.asarray(inputs["L3_attn_proj_bias"], f).reshape(1, D)
    A["L3_sim"] = np.ascontiguousarray(np.asarray(inputs["L3_ffn_sim_matrix"], f))
    A["L3_gates_row"] = np.asarray(inputs["L3_ffn_gates"], f).reshape(1, 16)
    qp = np.asarray(inputs["L3_ffn_q_proj"], f).transpose(1, 0, 2).reshape(1024, 1024)
    kp = np.asarray(inputs["L3_ffn_k_proj"], f).transpose(1, 0, 2).reshape(1024, 1024)
    vp = np.asarray(inputs["L3_ffn_v_proj"], f).transpose(1, 0, 2).reshape(1024, 1024)
    A["L3_allP"] = np.ascontiguousarray(np.concatenate([qp, kp, vp], axis=1))
    A["L3_OP"] = np.ascontiguousarray(np.asarray(inputs["L3_ffn_o_proj"], f).reshape(1024, 1024))
    return A


def build_program(layers=(0, 1, 2, 3)):
    nc = bass.Bass("TRN2", target_bir_lowering=False, debug=False)
    aps = {}
    aps["x_in"] = nc.dram_tensor("x_in", [T, D], F32, kind="ExternalInput").ap()
    aps["x_out"] = nc.dram_tensor("x_out", [T, D], F32, kind="ExternalOutput").ap()
    host = _HOST_SPEC
    for name, (shape, dt) in host.items():
        if _need(name, layers):
            aps[name] = nc.dram_tensor(name, list(shape), dt, kind="ExternalInput").ap()
    aps["scr_a"] = nc.dram_tensor("scr_a", [4096 * T], BF16, kind="Internal").ap()
    aps["scr_b"] = nc.dram_tensor("scr_b", [4096 * T], BF16, kind="Internal").ap()
    aps["xbuf"] = nc.dram_tensor("xbuf", [T, D], F32, kind="Internal").ap()
    aps["s_scr"] = nc.dram_tensor("s_scr", [T * D], BF16, kind="Internal").ap()
    aps["scr_sh0"] = nc.dram_tensor("scr_sh0", [129 * T + 128], BF16, kind="Internal").ap()
    aps["scr_sh1"] = nc.dram_tensor("scr_sh1", [129 * T + 128], BF16, kind="Internal").ap()
    aps["ctx_scr"] = nc.dram_tensor("ctx_scr", [128 * ND * T], BF16, kind="Internal").ap()
    aps["l2wg_q"] = nc.dram_tensor("l2wg_q", [1024 * 2816], BF16, kind="Internal").ap()
    aps["l2wu_q"] = nc.dram_tensor("l2wu_q", [1024 * 2816], BF16, kind="Internal").ap()
    aps["l2wd_q"] = nc.dram_tensor("l2wd_q", [2816 * 1024], BF16, kind="Internal").ap()

    with ExitStack() as stack:
        p = Prog(nc, stack)
        block_cm = nc.Block()
        C = Ctx(nc, p, stack, aps)
        builders = {0: build_layer0, 1: build_layer1, 2: build_layer2,
                    3: build_layer3}
        cur = aps["x_in"]
        for i, l in enumerate(layers):
            dst = aps["x_out"] if i == len(layers) - 1 else aps["xbuf"]
            builders[l](C, cur, dst)
            cur = aps["xbuf"]
        p.raw_wait("sync", [p.barrier_token(e) for e in ENGS])
        with block_cm as block:
            p.replay(block)
    return nc


_HOST_SPEC = {
    "identb": ((128, 128), BF16),
    "identf": ((128, 128), F32),
    "L0_lnw_col": ((128, 8), F32), "L0_lnb_col": ((128, 8), F32),
    "L0_wqkvT": ((1024, 3072), F32), "L0_woutT": ((1024, 1024), F32),
    "L0_w1T": ((1024, 4096), F32), "L0_w2T": ((1024, 4096), F32),
    "L0_h1T": ((4096, 4096), F32), "L0_h2T": ((4096, 4096), F32),
    "L0_w3T": ((4096, 1024), F32),
    "L0_s1c": ((128, 32), F32), "L0_s2c": ((128, 32), F32),
    "L0_sh1c": ((128, 32), F32), "L0_sh2c": ((128, 32), F32),
    "L0_s3c": ((128, 8), F32),
    "L0_cos64": ((T, 64), F32), "L0_sinm64": ((T, 64), F32),
    "L0_maskg": ((128, 160), F32), "L0_mask0": ((128, 160), F32),
    "diagbias": ((128, 128), F32),
    "L1_wqT": ((1024, 1024), F32), "L1_wkT": ((1024, 1024), F32),
    "L1_wvT": ((1024, 1024), F32),
    "L1_bqc": ((128, 8), F32), "L1_bkc": ((128, 8), F32), "L1_bv_row": ((1, 1024), F32),
    "L1_wqkv": ((1024, 3072), F32), "L1_wr": ((1024, 1024), F32),
    "L1_peT": ((1024, 2048), F32),
    "L1_uc": ((128, 8), F32), "L1_vc": ((128, 8), F32),
    "L1_wo": ((1024, 1024), F32),
    "L1_ln1g_row": ((1, 1024), F32), "L1_ln1b_row": ((1, 1024), F32),
    "L1_ln2g_row": ((1, 1024), F32), "L1_ln2b_row": ((1, 1024), F32),
    "L1_ln3g_row": ((1, 1024), F32), "L1_ln3b_row": ((1, 1024), F32),
    "L1_wff1": ((1024, 4096), F32), "L1_bff1c": ((128, 32), F32),
    "L1_wff2": ((4096, 1024), F32), "L1_bff2_row": ((1, 1024), F32),
    "L1_wg": ((1024, 8), F32),
    "L1_we1": ((8192, 4096), F32), "L1_be1c": ((128, 256), F32),
    "L1_we2": ((32768, 1024), F32), "L1_be2_row": ((1, 8192), F32),
    "L1_wout": ((1024, 1024), F32),
    "L2_wqT": ((1024, 1024), F32), "L2_wkT": ((1024, 1024), F32),
    "L2_wvT": ((1024, 1024), F32), "L2_woT": ((1024, 1024), F32),
    "L2_bqc": ((128, 8), F32),
    "L2_bk_row": ((1, 1024), F32), "L2_bv_row": ((1, 1024), F32),
    "L2_bo_row": ((1, 1024), F32),
    "L2_dec": ((128, 16), F32), "L2_epsdec": ((128, 16), F32),
    "L2_wgate": ((2816, 1024), F32), "L2_wup": ((2816, 1024), F32),
    "L2_wdown": ((1024, 2816), F32),
    "L3_qkvT": ((1024, 3072), F32), "L3_projT": ((1024, 1024), F32),
    "L3_pb_row": ((1, 1024), F32),
    "L3_sim": ((1024, 16), F32), "L3_gates_row": ((1, 16), F32),
    "L3_allP": ((1024, 3072), F32), "L3_OP": ((1024, 1024), F32),
}


def _need(name, layers):
    if name.startswith("ident") or name == "diagbias":
        return True
    for l in layers:
        if name.startswith(f"L{l}_"):
            return True
    return False


def kernel(**inputs):
    A = prep_host(inputs)
    x = np.asarray(inputs["L0_attn_x"], np.float32)
    nc = build_program((0, 1, 2, 3))
    in_maps = []
    for c in range(NCORES):
        m = dict(A)
        m["x_in"] = np.ascontiguousarray(x[c])
        in_maps.append(m)
    res = run_bass_kernel_spmd(nc, in_maps, core_ids=list(range(NCORES)))
    return np.stack([np.asarray(res.results[c]["x_out"]) for c in range(NCORES)], axis=0)
